# version 05
# baseline (speedup 1.0000x reference)
import numpy as np
import ml_dtypes
import concourse.bass as bass
import concourse.mybir as mybir
from concourse.bass_utils import run_bass_kernel_spmd

F32 = mybir.dt.float32
BF16 = mybir.dt.bfloat16
AF = mybir.ActivationFunctionType
ALU = mybir.AluOpType
AX = mybir.AxisListType.X

D = 2048
S = 8192
TOK = 2048
NH = 4
HD = 128
KC = D // 128
SCL = float(HD) ** -0.5
NEG = -1.0e30

ENGS = ("tensor", "vector", "scalar", "gpsimd", "sync")

BUILD_PHASES = 99


class Sem:
    __slots__ = ("h", "n", "name")

    def __init__(self, h, name):
        self.h = h
        self.n = 0
        self.name = name


class Ctx:

    def __init__(self, who, eng, sems):
        self.who = who
        self.eng = eng
        self.sems = sems
        self.cache = {}
        self.own = {"vector": sems["s_ve"], "scalar": sems["s_ac"]}

    def op(self, engine, fn, waits=(), incs=()):
        waits = list(waits)
        incs = list(incs)
        own = self.own.get(engine)
        if own is not None:
            waits.append((own, own.n))
            if not any(s is own for s, _ in incs):
                incs.append((own, 1))
        if self.who == engine:
            for s, v in waits:
                self.eng.wait_ge(s.h, v)
            inst = fn(self.eng)
            for s, n in incs:
                inst = inst.then_inc(s.h, n)
        for s, n in incs:
            s.n += n

    def wait(self, engine, waits):
        if self.who == engine:
            for s, v in waits:
                self.eng.wait_ge(s.h, v)


def _hadamard(n):
    h = np.array([[1.0]], dtype=np.float32)
    while h.shape[0] < n:
        h = np.block([[h, h], [h, -h]])
    return h



def build_nc(debug=False):
    nc = bass.Bass("TRN2", target_bir_lowering=False)

    EIN = dict(kind="ExternalInput")
    inp = {}

    def din(name, shape, dtype=F32):
        inp[name] = nc.dram_tensor(name, list(shape), dtype, **EIN)
        return inp[name]

    din("x_full", (S, D))
    din("x_own", (TOK, D))
    din("l0_wq", (D, D)); din("l0_bq", (1, D))
    din("l0_wk", (D, 512)); din("l0_bk", (128, 4))
    din("l0_wv", (D, 512)); din("l0_bv", (1, 512))
    din("l0_wo", (D, D)); din("l0_bo", (1, D))
    din("l0_su_ug", (128, 16)); din("l0_su_dp", (128, 64))
    din("l0_ws_ug", (128, 16384)); din("l0_ws_dp", (128, D))
    din("l0_wupT", (D, 8192)); din("l0_wgateT", (D, 8192)); din("l0_wdownT", (8192, D))
    din("l1_wq", (D, 512)); din("l1_wk", (D, 256)); din("l1_wv", (D, 256))
    din("l1_wo", (D, D))
    din("l1_qnw", (128, 1)); din("l1_knw", (128, 1))
    din("l1_wg", (D, 8)); din("l1_bg", (1, 8))
    din("l1_wgate", (6, D, 1024)); din("l1_bgate", (128, 48))
    din("l1_wup", (6, D, 1024)); din("l1_bup", (128, 48))
    din("l1_wdown", (6, 1024, D)); din("l1_bdown", (6, D))
    din("l1_wsup", (D, D)); din("l1_bsup", (128, 16))
    din("l1_wsdown", (D, D)); din("l1_bsdown", (1, D))
    din("ident", (128, 128)); din("h128", (128, 128)); din("h64", (64, 64))
    din("l0_mask", (4, 128, 512)); din("l1_mask", (16, 128, 512))

    out = nc.dram_tensor("out", [TOK, D], F32, kind="ExternalOutput")

    dr = {}

    def dint(name, shape, dtype):
        dr[name] = nc.dram_tensor(name, list(shape), dtype)
        return dr[name]

    dint("wupf", (D, 8192), BF16)
    dint("wgatef", (D, 8192), BF16)
    dint("wdtmp", (8192, D), BF16)
    dint("wdownf", (8192, D), BF16)
    dint("q0d", (TOK, D), BF16)
    dint("xn0d", (D, TOK), BF16)
    dint("k0d", (NH * 128, S), BF16)
    dint("v0d", (NH * 128, S), BF16)
    dint("attn_h0", (S, 512), BF16)
    dint("ag0", (4 * S, 512), BF16)
    dint("attn_own0", (TOK, D), BF16)
    dint("attn_own1", (TOK, D), BF16)
    dint("base0", (TOK, D), F32)
    dint("x1d", (TOK, D), F32)
    dint("x1ag", (4 * TOK, D), F32)
    dint("xn1d", (D, TOK), BF16)
    dint("q1d", (NH * 128, S), BF16)
    dint("k1d", (2 * 128, S), BF16)
    dint("v1d", (2 * 128, S), BF16)
    dint("attn_h1", (S, 512), BF16)
    dint("ag1", (4 * S, 512), BF16)
    dint("base1", (TOK, D), F32)
    dint("wgate_b", (6, D, 1024), BF16)
    dint("wup_b", (6, D, 1024), BF16)
    dint("wdown_b", (6, 1024, D), BF16)
    dint("wsup_b", (D, D), BF16)
    dint("wsdown_b", (D, D), BF16)

    dbg = {}
    if debug:
        for name, shape, dt in [
            ("dbg_wupf", (D, 8192), BF16), ("dbg_wdownf", (8192, D), BF16),
            ("dbg_q0", (TOK, D), BF16), ("dbg_k0", (NH * 128, S), BF16),
            ("dbg_v0", (NH * 128, S), BF16),
            ("dbg_attnh0", (S, 512), BF16), ("dbg_base0", (TOK, D), F32),
            ("dbg_x1", (TOK, D), F32),
            ("dbg_attnh1", (S, 512), BF16), ("dbg_base1", (TOK, D), F32),
        ]:
            dbg[name] = nc.dram_tensor(name, list(shape), dt, kind="ExternalOutput")

    import contextlib
    with contextlib.ExitStack() as stk:
        ent = stk.enter_context

        sb = {}
        def sbuf(name, shape, dtype):
            sb[name] = ent(nc.sbuf_tensor(name, list(shape), dtype))
            return sb[name]

        sbuf("ident_bf", (128, 128), BF16)
        sbuf("h128_bf", (128, 128), BF16)
        sbuf("h64_bf", (64, 64), BF16)
        sbuf("ones_col", (1, 128), BF16)
        sbuf("ones128", (128, 1), BF16)
        sbuf("su_ug", (128, 16), F32)
        sbuf("su_dp", (128, 64), F32)
        sbuf("bk_sb", (128, 4), F32)
        sbuf("bq_row", (1, D), BF16)
        sbuf("bv_row", (1, 512), BF16)
        sbuf("bo_row", (1, D), BF16)
        sbuf("qnw", (128, 1), F32)
        sbuf("knw", (128, 1), F32)
        sbuf("bg_row", (1, 8), BF16)
        sbuf("bgate_sb", (128, 48), F32)
        sbuf("bup_sb", (128, 48), F32)
        sbuf("bdown_sb", (6, D), BF16)
        sbuf("bsup_sb", (128, 16), F32)
        sbuf("bsdown_row", (1, D), BF16)
        sbuf("masks0", (128, 4 * 512), BF16)
        sbuf("masks1", (128, 16 * 512), BF16)

        sbuf("cvals", (128, 4), F32)
        nc.const_aps.register(F32, 1e-5, sb["cvals"][:, 0:1])
        nc.const_aps.register(F32, 1e-6, sb["cvals"][:, 1:2])
        nc.const_aps.register(F32, SCL, sb["cvals"][:, 2:3])
        nc.const_aps.register(F32, 1.0 / HD, sb["cvals"][:, 3:4])

        sbuf("abf", (128, 52224), BF16)
        sbuf("af32", (128, 12288), F32)
        sbuf("stat", (128, 256), F32)
        sbuf("w_fmb", (8, TOK), BF16)
        sbuf("w_row", (1, 6 * 512), BF16)

        psA = ent(nc.psum_tensor("psA", [128, 2048], F32))
        psB = ent(nc.psum_tensor("psB", [128, 2048], F32))

        sems = {}
        for name in ("s_wl", "s_al", "s_st", "s_gst", "s_mm", "s_ve", "s_ac", "s_cc",
                     "s_kv", "s_qA", "s_qB", "s_x2", "s_wA", "s_wB", "s_gA", "s_gB",
                     "s_gb", "s_sA", "s_sB", "s_sC", "s_xA", "s_xB"):
            sems[name] = Sem(ent(nc.semaphore(name)), name)

        block = ent(nc.Block())

        def prog(ctx):
            emit_program(ctx, nc, inp, out, dr, sb, psA, psB, sems, dbg)

        block.tensor(lambda e: prog(Ctx("tensor", e, sems)))
        block.vector(lambda e: prog(Ctx("vector", e, sems)))
        block.scalar(lambda e: prog(Ctx("scalar", e, sems)))
        block.gpsimd(lambda e: prog(Ctx("gpsimd", e, sems)))
        block.sync(lambda e: prog(Ctx("sync", e, sems)))

    return nc



def emit_program(ctx, nc, inp, out, dr, sb, psA, psB, sems, dbg):
    for s in sems.values():
        s.n = 0
    s_wl = sems["s_wl"]
    s_al = sems["s_al"]
    s_st = sems["s_st"]
    s_mm = sems["s_mm"]
    s_ve = sems["s_ve"]
    s_ac = sems["s_ac"]
    s_cc = sems["s_cc"]

    op = ctx.op

    def ld(engine, sem, dst, src):
        op(engine, lambda e, d=dst, s=src: e.dma_start(out=d, in_=s), incs=[(sem, 16)])

    G = "gpsimd"; SY = "sync"
    ld(G, s_wl, sb["ident_bf"][:, :], inp["ident"][:, :])
    ld(G, s_wl, sb["h128_bf"][:, :], inp["h128"][:, :])
    ld(G, s_wl, sb["h64_bf"][:, :], inp["h64"][:, :])
    ld(G, s_wl, sb["su_ug"][:, :], inp["l0_su_ug"][:, :])
    ld(G, s_wl, sb["su_dp"][:, :], inp["l0_su_dp"][:, :])
    ld(G, s_wl, sb["bk_sb"][:, :], inp["l0_bk"][:, :])
    ld(G, s_wl, sb["bq_row"][:, :], inp["l0_bq"][:, :])
    ld(G, s_wl, sb["bv_row"][:, :], inp["l0_bv"][:, :])
    ld(G, s_wl, sb["bo_row"][:, :], inp["l0_bo"][:, :])
    ld(G, s_wl, sb["qnw"][:, :], inp["l1_qnw"][:, :])
    ld(G, s_wl, sb["knw"][:, :], inp["l1_knw"][:, :])
    ld(G, s_wl, sb["bg_row"][:, :], inp["l1_bg"][:, :])
    ld(G, s_wl, sb["bgate_sb"][:, :], inp["l1_bgate"][:, :])
    ld(G, s_wl, sb["bup_sb"][:, :], inp["l1_bup"][:, :])
    ld(G, s_wl, sb["bdown_sb"][:, :], inp["l1_bdown"][:, :])
    ld(G, s_wl, sb["bsup_sb"][:, :], inp["l1_bsup"][:, :])
    ld(G, s_wl, sb["bsdown_row"][:, :], inp["l1_bsdown"][:, :])
    m0 = inp["l0_mask"].ap().rearrange("n p f -> p n f")
    ld(G, s_wl, sb["masks0"][:, :].rearrange("p (n f) -> p n f", n=4), m0)
    m1 = inp["l1_mask"].ap().rearrange("n p f -> p n f")
    ld(G, s_wl, sb["masks1"][:, :].rearrange("p (n f) -> p n f", n=16), m1)
    consts_done = s_wl.n
    op("vector", lambda e: e.memset(sb["ones_col"][:, :], 1.0), incs=[(s_ve, 1)])
    for i, v in enumerate([1e-5, 1e-6, SCL, 1.0 / HD]):
        op("vector", lambda e, i=i, v=v: e.memset(sb["cvals"][:, i:i + 1], v), incs=[(s_ve, 1)])

    if BUILD_PHASES >= 1:
        emit_phaseW(ctx, nc, inp, dr, sb, psA, psB, sems, consts_done, dbg)
    if BUILD_PHASES >= 2:
        emit_layer0(ctx, nc, inp, out, dr, sb, psA, psB, sems, dbg)
    if BUILD_PHASES >= 3:
        emit_layer1(ctx, nc, inp, out, dr, sb, psA, psB, sems, dbg)

    ctx.wait(SY, [(sems["s_st"], sems["s_st"].n), (sems["s_sA"], sems["s_sA"].n),
                  (sems["s_sB"], sems["s_sB"].n), (sems["s_sC"], sems["s_sC"].n)])
    ctx.wait(G, [(sems["s_gst"], sems["s_gst"].n), (sems["s_gb"], sems["s_gb"].n)])



def emit_phaseW(ctx, nc, inp, dr, sb, psA, psB, sems, consts_done, dbg):
    op = ctx.op
    s_wl, s_al, s_st, s_gst, s_mm, s_ve = (sems[k] for k in ("s_wl", "s_al", "s_st", "s_gst", "s_mm", "s_ve"))
    s_wA, s_wB, s_gA, s_gB = (sems[k] for k in ("s_wA", "s_wB", "s_gA", "s_gB"))
    s_sA, s_sB, s_ac = sems["s_sA"], sems["s_sB"], sems["s_ac"]
    abf, af32 = sb["abf"], sb["af32"]
    h128, h64 = sb["h128_bf"], sb["h64_bf"]
    ws_bc = af32[:, 0:8192]
    wsd_bc = af32[:, 8192:10240]

    op("sync", lambda e: e.dma_start(out=wsd_bc, in_=inp["l0_ws_dp"][:, :]), incs=[(s_al, 16)])
    op("vector", lambda e: e.tensor_scalar_mul(wsd_bc, wsd_bc, 2.0 ** -6.5),
       waits=[(s_al, s_al.n)], incs=[(s_ve, 1)])
    wsd_ready = s_ve.n

    for wname, dst in (("l0_wupT", "wupf"), ("l0_wgateT", "wgatef")):
        wsrc = inp[wname].ap().rearrange("(r c) h -> c r h", c=128)
        wdst = dr[dst].ap().rearrange("(r c) h -> c r h", c=128)
        base_off = 0 if wname == "l0_wupT" else 8192
        op("sync", lambda e, bo=base_off:
           e.dma_start(out=ws_bc, in_=inp["l0_ws_ug"][:, bo:bo + 8192]),
           waits=[(s_ve, s_ve.n)], incs=[(s_al, 16)])
        op("vector", lambda e: e.tensor_scalar_mul(ws_bc, ws_bc, 2.0 ** -5.5),
           waits=[(s_al, s_al.n)], incs=[(s_ve, 1)])
        ws_ready = s_ve.n
        for hc4 in range(16):
            h0 = hc4 * 512
            lbuf = abf[:, (hc4 % 2) * 8192:(hc4 % 2) * 8192 + 16 * 512]
            s_g = [s_gA, s_gB][hc4 % 2]
            op("gpsimd", lambda e, lb=lbuf, w=wsrc, h0=h0:
               e.dma_start(out=lb.rearrange("p (r h) -> p r h", r=16), in_=w[:, :, h0:h0 + 512]),
               waits=[(s_mm, ctx.cache.get(("wld1", (hc4 - 2) % 16), 0))], incs=[(s_g, 16)])
            lv = (s_g, s_g.n)
            for r in range(16):
                bank = (r % 2) * 512
                op("tensor", lambda e, lb=lbuf, r=r, bank=bank:
                   e.matmul(psA[:, bank:bank + 512], h128[:, :], lb[:, r * 512:(r + 1) * 512],
                            start=True, stop=True),
                   waits=[lv, (s_wl, consts_done),
                          (s_ve, ctx.cache.get(("wfe", (r - 2) % 16), 0))], incs=[(s_mm, 1)])
                mmv = s_mm.n
                ob = abf[:, 16384 + (r % 2) * 512: 16384 + (r % 2) * 512 + 512]
                s_s = [s_sA, s_sB][r % 2]
                op("vector", lambda e, ob=ob, bank=bank, h0=h0:
                   e.tensor_tensor(ob, psA[:, bank:bank + 512], ws_bc[:, h0:h0 + 512], ALU.mult),
                   waits=[(s_mm, mmv), (s_ve, ws_ready),
                          (s_s, ctx.cache.get(("wst", (r - 2) % 16), 0))], incs=[(s_ve, 1)])
                ctx.cache[("wfe", r)] = s_ve.n
                vv = s_ve.n
                op("sync", lambda e, ob=ob, w=wdst, r=r, h0=h0:
                   e.dma_start(out=w[:, r, h0:h0 + 512], in_=ob),
                   waits=[(s_ve, vv)], incs=[(s_s, 16)])
                ctx.cache[("wst", r)] = s_s.n
            ctx.cache[("wld1", hc4)] = s_mm.n

    ctx.cache[("wfe2", 30)] = (s_ve, s_ve.n)
    ctx.cache[("wfe2", 31)] = (s_ve, s_ve.n)
    ctx.cache[("wld2x",)] = s_mm.n
    wsrc = inp["l0_wdownT"].ap().rearrange("(r c) o -> c r o", c=128)
    wdst = dr["wdtmp"].ap().rearrange("(r c) o -> c r o", c=128)
    for oc in range(4):
        o0 = oc * 512
        for hh in range(2):
            lbuf = abf[:, ((oc * 2 + hh) % 2) * 16384:((oc * 2 + hh) % 2) * 16384 + 32 * 512]
            s_g = [s_gA, s_gB][(oc * 2 + hh) % 2]
            it = oc * 2 + hh
            op("gpsimd", lambda e, lb=lbuf, o0=o0, hh=hh:
               e.dma_start(out=lb.rearrange("p (r o) -> p r o", r=32),
                           in_=wsrc[:, hh * 32:(hh + 1) * 32, o0:o0 + 512]),
               waits=[(s_mm, ctx.cache.get(("wld2", it - 2), ctx.cache.get(("wld2x",), 0)))] +
                     ([(s_sA, s_sA.n), (s_sB, s_sB.n)] if it < 2 else []),
               incs=[(s_g, 16)])
            lv = (s_g, s_g.n)
            for rr in range(32):
                r = hh * 32 + rr
                bank = (rr % 2) * 512
                op("tensor", lambda e, lb=lbuf, rr=rr, bank=bank:
                   e.matmul(psA[:, bank:bank + 512], h128[:, :], lb[:, rr * 512:(rr + 1) * 512],
                            start=True, stop=True),
                   waits=[lv, (s_wl, consts_done),
                          ctx.cache.get(("wfe2", (rr - 2) % 32), (s_ve, 0))], incs=[(s_mm, 1)])
                mmv = s_mm.n
                ob = abf[:, 33280 + (rr % 2) * 512: 33280 + (rr % 2) * 512 + 512]
                s_s = [s_sA, s_sB][rr % 2]
                op("scalar", lambda e, ob=ob, bank=bank: e.activation(ob, psA[:, bank:bank + 512], AF.Copy),
                   waits=[(s_mm, mmv), (s_s, ctx.cache.get(("wst2", (rr - 2) % 32), 0))],
                   incs=[(s_ac, 1)])
                ctx.cache[("wfe2", rr)] = (s_ac, s_ac.n)
                vv = (s_ac, s_ac.n)
                op("sync", lambda e, ob=ob, r=r, o0=o0:
                   e.dma_start(out=wdst[:, r, o0:o0 + 512], in_=ob),
                   waits=[vv], incs=[(s_s, 16)])
                ctx.cache[("wst2", rr)] = s_s.n
            ctx.cache[("wld2", it)] = s_mm.n
    f1_done_sA, f1_done_sB = s_sA.n, s_sB.n

    w2 = dr["wdtmp"].ap().rearrange("(r c) o -> r c o", c=128)
    w2dst = dr["wdownf"].ap().rearrange("(r c) o -> c r o", c=128)
    for c in range(128):
        lbuf = abf[0:64, (c % 2) * 2048:(c % 2) * 2048 + 2048]
        s_w = [s_wA, s_wB][c % 2]
        op("sync", lambda e, lb=lbuf, c=c: e.dma_start(out=lb, in_=w2[:, c, :]),
           waits=[(s_sA, f1_done_sA), (s_sB, f1_done_sB),
                  (s_mm, ctx.cache.get(("wld3", c - 2), 0))], incs=[(s_w, 16)])
        lv = (s_w, s_w.n)
        for oc in range(4):
            bank = 1024 + (oc % 2) * 512
            op("tensor", lambda e, lb=lbuf, oc=oc, bank=bank:
               e.matmul(psA[0:64, bank:bank + 512], h64[:, :], lb[:, oc * 512:(oc + 1) * 512],
                        start=True, stop=True),
               waits=[lv, (s_wl, consts_done),
                      (s_ve, ctx.cache.get(("wfe3", c, oc - 2),
                                           ctx.cache.get(("wfe3", c - 1, oc + 2), 0)))],
               incs=[(s_mm, 1)])
            mmv = s_mm.n
            ob = abf[0:64, 4096 + (oc % 2) * 512: 4096 + (oc % 2) * 512 + 512]
            s_s2 = [s_gst, sems["s_gb"]][oc % 2]
            op("vector", lambda e, ob=ob, bank=bank, oc=oc:
               e.tensor_tensor(ob, psA[0:64, bank:bank + 512], wsd_bc[0:64, oc * 512:(oc + 1) * 512], ALU.mult),
               waits=[(s_mm, mmv), (s_ve, wsd_ready),
                      (s_s2, ctx.cache.get(("wst3", c, oc - 2),
                                           ctx.cache.get(("wst3", c - 1, oc + 2), 0)))],
               incs=[(s_ve, 1)])
            ctx.cache[("wfe3", c, oc)] = s_ve.n
            vv = s_ve.n
            op("gpsimd", lambda e, ob=ob, c=c, oc=oc:
               e.dma_start(out=w2dst[c, :, oc * 512:(oc + 1) * 512], in_=ob),
               waits=[(s_ve, vv)], incs=[(s_s2, 16)])
            ctx.cache[("wst3", c, oc)] = s_s2.n
        ctx.cache[("wld3", c)] = s_mm.n

    for nm, dst in (("l1_wgate", "wgate_b"), ("l1_wup", "wup_b"), ("l1_wdown", "wdown_b"),
                    ("l1_wsup", "wsup_b"), ("l1_wsdown", "wsdown_b")):
        srcap = inp[nm].ap()
        dstap = dr[dst].ap()
        if len(srcap.shape) == 3:
            for n in range(srcap.shape[0]):
                op("gpsimd", lambda e, srcap=srcap, dstap=dstap, n=n:
                   e.dma_start(out=dstap[n], in_=srcap[n]), incs=[(s_gst, 16)])
        else:
            op("gpsimd", lambda e, srcap=srcap, dstap=dstap:
               e.dma_start(out=dstap[:, :], in_=srcap[:, :]), incs=[(s_gst, 16)])
    if dbg:
        wbar = [(s_st, s_st.n), (s_gst, s_gst.n), (sems["s_gb"], sems["s_gb"].n),
                (s_sA, s_sA.n), (s_sB, s_sB.n)]
        op("sync", lambda e: e.dma_start(out=dbg["dbg_wupf"][:, :], in_=dr["wupf"][:, :]),
           waits=wbar, incs=[(s_st, 16)])
        op("sync", lambda e: e.dma_start(out=dbg["dbg_wdownf"][:, :], in_=dr["wdownf"][:, :]),
           waits=wbar, incs=[(s_st, 16)])


def emit_ln(ctx, sems, sb, xf32, sq_scr, stat, xn_bf, cb=0):
    op = ctx.op
    s_ve, s_ac = sems["s_ve"], sems["s_ac"]
    op("vector", lambda e: e.reduce_sum(stat[:, cb:cb + 1], xf32, axis=AX), incs=[(s_ve, 1)])
    op("scalar", lambda e: e.activation(sq_scr, xf32, AF.Square, accum_out=stat[:, cb + 1:cb + 2]),
       incs=[(s_ac, 1)])
    sqv = s_ac.n
    op("vector", lambda e: e.tensor_scalar_mul(stat[:, cb + 2:cb + 3], stat[:, cb:cb + 1], 1.0 / D),
       incs=[(s_ve, 1)])
    op("vector", lambda e: e.tensor_mul(stat[:, cb + 3:cb + 4], stat[:, cb + 2:cb + 3], stat[:, cb + 2:cb + 3]),
       incs=[(s_ve, 1)])
    op("vector", lambda e: e.scalar_tensor_tensor(stat[:, cb + 4:cb + 5], stat[:, cb + 1:cb + 2], 1.0 / D,
                                                  stat[:, cb + 3:cb + 4], ALU.mult, ALU.subtract),
       waits=[(s_ac, sqv)], incs=[(s_ve, 1)])
    vv = s_ve.n
    op("scalar", lambda e: e.activation(stat[:, cb + 5:cb + 6], stat[:, cb + 4:cb + 5], AF.Ln, bias=1e-5),
       waits=[(s_ve, vv)], incs=[(s_ac, 1)])
    op("scalar", lambda e: e.activation(stat[:, cb + 6:cb + 7], stat[:, cb + 5:cb + 6], AF.Exp, scale=-0.5),
       incs=[(s_ac, 1)])
    sqv2 = s_ac.n
    op("vector", lambda e: e.tensor_mul(stat[:, cb + 7:cb + 8], stat[:, cb + 2:cb + 3], stat[:, cb + 6:cb + 7]),
       waits=[(s_ac, sqv2)], incs=[(s_ve, 1)])
    op("vector", lambda e: e.tensor_scalar_mul(stat[:, cb + 8:cb + 9], stat[:, cb + 7:cb + 8], -1.0),
       incs=[(s_ve, 1)])
    vv = s_ve.n
    op("scalar", lambda e: e.activation(xn_bf, xf32, AF.Identity, scale=stat[:, cb + 6:cb + 7],
                                        bias=stat[:, cb + 8:cb + 9]),
       waits=[(s_ve, vv)], incs=[(s_ac, 1)])
    return s_ve.n, s_ac.n


def emit_layer0(ctx, nc, inp, out, dr, sb, psA, psB, sems, dbg):
    op = ctx.op
    s_wl, s_al, s_st, s_gst, s_mm, s_ve, s_ac, s_cc = (
        sems[k] for k in ("s_wl", "s_al", "s_st", "s_gst", "s_mm", "s_ve", "s_ac", "s_cc"))
    s_sA, s_sB = sems["s_sA"], sems["s_sB"]
    abf, af32, stat = sb["abf"], sb["af32"], sb["stat"]
    ident, ones = sb["ident_bf"], sb["ones_col"]

    wq_src = inp["l0_wq"].ap().rearrange("(k p) o -> p k o", p=128)
    x_own_t = inp["x_own"].ap().rearrange("(n p) d -> n p d", p=128)
    x_full_t = inp["x_full"].ap().rearrange("(n p) d -> n p d", p=128)
    xn0_v = dr["xn0d"].ap().rearrange("(k p) t -> p k t", p=128)
    q0_v = dr["q0d"].ap()

    phase_barrier = [(s_st, s_st.n), (s_gst, s_gst.n), (s_ve, s_ve.n), (s_mm, s_mm.n)]
    XT = [af32[:, 0:2048], af32[:, 2048:4096]]
    SQp = [af32[:, 4096:6144], af32[:, 6144:8192]]
    XNT = [abf[:, 0:2048], abf[:, 2048:4096]]
    XNF = abf[:, 4096:36864]

    def xnf_tile(kc, t0, w=128):
        return XNF[:, kc * 2048 + t0: kc * 2048 + t0 + w]

    ln_acv = 0
    for ti in range(16):
        xt = XT[ti % 2]
        s_x = [sems["s_xA"], sems["s_xB"]][ti % 2]
        op("sync", lambda e, xt=xt, ti=ti: e.dma_start(out=xt, in_=x_own_t[ti]),
           waits=phase_barrier if ti == 0 else [(s_ac, ctx.cache.get(("a0ln", ti - 2), 0))],
           incs=[(s_x, 16)])
        lv = s_x.n
        ctx.wait("vector", [(s_x, lv)])
        ctx.wait("scalar", [(s_x, lv)])
        if ti >= 2:
            ctx.wait("scalar", [(s_mm, ctx.cache.get(("a0tr", ti - 2), 0))])
        _, ln_acv = emit_ln(ctx, sems, sb, xt, SQp[ti % 2], stat, XNT[ti % 2], cb=(ti % 2) * 40)
        ctx.cache[("a0ln", ti)] = ln_acv
        acv = s_ac.n
        for kc in range(16):
            op("tensor", lambda e, ti=ti, kc=kc:
               e.matmul(psB[:, kc * 128:(kc + 1) * 128], XNT[ti % 2][:, kc * 128:(kc + 1) * 128],
                        ident[:, :], start=True, stop=True),
               waits=([(s_ac, acv), (s_ve, ctx.cache.get(("a0ev", ti - 1), 0))] if kc == 0 else []),
               incs=([(s_mm, 1)] if kc == 15 else []))
        ctx.cache[("a0tr", ti)] = s_mm.n
        mmv = s_mm.n
        dst = XNF[:, :].rearrange("p (k t) -> p k t", k=16)[:, :, ti * 128:(ti + 1) * 128]
        op("vector", lambda e, dst=dst:
           e.tensor_copy(dst, psB[:, :].rearrange("p (k t) -> p k t", k=16)),
           waits=[(s_mm, mmv)], incs=[(s_ve, 1)])
        ctx.cache[("a0ev", ti)] = s_ve.n
    xnf_ready = s_ve.n
    op("sync", lambda e: e.dma_start(out=xn0_v[:, :, :],
                                     in_=XNF[:, :].rearrange("p (k t) -> p k t", k=16)),
       waits=[(s_ve, xnf_ready)], incs=[(s_st, 16)])

    WQB = abf[:, 36864:45056]
    qev = 0
    for og in range(4):
        op("gpsimd", lambda e, og=og:
           e.dma_start(out=WQB.rearrange("p (k o) -> p k o", k=16),
                       in_=wq_src[:, :, og * 512:(og + 1) * 512]),
           waits=[(s_mm, s_mm.n)], incs=[(s_wl, 16)])
        wv = s_wl.n
        for ti in range(16):
            bank = (ti % 2) * 512
            for kc in range(16):
                op("tensor", lambda e, ti=ti, kc=kc, bank=bank:
                   e.matmul(psA[:, bank:bank + 512], xnf_tile(kc, ti * 128),
                            WQB[:, kc * 512:(kc + 1) * 512],
                            start=(kc == 0), stop=False),
                   waits=([(s_wl, wv), (s_ve, xnf_ready), (s_ac, qev)] if kc == 0 else []),
                   incs=[])
            op("tensor", lambda e, og=og, bank=bank:
               e.matmul(psA[:, bank:bank + 512], ones[:, :],
                        sb["bq_row"][0:1, og * 512:(og + 1) * 512], start=False, stop=True),
               incs=[(s_mm, 1)])
            mmv = s_mm.n
            qst = abf[:, 45056 + (ti % 2) * 512: 45056 + (ti % 2) * 512 + 512]
            op("scalar", lambda e, qst=qst, bank=bank: e.activation(qst, psA[:, bank:bank + 512], AF.Copy),
               waits=[(s_mm, mmv), (s_sA, s_sA.n)], incs=[(s_ac, 1)])
            qev = s_ac.n
            op("sync", lambda e, qst=qst, ti=ti, og=og:
               e.dma_start(out=q0_v[ti * 128:(ti + 1) * 128, og * 512:(og + 1) * 512], in_=qst),
               waits=[(s_ac, qev)], incs=[(s_sA, 16)])
    a0_done = (s_st.n, s_gst.n)

    WKB = abf[:, 20480:28672]
    WVB = abf[:, 28672:36864]
    op("gpsimd", lambda e: e.dma_start(out=WKB.rearrange("p (k o) -> p k o", k=16),
                                       in_=inp["l0_wk"].ap().rearrange("(k p) o -> p k o", p=128)),
       waits=[(s_mm, s_mm.n), (s_st, s_st.n), (s_sA, s_sA.n)], incs=[(s_wl, 16)])
    op("gpsimd", lambda e: e.dma_start(out=WVB.rearrange("p (k o) -> p k o", k=16),
                                       in_=inp["l0_wv"].ap().rearrange("(k p) o -> p k o", p=128)),
       incs=[(s_wl, 16)])
    wkv_v = s_wl.n
    k0_v = dr["k0d"].ap()
    v0_v = dr["v0d"].ap().rearrange("(h p) s -> p h s", h=NH)
    ln_acv2 = 0
    kst_ve = {}
    for mac in range(16):
        XF = abf[:, 4096 + (mac % 2) * 8192: 4096 + (mac % 2) * 8192 + 8192]
        for sub in range(4):
            ti = mac * 4 + sub
            xt = XT[ti % 2]
            s_x = [sems["s_xA"], sems["s_xB"]][ti % 2]
            op("sync", lambda e, xt=xt, ti=ti: e.dma_start(out=xt, in_=x_full_t[ti]),
               waits=[(s_ac, ctx.cache.get(("a1ln", ti - 2), 0))] if ti else
                     [(s_ac, s_ac.n), (s_ve, s_ve.n)],
               incs=[(s_x, 16)])
            lv = s_x.n
            ctx.wait("vector", [(s_x, lv)])
            ctx.wait("scalar", [(s_x, lv)])
            if ti >= 2:
                ctx.wait("scalar", [(s_mm, ctx.cache.get(("a1tr", ti - 2), 0))])
            _, ln_acv2 = emit_ln(ctx, sems, sb, xt, SQp[ti % 2], stat, XNT[ti % 2], cb=(ti % 2) * 40)
            ctx.cache[("a1ln", ti)] = ln_acv2
            acv = s_ac.n
            for kc in range(16):
                op("tensor", lambda e, ti=ti, kc=kc:
                   e.matmul(psB[:, kc * 128:(kc + 1) * 128], XNT[ti % 2][:, kc * 128:(kc + 1) * 128],
                            ident[:, :], start=True, stop=True),
                   waits=([(s_ac, acv), (s_ve, ctx.cache.get(("a1ev", ti - 2), 0))] if kc == 0 else []),
                   incs=([(s_mm, 1)] if kc == 15 else []))
            ctx.cache[("a1tr", ti)] = s_mm.n
            mmv = s_mm.n
            dst = XF.rearrange("p (k t) -> p k t", k=16)[:, :, sub * 128:(sub + 1) * 128]
            op("vector", lambda e, dst=dst:
               e.tensor_copy(dst, psB[:, :].rearrange("p (k t) -> p k t", k=16)),
               waits=([(s_mm, mmv)] + ([(s_st, s_st.n), (s_sA, s_sA.n)] if ti == 0 else [])),
               incs=[(s_ve, 1)])
            ctx.cache[("a1ev", ti)] = s_ve.n
        xf_ready = s_ve.n

        for h in range(NH):
            bank = (h % 2) * 512
            for kc in range(16):
                op("tensor", lambda e, h=h, kc=kc, bank=bank, XF=XF:
                   e.matmul(psA[:, bank:bank + 512], WKB[:, kc * 512 + h * 128: kc * 512 + h * 128 + 128],
                            XF[:, kc * 512:(kc + 1) * 512], start=(kc == 0), stop=(kc == 15)),
                   waits=([(s_wl, wkv_v), (s_ve, xf_ready), (s_ac, kst_ve.get(("k", h % 2), 0))]
                          if kc == 0 else []),
                   incs=([(s_mm, 1)] if kc == 15 else []))
            mmv = s_mm.n
            kst = abf[:, 36864 + (h % 2) * 512: 36864 + (h % 2) * 512 + 512]
            op("scalar", lambda e, kst=kst, bank=bank, h=h:
               e.activation(kst, psA[:, bank:bank + 512], AF.Identity, bias=sb["bk_sb"][:, h:h + 1]),
               waits=[(s_mm, mmv), (s_sA, s_sA.n)], incs=[(s_ac, 1)])
            kst_ve[("k", h % 2)] = s_ac.n
            acv = s_ac.n
            op("sync", lambda e, kst=kst, h=h, mac=mac:
               e.dma_start(out=k0_v[h * 128:(h + 1) * 128, mac * 512:(mac + 1) * 512], in_=kst),
               waits=[(s_ac, acv)], incs=[(s_sA, 16)])
        for sub in range(4):
            bank = 1024 + (sub % 2) * 512
            for kc in range(16):
                op("tensor", lambda e, sub=sub, kc=kc, bank=bank, XF=XF:
                   e.matmul(psA[:, bank:bank + 512], XF[:, kc * 512 + sub * 128: kc * 512 + sub * 128 + 128],
                            WVB[:, kc * 512:(kc + 1) * 512], start=(kc == 0), stop=False),
                   waits=([(s_ve, xf_ready), (s_ac, kst_ve.get(("v", sub % 2), 0))] if kc == 0 else []),
                   incs=[])
            op("tensor", lambda e, bank=bank:
               e.matmul(psA[:, bank:bank + 512], ones[:, :], sb["bv_row"][0:1, :], start=False, stop=True),
               incs=[(s_mm, 1)])
            mmv = s_mm.n
            vst = abf[:, 37888 + (sub % 2) * 512: 37888 + (sub % 2) * 512 + 512]
            op("scalar", lambda e, vst=vst, bank=bank:
               e.activation(vst, psA[:, bank:bank + 512], AF.Copy),
               waits=[(s_mm, mmv), (s_sB, s_sB.n)], incs=[(s_ac, 1)])
            kst_ve[("v", sub % 2)] = s_ac.n
            acv = s_ac.n
            n = mac * 4 + sub
            op("sync", lambda e, vst=vst, n=n:
               e.dma_start(out=v0_v[:, :, n * 128:(n + 1) * 128],
                           in_=vst.rearrange("p (h j) -> p h j", h=4)),
               waits=[(s_ac, acv)], incs=[(s_sB, 16)])
    a1_done = s_st.n

    emit_attention(ctx, nc, sems, sb, psA, psB, layer=0,
                   kd=dr["k0d"], vd=dr["v0d"], qd=dr["q0d"], attn_out=dr["attn_h0"],
                   proj_done=s_st.n)

    barrier = [(s_st, s_st.n), (s_gst, s_gst.n), (s_sA, s_sA.n), (s_sB, s_sB.n)]
    op("gpsimd", lambda e: e.collective_compute(
        "AllGather", ALU.bypass, replica_groups=[[0, 1, 2, 3], [4, 5, 6, 7]],
        ins=[dr["attn_h0"].ap().opt()], outs=[dr["ag0"].ap().opt()]),
       waits=barrier, incs=[(s_cc, 1)])
    ccv = s_cc.n

    if dbg:
        for nm, srcn in (("dbg_q0", "q0d"), ("dbg_k0", "k0d"), ("dbg_v0", "v0d"),
                         ("dbg_attnh0", "attn_h0")):
            op("sync", lambda e, nm=nm, srcn=srcn: e.dma_start(out=dbg[nm][:, :], in_=dr[srcn][:, :]),
               waits=barrier, incs=[(s_st, 16)])

    emit_wo_base(ctx, nc, sems, sb, psA, psB, ag=dr["ag0"], w_in=inp["l0_wo"],
                 bias_row=sb["bo_row"], x_res=inp["x_own"], base=dr["base0"], cc_wait=ccv,
                 own_scr=dr["attn_own0"])
    if dbg:
        op("sync", lambda e: e.dma_start(out=dbg["dbg_base0"][:, :], in_=dr["base0"][:, :]),
           waits=[(s_st, s_st.n), (s_sB, s_sB.n), (s_ve, s_ve.n)], incs=[(s_st, 16)])

    emit_ffn0(ctx, nc, sems, sb, psA, psB, dr, inp)
    if dbg:
        op("sync", lambda e: e.dma_start(out=dbg["dbg_x1"][:, :], in_=dr["x1d"][:, :]),
           waits=[(s_st, s_st.n), (s_sA, s_sA.n)], incs=[(s_st, 16)])

    op("gpsimd", lambda e: e.collective_compute(
        "AllGather", ALU.bypass, replica_groups=[[0, 1, 2, 3], [4, 5, 6, 7]],
        ins=[dr["x1d"].ap().opt()], outs=[dr["x1ag"].ap().opt()]),
       waits=[(s_st, s_st.n), (s_gst, s_gst.n), (s_sA, s_sA.n), (s_sB, s_sB.n)],
       incs=[(s_cc, 1)])


def emit_attention(ctx, nc, sems, sb, psA, psB, layer, kd, vd, qd, attn_out, proj_done):
    op = ctx.op
    s_al, s_st, s_mm, s_ve, s_ac = (sems[k] for k in ("s_al", "s_st", "s_mm", "s_ve", "s_ac"))
    s_kv, s_qA, s_qB, s_sA, s_sB = (sems[k] for k in ("s_kv", "s_qA", "s_qB", "s_sA", "s_sB"))
    s_gst = sems["s_gst"]
    abf, af32, stat = sb["abf"], sb["af32"], sb["stat"]
    ident = sb["ident_bf"]
    KF = abf[:, 0:8192]
    VT = abf[:, 8192:16384]
    KSUB = abf[:, 16384:17408]
    QT = abf[:, 17408:25600]
    PRB = [abf[:, 25600:26624], abf[:, 26624:27648]]
    PTS = [[abf[:, 27648:28160], abf[:, 28160:28672]],
           [abf[:, 29696:30208], abf[:, 30208:30720]]]
    AST = [abf[:, 28672:28800], abf[:, 28800:28928]]
    NEGC = af32[:, 0:64]
    masks = sb["masks0"] if layer == 0 else sb["masks1"]

    if layer == 0:
        q_view = qd.ap().rearrange("(j r) (m d) -> j (r m) d", j=4, m=16)
    else:
        q_view = qd.ap()

    def t_block(h, qb, w, sts):
        prb = PRB[w % 2]
        expv = ctx.cache[("exp", h, qb, w)]
        for i, st in enumerate(sts):
            for j in range(4):
                op("tensor", lambda e, prb=prb, i=i, j=j:
                   e.matmul(psB[:, i * 512 + j * 128: i * 512 + j * 128 + 128],
                            prb[:, i * 512 + j * 128: i * 512 + j * 128 + 128],
                            ident[:, :], start=True, stop=True),
                   waits=([(s_ac, expv), ctx.cache.get(("ptseL", i), (s_ve, 0))]
                          if j == 0 else []),
                   incs=([(s_mm, 1)] if j == 3 else []))
            mmv = s_mm.n
            pts = PTS[w % 2][i]
            op("vector", lambda e, pts=pts, i=i: e.tensor_copy(pts, psB[:, i * 512:(i + 1) * 512]),
               waits=[(s_mm, mmv)], incs=[(s_ve, 1)])
            ctx.cache[("ptseL", i)] = (s_ve, s_ve.n)
            ctx.cache[("ptse2", h, qb, w, i)] = (s_ve, s_ve.n)

    def pv_block(h, qb, w, sts, nst):
        for i, st in enumerate(sts):
            pts = PTS[w % 2][i]
            evw = ctx.cache[("ptse2", h, qb, w, i)]
            for j in range(4):
                first = (st == 0 and j == 0)
                last = (st == nst - 1 and j == 3)
                op("tensor", lambda e, pts=pts, st=st, j=j, first=first, last=last:
                   e.matmul(psB[:, 1536:1664], pts[:, j * 128:(j + 1) * 128],
                            VT[:, (st * 4 + j) * 128:(st * 4 + j) * 128 + 128],
                            start=first, stop=last),
                   waits=([evw] if j == 0 else []) +
                         ([(s_ac, ctx.cache.get(("nrm", h, qb - 1), 0))] if first else []),
                   incs=([(s_mm, 1)] if last else []))
        ctx.cache[("tpv", h, qb, None)] = s_mm.n

    for h in range(NH):
        kvh = h if layer == 0 else h // 2
        op("sync", lambda e, kvh=kvh, kd=kd: e.dma_start(out=KF, in_=kd.ap()[kvh * 128:(kvh + 1) * 128, :]),
           waits=[(s_st, s_st.n), (s_sA, s_sA.n), (s_sB, s_sB.n), (s_gst, s_gst.n),
                  (sems["s_sC"], sems["s_sC"].n),
                  (s_mm, s_mm.n), (s_ve, s_ve.n), (s_ac, s_ac.n)],
           incs=[(s_kv, 16)])
        op("sync", lambda e, kvh=kvh, vd=vd: e.dma_start(out=VT, in_=vd.ap()[kvh * 128:(kvh + 1) * 128, :]),
           incs=[(s_kv, 16)])
        kv_v = s_kv.n
        op("vector", lambda e: e.tensor_copy(KSUB, KF.rearrange("p (n e) -> p n e", e=8)[:, :, 0]),
           waits=[(s_kv, kv_v)], incs=[(s_ve, 1)])
        ksub_v = s_ve.n

        for qb in range(64):
            s_q = [s_qA, s_qB][qb % 2]
            if layer == 0:
                qn = abf[:, 28928 + (qb % 2) * 128: 28928 + (qb % 2) * 128 + 128]
                op("sync", lambda e, qn=qn, h=h, qb=qb:
                   e.dma_start(out=qn, in_=q_view[h, qb * 128:(qb + 1) * 128, :]),
                   waits=[(s_mm, ctx.cache.get(("qtr", h, qb - 2), 0))], incs=[(s_q, 16)])
                qlv = s_q.n
                tb = 1024 + (qb % 2) * 512
                op("tensor", lambda e, qn=qn, tb=tb:
                   e.matmul(psB[:, tb:tb + 128], qn, ident[:, :], start=True, stop=True),
                   waits=[(s_q, qlv), (s_ve, ctx.cache.get(("qtc", h, qb - 2), 0))],
                   incs=[(s_mm, 1)])
                ctx.cache[("qtr", h, qb)] = s_mm.n
                mmv = s_mm.n
                op("vector", lambda e, qb=qb, tb=tb:
                   e.tensor_copy(QT[:, qb * 128:(qb + 1) * 128], psB[:, tb:tb + 128]),
                   waits=[(s_mm, mmv)], incs=[(s_ve, 1)])
                ctx.cache[("qtc", h, qb)] = s_ve.n
                qt_w = (s_ve, s_ve.n)
            else:
                op("sync", lambda e, h=h, qb=qb:
                   e.dma_start(out=QT[:, qb * 128:(qb + 1) * 128],
                               in_=q_view[h * 128:(h + 1) * 128, qb * 128:(qb + 1) * 128]),
                   waits=[(s_mm, ctx.cache.get(("sqk", h, qb - 2), 0))], incs=[(s_q, 16)])
                qt_w = (s_q, s_q.n)
            sb0 = (qb % 2) * 1024
            for half in range(2):
                op("tensor", lambda e, qb=qb, half=half, sb0=sb0:
                   e.matmul(psA[:, sb0 + half * 512: sb0 + half * 512 + 512],
                            QT[:, qb * 128:(qb + 1) * 128], KSUB[:, half * 512:(half + 1) * 512],
                            start=True, stop=True),
                   waits=([qt_w, (s_ve, ksub_v),
                           (s_ve, ctx.cache.get(("ngc", h, qb - 2), 0))] if half == 0 else []),
                   incs=([(s_mm, 1)] if half == 1 else []))
            ctx.cache[("sqk", h, qb)] = s_mm.n
            mmv = s_mm.n
            nst = (qb // 4 + 1) if layer == 0 else 16
            ncols = min(nst * 512, 8192) // 8
            op("vector", lambda e, qb=qb, ncols=ncols, sb0=sb0:
               e.reduce_max(NEGC[:, qb:qb + 1], psA[:, sb0:sb0 + ncols], axis=AX, negate=True),
               waits=[(s_mm, mmv)], incs=[(s_ve, 1)])
            ctx.cache[("ngc", h, qb)] = s_ve.n
        op("vector", lambda e: e.tensor_scalar_mul(NEGC[:, :], NEGC[:, :], SCL),
           incs=[(s_ve, 1)])
        negc_v = s_ve.n

        pending = None

        def epilogue(qb2, h=h):
            zb = 9 + (qb2 % 2) * 3
            pv_v = ctx.cache[("tpv", h, qb2, None)]
            op("vector", lambda e, zb=zb: e.reciprocal(stat[:, zb + 2:zb + 3], stat[:, zb + 1:zb + 2]),
               incs=[(s_ve, 1)])
            rzv = s_ve.n
            ast = AST[qb2 % 2]
            op("scalar", lambda e, ast=ast, zb=zb:
               e.activation(ast, psB[:, 1536:1664], AF.Copy, scale=stat[:, zb + 2:zb + 3]),
               waits=[(s_mm, pv_v), (s_ve, rzv), (s_sA, s_sA.n)],
               incs=[(s_ac, 1)])
            ctx.cache[("nrm", h, qb2)] = s_ac.n
            acv = s_ac.n
            op("sync", lambda e, ast=ast, h=h, qb2=qb2:
               e.dma_start(out=attn_out.ap()[qb2 * 128:(qb2 + 1) * 128, h * 128:(h + 1) * 128], in_=ast),
               waits=[(s_ac, acv)], incs=[(s_sA, 16)])

        for qb in range(64):
            nst = (qb // 4 + 1) if layer == 0 else 16
            diag = qb // 4 if layer == 0 else -1
            nwav = (nst + 1) // 2
            waves = [[st for st in (2 * w, 2 * w + 1) if st < nst] for w in range(nwav)]
            for w, sts in enumerate(waves):
                half = (w % 2) * 1024
                wlen = len(sts) * 512
                for i, st in enumerate(sts):
                    wts = []
                    if i == 0:
                        wts = [(s_ve, negc_v),
                               (s_ac, ctx.cache.get(("exp", h, qb, w - 2),
                                                    ctx.cache.get(("expL", h, qb - 1, (w % 2)), 0)))]
                    op("tensor", lambda e, qb=qb, st=st, half=half, i=i:
                       e.matmul(psA[:, half + i * 512: half + i * 512 + 512],
                                QT[:, qb * 128:(qb + 1) * 128], KF[:, st * 512:(st + 1) * 512],
                                start=True, stop=True),
                       waits=wts,
                       incs=([(s_mm, 1)] if i == len(sts) - 1 else []))
                mmv = s_mm.n
                for i, st in enumerate(sts):
                    mi = -1
                    if layer == 0 and st == diag:
                        mi = qb % 4
                    if layer == 1 and qb >= 56 and st >= 14:
                        mi = (qb - 56) * 2 + (st - 14)
                    if mi >= 0:
                        op("vector", lambda e, half=half, i=i, mi=mi:
                           e.tensor_tensor(psA[:, half + i * 512: half + i * 512 + 512],
                                           psA[:, half + i * 512: half + i * 512 + 512],
                                           masks[:, mi * 512:(mi + 1) * 512], ALU.add),
                           waits=[(s_mm, mmv)], incs=[(s_ve, 1)])
                if w >= 1:
                    t_block(h, qb, w - 1, waves[w - 1])
                if w >= 2:
                    pv_block(h, qb, w - 2, waves[w - 2], nst)
                prb = PRB[w % 2]
                zb = 9 + (qb % 2) * 3
                op("scalar", lambda e, qb=qb, half=half, wlen=wlen, prb=prb, zb=zb:
                   e.activation(prb[:, 0:wlen], psA[:, half:half + wlen], AF.Exp,
                                scale=SCL, bias=NEGC[:, qb:qb + 1], accum_out=stat[:, zb:zb + 1]),
                   waits=[(s_mm, mmv), (s_ve, s_ve.n)],
                   incs=[(s_ac, 1)])
                ctx.cache[("exp", h, qb, w)] = s_ac.n
                expv = s_ac.n
                if w == 0:
                    op("vector", lambda e, zb=zb: e.tensor_copy(stat[:, zb + 1:zb + 2], stat[:, zb:zb + 1]),
                       waits=[(s_ac, expv)], incs=[(s_ve, 1)])
                    if pending is not None:
                        epilogue(pending)
                        pending = None
                else:
                    op("vector", lambda e, zb=zb: e.tensor_add(stat[:, zb + 1:zb + 2], stat[:, zb + 1:zb + 2],
                                                               stat[:, zb:zb + 1]),
                       waits=[(s_ac, expv)], incs=[(s_ve, 1)])
            p_last = (nwav - 1) % 2
            ctx.cache[("expL", h, qb, p_last)] = ctx.cache[("exp", h, qb, nwav - 1)]
            ctx.cache[("expL", h, qb, 1 - p_last)] = ctx.cache.get(
                ("exp", h, qb, nwav - 2), ctx.cache[("exp", h, qb, nwav - 1)])
            t_block(h, qb, nwav - 1, waves[nwav - 1])
            if nwav >= 2:
                pv_block(h, qb, nwav - 2, waves[nwav - 2], nst)
            pv_block(h, qb, nwav - 1, waves[nwav - 1], nst)
            pending = qb
        epilogue(pending)
        pending = None


def emit_wo_base(ctx, nc, sems, sb, psA, psB, ag, w_in, bias_row, x_res, base, cc_wait, own_scr):
    op = ctx.op
    s_wl, s_al, s_st, s_mm, s_ve, s_ac, s_cc = (
        sems[k] for k in ("s_wl", "s_al", "s_st", "s_mm", "s_ve", "s_ac", "s_cc"))
    s_gb, s_x2, s_sB = sems["s_gb"], sems["s_x2"], sems["s_sB"]
    abf, af32 = sb["abf"], sb["af32"]
    ident, ones = sb["ident_bf"], sb["ones_col"]
    WOB = abf[:, 0:32768]
    ATMp = [abf[:, 32768:34816], abf[:, 38912:40960]]
    AFM = [abf[:, 34816:36864], abf[:, 36864:38912]]
    x_res_t = x_res.ap().rearrange("(n p) d -> n p d", p=128)
    base_t = base.ap().rearrange("(n p) d -> n p d", p=128)

    op("gpsimd", lambda e: e.dma_start(out=WOB.rearrange("p (k o) -> p k o", k=16),
                                       in_=w_in.ap().rearrange("(k p) o -> p k o", p=128)),
       waits=[(s_mm, s_mm.n), (s_ve, s_ve.n)], incs=[(s_wl, 16)])
    wo_v = s_wl.n

    if ctx.who == "gpsimd" and "my_off" not in ctx.cache:
        pid = ctx.eng.partition_id()
        ctx.cache["my_off"] = ctx.eng.snap((pid % 4) * TOK)
    for r in range(4):
        def gfn(e, r=r):
            mo = ctx.cache["my_off"]
            return e.dma_start(out=own_scr.ap()[:, r * 512:(r + 1) * 512],
                               in_=ag.ap()[r * S:, :][bass.ds(mo, TOK), :])
        op("gpsimd", gfn, waits=([(s_cc, cc_wait)] if r == 0 else []), incs=[(s_gb, 16)])
    own_v = s_gb.n
    own_t = own_scr.ap().rearrange("(n p) d -> n p d", p=128)

    for t in range(16):
        ATM = ATMp[t % 2]
        s_x = [sems["s_xA"], sems["s_xB"]][t % 2]
        op("sync", lambda e, t=t, ATM=ATM: e.dma_start(out=ATM, in_=own_t[t]),
           waits=[(s_gb, own_v), (s_mm, ctx.cache.get(("wotr", t - 2), 0))],
           incs=[(s_x, 16)])
        gv = (s_x, s_x.n)
        op("sync", lambda e, t=t: e.dma_start(out=af32[:, 0:2048], in_=x_res_t[t]),
           waits=[(s_ve, ctx.cache.get(("wocb", t - 1), 0))], incs=[(s_x2, 16)])
        xv = s_x2.n
        for kc in range(16):
            op("tensor", lambda e, kc=kc, ATM=ATM:
               e.matmul(psB[:, kc * 128:(kc + 1) * 128], ATM[:, kc * 128:(kc + 1) * 128],
                        ident[:, :], start=True, stop=True),
               waits=([gv, (s_wl, wo_v), (s_ve, ctx.cache.get(("woev", t - 1), 0))]
                      if kc == 0 else []),
               incs=([(s_mm, 1)] if kc == 15 else []))
        ctx.cache[("wotr", t)] = s_mm.n
        mmv = s_mm.n
        afm = AFM[t % 2]
        op("vector", lambda e, afm=afm: e.tensor_copy(afm, psB[:, :]),
           waits=[(s_mm, mmv)], incs=[(s_ve, 1)])
        ctx.cache[("woev", t)] = s_ve.n
        afm_v = s_ve.n
        for og in range(4):
            bank = (og % 2) * 512
            for kc in range(16):
                op("tensor", lambda e, afm=afm, kc=kc, og=og, bank=bank:
                   e.matmul(psA[:, bank:bank + 512], afm[:, kc * 128:(kc + 1) * 128],
                            WOB[:, kc * 2048 + og * 512: kc * 2048 + og * 512 + 512],
                            start=(kc == 0), stop=(kc == 15 and bias_row is None)),
                   waits=([(s_ve, afm_v), (s_ve, ctx.cache.get(("wocbv", t, og - 2), 0))]
                          if kc == 0 else []),
                   incs=([(s_mm, 1)] if (kc == 15 and bias_row is None) else []))
            if bias_row is not None:
                op("tensor", lambda e, og=og, bank=bank:
                   e.matmul(psA[:, bank:bank + 512], ones[:, :], bias_row[0:1, og * 512:(og + 1) * 512],
                            start=False, stop=True),
                   incs=[(s_mm, 1)])
            mmv = s_mm.n
            op("vector", lambda e, og=og, bank=bank:
               e.tensor_tensor(af32[:, 2048 + og * 512: 2048 + og * 512 + 512],
                               psA[:, bank:bank + 512], af32[:, og * 512:(og + 1) * 512], ALU.add),
               waits=[(s_mm, mmv), (s_x2, xv), (s_sB, s_sB.n)],
               incs=[(s_ve, 1)])
            ctx.cache[("wocbv", t, og)] = s_ve.n
        ctx.cache[("wocb", t)] = s_ve.n
        cbv = s_ve.n
        op("sync", lambda e, t=t: e.dma_start(out=base_t[t], in_=af32[:, 2048:4096]),
           waits=[(s_ve, cbv)], incs=[(s_sB, 16)])


def emit_ffn0(ctx, nc, sems, sb, psA, psB, dr, inp):
    op = ctx.op
    s_wl, s_al, s_st, s_mm, s_ve, s_ac = (
        sems[k] for k in ("s_wl", "s_al", "s_st", "s_mm", "s_ve", "s_ac"))
    s_wA, s_wB, s_x2, s_sA, s_sB = (sems[k] for k in ("s_wA", "s_wB", "s_x2", "s_sA", "s_sB"))
    abf, af32 = sb["abf"], sb["af32"]
    su_ug, su_dp = sb["su_ug"], sb["su_dp"]
    XNF = abf[:, 0:8192]
    UBF = abf[:, 0:8192]
    CBF = abf[:, 8192:40960]
    WUBp = [abf[:, 40960:43008], abf[:, 48128:50176]]
    WGBp = [abf[:, 43008:45056], abf[:, 50176:52224]]
    XH = af32[:, 0:8192]
    SIL = af32[:, 8192:8704]
    xn0_v = dr["xn0d"].ap().rearrange("(k p) t -> p k t", p=128)
    wupf_v = dr["wupf"].ap().rearrange("(k p) h -> p k h", p=128)
    wgatef_v = dr["wgatef"].ap().rearrange("(k p) h -> p k h", p=128)
    base_t = dr["base0"].ap().rearrange("(n p) d -> n p d", p=128)
    x1_t = dr["x1d"].ap().rearrange("(n p) d -> n p d", p=128)

    for tt in range(4):
        op("sync", lambda e, tt=tt:
           e.dma_start(out=XNF.rearrange("p (k t) -> p k t", k=16), in_=xn0_v[:, :, tt * 512:(tt + 1) * 512]),
           waits=[(s_mm, s_mm.n), (s_ve, s_ve.n), (s_st, s_st.n),
                  (sems["s_gst"], sems["s_gst"].n), (sems["s_gb"], sems["s_gb"].n)],
           incs=[(s_al, 16)])
        lv = s_al.n
        ctx.cache[("f0mm", tt, -1)] = s_mm.n
        for kc in range(16):
            op("vector", lambda e, kc=kc:
               e.tensor_scalar_mul(XH[:, kc * 512:(kc + 1) * 512], XNF[:, kc * 512:(kc + 1) * 512],
                                   su_ug[:, kc:kc + 1]),
               waits=([(s_al, lv)] if kc == 0 else []), incs=([(s_ve, 1)] if kc == 15 else []))
        for stg in range(4):
            b = 1 << stg
            for i in range(16):
                if i & b:
                    continue
                p = i | b
                op("vector", lambda e, i=i, p=p:
                   e.tensor_add(XH[:, i * 512:(i + 1) * 512], XH[:, i * 512:(i + 1) * 512],
                                XH[:, p * 512:(p + 1) * 512]), incs=[])
                op("vector", lambda e, i=i, p=p:
                   e.scalar_tensor_tensor(XH[:, p * 512:(p + 1) * 512], XH[:, p * 512:(p + 1) * 512],
                                          -2.0, XH[:, i * 512:(i + 1) * 512], ALU.mult, ALU.add),
                   incs=[])
        for kc in range(16):
            op("vector", lambda e, kc=kc:
               e.tensor_copy(UBF[:, kc * 512:(kc + 1) * 512], XH[:, kc * 512:(kc + 1) * 512]),
               incs=([(s_ve, 1)] if kc == 15 else []))
        u_v = s_ve.n
        for hc in range(64):
            WUB, WGB = WUBp[hc % 2], WGBp[hc % 2]
            s_w = [s_wA, s_wB][hc % 2]
            op("sync", lambda e, hc=hc, WUB=WUB:
               e.dma_start(out=WUB.rearrange("p (k h) -> p k h", k=16), in_=wupf_v[:, :, hc * 128:(hc + 1) * 128]),
               waits=[(s_mm, ctx.cache.get(("f0mm", tt, hc - 2), ctx.cache.get(("f0mm", tt, -1), 0)))],
               incs=[(s_w, 16)])
            op("sync", lambda e, hc=hc, WGB=WGB:
               e.dma_start(out=WGB.rearrange("p (k h) -> p k h", k=16), in_=wgatef_v[:, :, hc * 128:(hc + 1) * 128]),
               incs=[(s_w, 16)])
            wv = (s_w, s_w.n)
            bu = (hc % 2) * 1024
            for kc in range(16):
                op("tensor", lambda e, kc=kc, bu=bu, WUB=WUB:
                   e.matmul(psA[:, bu:bu + 512], WUB[:, kc * 128:(kc + 1) * 128],
                            UBF[:, kc * 512:(kc + 1) * 512], start=(kc == 0), stop=(kc == 15)),
                   waits=([wv, (s_ve, u_v), (s_ve, ctx.cache.get(("f0c", tt, hc - 2), 0))]
                          if kc == 0 else []), incs=[])
            for kc in range(16):
                op("tensor", lambda e, kc=kc, bu=bu, WGB=WGB:
                   e.matmul(psA[:, bu + 512:bu + 1024], WGB[:, kc * 128:(kc + 1) * 128],
                            UBF[:, kc * 512:(kc + 1) * 512], start=(kc == 0), stop=(kc == 15)),
                   incs=([(s_mm, 1)] if kc == 15 else []))
            ctx.cache[("f0mm", tt, hc)] = s_mm.n
            mmv = s_mm.n
            op("scalar", lambda e, bu=bu: e.activation(SIL, psA[:, bu + 512:bu + 1024], AF.Silu),
               waits=[(s_mm, mmv), (s_ve, ctx.cache.get(("f0c", tt, hc - 1), 0))], incs=[(s_ac, 1)])
            acv = s_ac.n
            op("vector", lambda e, hc=hc, bu=bu:
               e.scalar_tensor_tensor(CBF[:, hc * 512:(hc + 1) * 512], SIL, su_dp[:, hc:hc + 1],
                                      psA[:, bu:bu + 512], ALU.mult, ALU.mult),
               waits=[(s_ac, acv)], incs=[(s_ve, 1)])
            ctx.cache[("f0c", tt, hc)] = s_ve.n
        c_v = s_ve.n
        for sub in range(4):
            op("sync", lambda e, sub=sub, tt=tt:
               e.dma_start(out=XH[:, sub * 2048:(sub + 1) * 2048], in_=base_t[tt * 4 + sub]),
               waits=([(s_ve, c_v), (s_sB, s_sB.n)] if sub == 0 else []), incs=[(s_x2, 16)])
        bv = s_x2.n
        ctx.cache[("f0dmm0", tt, 0)] = s_mm.n
        wdownf_v = dr["wdownf"].ap()
        WDB = [abf[:, 40960:43008], abf[:, 43008:45056]]
        for og in range(4):
            for hc4 in range(16):
                wb = WDB[hc4 % 2]
                s_w = [s_wA, s_wB][hc4 % 2]
                op("sync", lambda e, wb=wb, hc4=hc4, og=og:
                   e.dma_start(out=wb.rearrange("p (k o) -> p k o", k=4),
                               in_=wdownf_v[hc4 * 512:(hc4 + 1) * 512, og * 512:(og + 1) * 512]
                               .rearrange("(k p) o -> p k o", p=128)),
                   waits=[(s_mm, ctx.cache.get(("f0dmm", tt, og, hc4 - 2),
                                               ctx.cache.get(("f0dmm0", tt, og), 0)))],
                   incs=[(s_w, 16)])
                wv = (s_w, s_w.n)
                for k in range(4):
                    hc = hc4 * 4 + k
                    for sub in range(4):
                        w_ = []
                        if k == 0 and sub == 0:
                            w_ = [wv, (s_ve, c_v)]
                            if hc == 0:
                                w_.append((s_ve, ctx.cache.get(("f0ev", tt, og - 1), 0)))
                        op("tensor", lambda e, wb=wb, k=k, hc=hc, sub=sub:
                           e.matmul(psB[:, sub * 512:(sub + 1) * 512],
                                    CBF[:, hc * 512 + sub * 128: hc * 512 + sub * 128 + 128],
                                    wb[:, k * 512:(k + 1) * 512],
                                    start=(hc == 0), stop=(hc == 63)),
                           waits=w_,
                           incs=([(s_mm, 1)] if (k == 3 and sub == 3) else []))
                ctx.cache[("f0dmm", tt, og, hc4)] = s_mm.n
            ctx.cache[("f0dmm0", tt, og + 1)] = s_mm.n
            dn_v = s_mm.n
            for sub in range(4):
                op("vector", lambda e, sub=sub, og=og:
                   e.tensor_tensor(af32[:, 10240 + sub * 512: 10240 + sub * 512 + 512],
                                   psB[:, sub * 512:(sub + 1) * 512],
                                   XH[:, sub * 2048 + og * 512: sub * 2048 + og * 512 + 512], ALU.add),
                   waits=[(s_mm, dn_v), (s_x2, bv), (s_sA, s_sA.n)],
                   incs=[(s_ve, 1)])
                vv = s_ve.n
                op("sync", lambda e, sub=sub, og=og, tt=tt:
                   e.dma_start(out=x1_t[tt * 4 + sub][:, og * 512:(og + 1) * 512],
                               in_=af32[:, 10240 + sub * 512: 10240 + sub * 512 + 512]),
                   waits=[(s_ve, vv)], incs=[(s_sA, 16)])
            ctx.cache[("f0ev", tt, og)] = s_ve.n


def emit_layer1(ctx, nc, inp, out, dr, sb, psA, psB, sems, dbg):
    op = ctx.op
    s_wl, s_al, s_st, s_gst, s_mm, s_ve, s_ac, s_cc = (
        sems[k] for k in ("s_wl", "s_al", "s_st", "s_gst", "s_mm", "s_ve", "s_ac", "s_cc"))
    s_sA, s_sB = sems["s_sA"], sems["s_sB"]
    abf, af32, stat = sb["abf"], sb["af32"], sb["stat"]
    ident, ones = sb["ident_bf"], sb["ones_col"]
    x1ag_cc = s_cc.n

    XT = [af32[:, 0:2048], af32[:, 2048:4096]]
    SQp = [af32[:, 4096:6144], af32[:, 6144:8192]]
    SQ = af32[:, 8704:9728]
    RMS = af32[:, 8192:8704]
    XNT = [abf[:, 0:2048], abf[:, 2048:4096]]
    WQB = abf[:, 20480:28672]
    WKB = abf[:, 28672:32768]
    WVB = abf[:, 32768:36864]
    x1ag_t = dr["x1ag"].ap().rearrange("(n p) d -> n p d", p=128)
    q1_v = dr["q1d"].ap().rearrange("(h p) s -> p h s", h=NH)
    k1_v = dr["k1d"].ap().rearrange("(h p) s -> p h s", h=2)
    v1_v = dr["v1d"].ap().rearrange("(h p) s -> p h s", h=2)

    op("gpsimd", lambda e: e.dma_start(out=WQB.rearrange("p (k o) -> p k o", k=16),
                                       in_=inp["l1_wq"].ap().rearrange("(k p) o -> p k o", p=128)),
       waits=[(s_mm, s_mm.n), (s_ve, s_ve.n)], incs=[(s_wl, 16)])
    op("gpsimd", lambda e: e.dma_start(out=WKB.rearrange("p (k o) -> p k o", k=16),
                                       in_=inp["l1_wk"].ap().rearrange("(k p) o -> p k o", p=128)),
       incs=[(s_wl, 16)])
    op("gpsimd", lambda e: e.dma_start(out=WVB.rearrange("p (k o) -> p k o", k=16),
                                       in_=inp["l1_wv"].ap().rearrange("(k p) o -> p k o", p=128)),
       incs=[(s_wl, 16)])
    w_v = s_wl.n

    ln_acv = 0
    for mac in range(16):
        XF = abf[:, 4096 + (mac % 2) * 8192: 4096 + (mac % 2) * 8192 + 8192]
        for sub in range(4):
            ti = mac * 4 + sub
            xt = XT[ti % 2]
            s_x = [sems["s_xA"], sems["s_xB"]][ti % 2]
            op("sync", lambda e, xt=xt, ti=ti: e.dma_start(out=xt, in_=x1ag_t[ti]),
               waits=([(s_ac, ctx.cache.get(("b1ln", ti - 2), 0))] if ti else
                      [(s_cc, x1ag_cc), (s_ac, s_ac.n), (s_ve, s_ve.n)]),
               incs=[(s_x, 16)])
            lv = s_x.n
            ctx.wait("vector", [(s_x, lv)])
            ctx.wait("scalar", [(s_x, lv)])
            if ti >= 2:
                ctx.wait("scalar", [(s_mm, ctx.cache.get(("b1tr", ti - 2), 0))])
            _, ln_acv = emit_ln(ctx, sems, sb, xt, SQp[ti % 2], stat, XNT[ti % 2], cb=(ti % 2) * 40)
            ctx.cache[("b1ln", ti)] = ln_acv
            acv = s_ac.n
            for kc in range(16):
                op("tensor", lambda e, ti=ti, kc=kc:
                   e.matmul(psB[:, kc * 128:(kc + 1) * 128], XNT[ti % 2][:, kc * 128:(kc + 1) * 128],
                            ident[:, :], start=True, stop=True),
                   waits=([(s_ac, acv), (s_ve, ctx.cache.get(("b1ev", ti - 2), 0))] if kc == 0 else []),
                   incs=([(s_mm, 1)] if kc == 15 else []))
            ctx.cache[("b1tr", ti)] = s_mm.n
            mmv = s_mm.n
            dst = XF.rearrange("p (k t) -> p k t", k=16)[:, :, sub * 128:(sub + 1) * 128]
            op("vector", lambda e, dst=dst:
               e.tensor_copy(dst, psB[:, :].rearrange("p (k t) -> p k t", k=16)),
               waits=[(s_mm, mmv)], incs=[(s_ve, 1)])
            ctx.cache[("b1ev", ti)] = s_ve.n
        xf_ready = s_ve.n

        for sub in range(4):
            ti = mac * 4 + sub
            bank = (sub % 2) * 512
            for kc in range(16):
                op("tensor", lambda e, kc=kc, sub=sub, bank=bank, XF=XF:
                   e.matmul(psA[:, bank:bank + 512], XF[:, kc * 512 + sub * 128: kc * 512 + sub * 128 + 128],
                            WQB[:, kc * 512:(kc + 1) * 512], start=(kc == 0), stop=(kc == 15)),
                   waits=([(s_wl, w_v), (s_ve, xf_ready),
                           (s_ac, ctx.cache.get(("qrm", ti - 2), 0))] if kc == 0 else []),
                   incs=([(s_mm, 1)] if kc == 15 else []))
            qmm = s_mm.n
            op("scalar", lambda e, bank=bank: e.activation(SQ[:, 0:512], psA[:, bank:bank + 512],
                                                           AF.Square),
               waits=[(s_mm, qmm), (s_ve, ctx.cache.get(("qsq", ti - 1), 0))], incs=[(s_ac, 1)])
            sqv = s_ac.n
            op("vector", lambda e: e.reduce_sum(RMS[:, 0:4],
                                                SQ[:, 0:512].rearrange("p (h j) -> p h j", h=4), axis=AX),
               waits=[(s_ac, sqv)], incs=[(s_ve, 1)])
            ctx.cache[("qsq", ti)] = s_ve.n
            rsv = s_ve.n
            op("scalar", lambda e: e.activation(RMS[:, 4:8], RMS[:, 0:4], AF.Sqrt,
                                                scale=1.0 / HD, bias=1e-6),
               waits=[(s_ve, rsv)], incs=[(s_ac, 1)])
            acv = s_ac.n
            op("vector", lambda e: e.reciprocal(RMS[:, 8:12], RMS[:, 4:8]),
               waits=[(s_ac, acv)], incs=[(s_ve, 1)])
            rzv = s_ve.n
            qn = abf[:, 36864 + (sub % 2) * 512: 36864 + (sub % 2) * 512 + 512]
            for hh in range(4):
                op("vector", lambda e, hh=hh, qn=qn, bank=bank:
                   e.tensor_scalar_mul(qn[:, hh * 128:(hh + 1) * 128],
                                       psA[:, bank + hh * 128: bank + hh * 128 + 128],
                                       RMS[:, 8 + hh:9 + hh]),
                   waits=([(s_ve, rzv), (s_mm, ctx.cache.get(("qtr1", ti - 2), 0))] if hh == 0 else []),
                   incs=([(s_ve, 1)] if hh == 3 else []))
            ctx.cache[("qrm", ti)] = s_ac.n
            qnv = s_ve.n
            for hh in range(4):
                op("tensor", lambda e, hh=hh, qn=qn:
                   e.matmul(psB[:, 1024 + hh * 128: 1024 + hh * 128 + 128],
                            qn[:, hh * 128:(hh + 1) * 128], ident[:, :], start=True, stop=True),
                   waits=([(s_ve, qnv), (s_ve, ctx.cache.get(("qfe", ti - 1), 0))] if hh == 0 else []),
                   incs=([(s_mm, 1)] if hh == 3 else []))
            ctx.cache[("qtr1", ti)] = s_mm.n
            mmv = s_mm.n
            qf = abf[:, 37888 + (sub % 2) * 512: 37888 + (sub % 2) * 512 + 512]
            op("vector", lambda e, qf=qf: e.tensor_scalar_mul(qf, psB[:, 1024:1536], sb["qnw"][:, 0:1]),
               waits=[(s_mm, mmv), (s_sA, s_sA.n)], incs=[(s_ve, 1)])
            ctx.cache[("qfe", ti)] = s_ve.n
            qfv = s_ve.n
            op("sync", lambda e, qf=qf, ti=ti:
               e.dma_start(out=q1_v[:, :, ti * 128:(ti + 1) * 128],
                           in_=qf.rearrange("p (h t) -> p h t", h=4)),
               waits=[(s_ve, qfv)], incs=[(s_sA, 16)])

            for kc in range(16):
                op("tensor", lambda e, kc=kc, sub=sub, XF=XF:
                   e.matmul(psA[:, 1024:1280], XF[:, kc * 512 + sub * 128: kc * 512 + sub * 128 + 128],
                            WKB[:, kc * 256:(kc + 1) * 256], start=(kc == 0), stop=(kc == 15)),
                   waits=([(s_ve, ctx.cache.get(("krm", ti - 1), 0))] if kc == 0 else []),
                   incs=([(s_mm, 1)] if kc == 15 else []))
            kmm = s_mm.n
            op("scalar", lambda e: e.activation(SQ[:, 512:768], psA[:, 1024:1280], AF.Square),
               waits=[(s_mm, kmm), (s_ve, ctx.cache.get(("ksq", ti - 1), 0))], incs=[(s_ac, 1)])
            sqv = s_ac.n
            op("vector", lambda e: e.reduce_sum(RMS[:, 12:14],
                                                SQ[:, 512:768].rearrange("p (h j) -> p h j", h=2), axis=AX),
               waits=[(s_ac, sqv)], incs=[(s_ve, 1)])
            ctx.cache[("ksq", ti)] = s_ve.n
            rsv = s_ve.n
            op("scalar", lambda e: e.activation(RMS[:, 14:16], RMS[:, 12:14], AF.Sqrt,
                                                scale=1.0 / HD, bias=1e-6),
               waits=[(s_ve, rsv)], incs=[(s_ac, 1)])
            acv = s_ac.n
            op("vector", lambda e: e.reciprocal(RMS[:, 16:18], RMS[:, 14:16]),
               waits=[(s_ac, acv)], incs=[(s_ve, 1)])
            rzv = s_ve.n
            kn = abf[:, 38912 + (sub % 2) * 256: 38912 + (sub % 2) * 256 + 256]
            for hh in range(2):
                op("vector", lambda e, hh=hh, kn=kn:
                   e.tensor_scalar_mul(kn[:, hh * 128:(hh + 1) * 128],
                                       psA[:, 1024 + hh * 128: 1024 + hh * 128 + 128],
                                       RMS[:, 16 + hh:17 + hh]),
                   waits=([(s_ve, rzv), (s_mm, ctx.cache.get(("ktr1", ti - 2), 0))] if hh == 0 else []),
                   incs=([(s_ve, 1)] if hh == 1 else []))
            ctx.cache[("krm", ti)] = s_ve.n
            knv = s_ve.n
            for hh in range(2):
                op("tensor", lambda e, hh=hh, kn=kn:
                   e.matmul(psB[:, 1536 + hh * 128: 1536 + hh * 128 + 128],
                            kn[:, hh * 128:(hh + 1) * 128], ident[:, :], start=True, stop=True),
                   waits=([(s_ve, knv), (s_ve, ctx.cache.get(("kfe", ti - 1), 0))] if hh == 0 else []),
                   incs=([(s_mm, 1)] if hh == 1 else []))
            ctx.cache[("ktr1", ti)] = s_mm.n
            mmv = s_mm.n
            kf = abf[:, 39424 + (sub % 2) * 256: 39424 + (sub % 2) * 256 + 256]
            op("vector", lambda e, kf=kf: e.tensor_scalar_mul(kf, psB[:, 1536:1792], sb["knw"][:, 0:1]),
               waits=[(s_mm, mmv), (s_sB, s_sB.n)], incs=[(s_ve, 1)])
            ctx.cache[("kfe", ti)] = s_ve.n
            kfv = s_ve.n
            op("sync", lambda e, kf=kf, ti=ti:
               e.dma_start(out=k1_v[:, :, ti * 128:(ti + 1) * 128],
                           in_=kf.rearrange("p (h t) -> p h t", h=2)),
               waits=[(s_ve, kfv)], incs=[(s_sB, 16)])

            for kc in range(16):
                op("tensor", lambda e, kc=kc, sub=sub, XF=XF:
                   e.matmul(psA[:, 1536:1792], XF[:, kc * 512 + sub * 128: kc * 512 + sub * 128 + 128],
                            WVB[:, kc * 256:(kc + 1) * 256], start=(kc == 0), stop=(kc == 15)),
                   waits=([(s_ac, ctx.cache.get(("vev", ti - 1), 0))] if kc == 0 else []),
                   incs=([(s_mm, 1)] if kc == 15 else []))
            vmm = s_mm.n
            vst = abf[:, 39936 + (sub % 2) * 256: 39936 + (sub % 2) * 256 + 256]
            op("scalar", lambda e, vst=vst: e.activation(vst, psA[:, 1536:1792], AF.Copy),
               waits=[(s_mm, vmm), (sems["s_sC"], sems["s_sC"].n)], incs=[(s_ac, 1)])
            ctx.cache[("vev", ti)] = s_ac.n
            acv = s_ac.n
            op("sync", lambda e, vst=vst, ti=ti:
               e.dma_start(out=v1_v[:, :, ti * 128:(ti + 1) * 128],
                           in_=vst.rearrange("p (h t) -> p h t", h=2)),
               waits=[(s_ac, acv)], incs=[(sems["s_sC"], 16)])
    a1q_done = s_st.n

    x1_t = dr["x1d"].ap().rearrange("(n p) d -> n p d", p=128)
    xn1_v = dr["xn1d"].ap().rearrange("(k p) t -> p k t", p=128)
    XNF = abf[:, 4096:36864]
    ln_acv2 = 0
    for ti in range(16):
        xt = XT[ti % 2]
        s_x = [sems["s_xA"], sems["s_xB"]][ti % 2]
        op("sync", lambda e, xt=xt, ti=ti: e.dma_start(out=xt, in_=x1_t[ti]),
           waits=([(s_ac, ctx.cache.get(("o1ln", ti - 2), 0))] if ti else
                  [(s_ac, s_ac.n), (s_ve, s_ve.n)]),
           incs=[(s_x, 16)])
        lv = s_x.n
        ctx.wait("vector", [(s_x, lv)])
        ctx.wait("scalar", [(s_x, lv)])
        if ti >= 2:
            ctx.wait("scalar", [(s_mm, ctx.cache.get(("o1tr", ti - 2), 0))])
        _, ln_acv2 = emit_ln(ctx, sems, sb, xt, SQp[ti % 2], stat, XNT[ti % 2], cb=(ti % 2) * 40)
        ctx.cache[("o1ln", ti)] = ln_acv2
        acv = s_ac.n
        for kc in range(16):
            op("tensor", lambda e, ti=ti, kc=kc:
               e.matmul(psB[:, kc * 128:(kc + 1) * 128], XNT[ti % 2][:, kc * 128:(kc + 1) * 128],
                        ident[:, :], start=True, stop=True),
               waits=([(s_ac, acv), (s_ve, ctx.cache.get(("o1ev", ti - 1), 0))] if kc == 0 else []),
               incs=([(s_mm, 1)] if kc == 15 else []))
        ctx.cache[("o1tr", ti)] = s_mm.n
        mmv = s_mm.n
        dst = XNF[:, :].rearrange("p (k t) -> p k t", k=16)[:, :, ti * 128:(ti + 1) * 128]
        op("vector", lambda e, dst=dst:
           e.tensor_copy(dst, psB[:, :].rearrange("p (k t) -> p k t", k=16)),
           waits=[(s_mm, mmv)], incs=[(s_ve, 1)])
        ctx.cache[("o1ev", ti)] = s_ve.n
    xnf_ready = s_ve.n
    op("sync", lambda e: e.dma_start(out=xn1_v[:, :, :],
                                     in_=XNF[:, :].rearrange("p (k t) -> p k t", k=16)),
       waits=[(s_ve, xnf_ready)], incs=[(s_st, 16)])

    emit_attention(ctx, nc, sems, sb, psA, psB, layer=1,
                   kd=dr["k1d"], vd=dr["v1d"], qd=dr["q1d"], attn_out=dr["attn_h1"],
                   proj_done=s_st.n)

    barrier = [(s_st, s_st.n), (s_gst, s_gst.n), (s_sA, s_sA.n), (s_sB, s_sB.n),
               (sems["s_sC"], sems["s_sC"].n)]
    op("gpsimd", lambda e: e.collective_compute(
        "AllGather", ALU.bypass, replica_groups=[[0, 1, 2, 3], [4, 5, 6, 7]],
        ins=[dr["attn_h1"].ap().opt()], outs=[dr["ag1"].ap().opt()]),
       waits=barrier, incs=[(s_cc, 1)])
    ccv = s_cc.n

    if dbg:
        op("sync", lambda e: e.dma_start(out=dbg["dbg_attnh1"][:, :], in_=dr["attn_h1"][:, :]),
           waits=barrier, incs=[(s_st, 16)])

    emit_wo_base(ctx, nc, sems, sb, psA, psB, ag=dr["ag1"], w_in=inp["l1_wo"],
                 bias_row=None, x_res=dr["x1d"], base=dr["base1"], cc_wait=ccv,
                 own_scr=dr["attn_own1"])
    if dbg:
        op("sync", lambda e: e.dma_start(out=dbg["dbg_base1"][:, :], in_=dr["base1"][:, :]),
           waits=[(s_st, s_st.n), (s_sB, s_sB.n), (s_ve, s_ve.n)], incs=[(s_st, 16)])

    emit_ffn1(ctx, nc, sems, sb, psA, psB, dr, inp, out)


def emit_ffn1(ctx, nc, sems, sb, psA, psB, dr, inp, out):
    op = ctx.op
    s_wl, s_al, s_st, s_mm, s_ve, s_ac = (
        sems[k] for k in ("s_wl", "s_al", "s_st", "s_mm", "s_ve", "s_ac"))
    s_gA, s_gB, s_x2, s_sA, s_sB = (sems[k] for k in ("s_gA", "s_gB", "s_x2", "s_sA", "s_sB"))
    s_wA, s_wB = sems["s_wA"], sems["s_wB"]
    abf, af32, stat = sb["abf"], sb["af32"], sb["stat"]
    ident, ones = sb["ident_bf"], sb["ones_col"]
    xn1_v = dr["xn1d"].ap().rearrange("(k p) t -> p k t", p=128)
    base_t = dr["base1"].ap().rearrange("(n p) d -> n p d", p=128)
    out_t = out.ap().rearrange("(n p) d -> n p d", p=128)
    w_fmb = sb["w_fmb"]

    WGS = abf[:, 2048:2176]
    op("gpsimd", lambda e: e.dma_start(out=WGS.rearrange("p (k o) -> p k o", k=16),
                                       in_=inp["l1_wg"].ap().rearrange("(k p) o -> p k o", p=128)),
       waits=[(s_mm, s_mm.n), (s_ve, s_ve.n)], incs=[(s_wl, 16)])
    wg_v = s_wl.n
    G8 = af32[:, 0:8]
    GT = af32[:, 8:16]
    GK = af32[:, 16:24]
    GE = af32[:, 24:32]
    for t in range(16):
        XC = abf[:, 0:2048]
        op("sync", lambda e, t=t: e.dma_start(out=XC.rearrange("p (k t) -> p k t", k=16),
                                              in_=xn1_v[:, :, t * 128:(t + 1) * 128]),
           waits=[(s_mm, ctx.cache.get(("g8mm", t - 1), 0))], incs=[(s_al, 16)])
        lv = s_al.n
        for kc in range(16):
            op("tensor", lambda e, kc=kc:
               e.matmul(psA[:, 0:8], XC[:, kc * 128:(kc + 1) * 128], WGS[:, kc * 8:(kc + 1) * 8],
                        start=(kc == 0), stop=False),
               waits=([(s_al, lv), (s_wl, wg_v), (s_ve, ctx.cache.get(("g8ev", t - 1), 0))]
                      if kc == 0 else []), incs=[])
        op("tensor", lambda e: e.matmul(psA[:, 0:8], ones[:, :], sb["bg_row"][0:1, :],
                                        start=False, stop=True), incs=[(s_mm, 1)])
        ctx.cache[("g8mm", t)] = s_mm.n
        mmv = s_mm.n
        op("vector", lambda e: e.tensor_copy(G8, psA[:, 0:8]), waits=[(s_mm, mmv), (s_ac, s_ac.n)],
           incs=[(s_ve, 1)])
        ctx.cache[("g8ev", t)] = s_ve.n
        op("vector", lambda e: e.tensor_reduce(stat[:, 20:21], G8, AX, ALU.min), incs=[(s_ve, 1)])
        op("vector", lambda e: e.tensor_single_scalar(GT, G8, stat[:, 20:21], ALU.is_equal), incs=[(s_ve, 1)])
        op("vector", lambda e: e.scalar_tensor_tensor(GT, GT, 1e30, G8, ALU.mult, ALU.add), incs=[(s_ve, 1)])
        op("vector", lambda e: e.tensor_reduce(stat[:, 21:22], GT, AX, ALU.min), incs=[(s_ve, 1)])
        op("vector", lambda e: e.tensor_single_scalar(GK, G8, stat[:, 21:22], ALU.is_gt), incs=[(s_ve, 1)])
        op("vector", lambda e: e.reduce_max(stat[:, 22:23], G8, axis=AX, negate=True), incs=[(s_ve, 1)])
        vv = s_ve.n
        op("scalar", lambda e: e.activation(GE, G8, AF.Exp, bias=stat[:, 22:23]),
           waits=[(s_ve, vv)], incs=[(s_ac, 1)])
        acv = s_ac.n
        op("vector", lambda e: e.tensor_mul(GE, GE, GK), waits=[(s_ac, acv)], incs=[(s_ve, 1)])
        op("vector", lambda e: e.reduce_sum(stat[:, 23:24], GE, axis=AX), incs=[(s_ve, 1)])
        op("vector", lambda e: e.reciprocal(stat[:, 24:25], stat[:, 23:24]), incs=[(s_ve, 1)])
        op("vector", lambda e: e.tensor_scalar_mul(GE, GE, stat[:, 24:25]), incs=[(s_ve, 1)])
        GEB = abf[:, 2176:2184]
        op("vector", lambda e: e.tensor_copy(GEB, GE), waits=[(s_mm, s_mm.n)], incs=[(s_ve, 1)])
        gv = s_ve.n
        op("tensor", lambda e: e.matmul(psB[0:8, 0:128], GEB, ident[:, :], start=True, stop=True),
           waits=[(s_ve, gv), (s_ve, ctx.cache.get(("gwe", t - 1), 0))], incs=[(s_mm, 1)])
        mmv = s_mm.n
        op("vector", lambda e, t=t: e.tensor_copy(w_fmb[0:8, t * 128:(t + 1) * 128], psB[0:8, 0:128]),
           waits=[(s_mm, mmv)], incs=[(s_ve, 1)])
        ctx.cache[("gwe", t)] = s_ve.n
    gate_done_ve = s_ve.n

    XNF = abf[:, 0:8192]
    S1B = abf[:, 8192:16384]
    CBF = abf[:, 16384:40960]
    WBg = [abf[:, 40960:43008], abf[:, 43008:45056]]
    WBu = [abf[:, 48128:50176], abf[:, 50176:52224]]
    WBC = abf[:, 45056:48128]
    SIL = af32[:, 8192:8704]
    UPS = af32[:, 8704:9216]
    wgate_in = dr["wgate_b"].ap()
    wup_in = dr["wup_b"].ap()
    wdown_in = dr["wdown_b"].ap()
    wsup_in = dr["wsup_b"].ap().rearrange("(k p) o -> p k o", p=128)
    wsdown_in = dr["wsdown_b"].ap()
    precast_v = sems["s_gst"].n

    for tt in range(4):
        op("sync", lambda e, tt=tt:
           e.dma_start(out=XNF.rearrange("p (k t) -> p k t", k=16), in_=xn1_v[:, :, tt * 512:(tt + 1) * 512]),
           waits=[(s_mm, s_mm.n), (s_ve, s_ve.n)], incs=[(s_al, 16)])
        xv = s_al.n
        for n in range(6):
            op("sync", lambda e, n=n, tt=tt:
               e.dma_start(out=sb["w_row"][0:1, n * 512:(n + 1) * 512],
                           in_=w_fmb[n:n + 1, tt * 512:(tt + 1) * 512]),
               waits=([(s_ve, gate_done_ve), (s_mm, ctx.cache.get(("wrow_mm", tt - 1), 0))]
                      if n == 0 else []),
               incs=[(s_x2, 16)])
        wrow_v = s_x2.n
        for n in range(6):
            op("tensor", lambda e, n=n:
               e.matmul(psB[:, 1024 + (n % 2) * 512: 1024 + (n % 2) * 512 + 512],
                        ones[:, :], sb["w_row"][0:1, n * 512:(n + 1) * 512],
                        start=True, stop=True),
               waits=[(s_x2, wrow_v), (s_ve, s_ve.n)],
               incs=[(s_mm, 1)])
            ctx.cache[("wrow_mm", tt)] = s_mm.n
            mmv = s_mm.n
            op("vector", lambda e, n=n:
               e.tensor_copy(WBC[:, n * 512:(n + 1) * 512], psB[:, 1024 + (n % 2) * 512: 1024 + (n % 2) * 512 + 512]),
               waits=[(s_mm, mmv)], incs=[(s_ve, 1)])
        wbc_v = s_ve.n

        tt_start_mm = s_mm.n
        for hc2 in range(16):
            wb = WBg[hc2 % 2]
            s_w = [s_wA, s_wB][hc2 % 2]
            op("sync", lambda e, wb=wb, hc2=hc2:
               e.dma_start(out=wb.rearrange("p (k h) -> p k h", k=16),
                           in_=wsup_in[:, :, hc2 * 128:(hc2 + 1) * 128]),
               waits=[(s_mm, ctx.cache.get(("sup", tt, hc2 - 2), tt_start_mm)),
                      (sems["s_gst"], precast_v)],
               incs=[(s_w, 16)])
            wv = (s_w, s_w.n)
            bank = (hc2 % 2) * 512
            for kc in range(16):
                op("tensor", lambda e, wb=wb, kc=kc, bank=bank:
                   e.matmul(psA[:, bank:bank + 512], wb[:, kc * 128:(kc + 1) * 128],
                            XNF[:, kc * 512:(kc + 1) * 512], start=(kc == 0), stop=(kc == 15)),
                   waits=([wv, (s_al, xv), (s_ac, ctx.cache.get(("s1e", tt, hc2 - 2), 0))]
                          if kc == 0 else []),
                   incs=([(s_mm, 1)] if kc == 15 else []))
            ctx.cache[("sup", tt, hc2)] = s_mm.n
            mmv = s_mm.n
            op("scalar", lambda e, hc2=hc2, bank=bank:
               e.activation(S1B[:, hc2 * 512:(hc2 + 1) * 512], psA[:, bank:bank + 512], AF.Silu,
                            bias=sb["bsup_sb"][:, hc2:hc2 + 1]),
               waits=[(s_mm, mmv)], incs=[(s_ac, 1)])
            ctx.cache[("s1e", tt, hc2)] = s_ac.n
        s1_v = s_ac.n

        ctx.cache[("exw0", tt)] = s_mm.n
        for n in range(6):
            for hc in range(8):
                ci = n * 8 + hc
                wbg, wbu = WBg[ci % 2], WBu[ci % 2]
                s_w = [s_wA, s_wB][ci % 2]
                op("sync", lambda e, wbg=wbg, n=n, hc=hc:
                   e.dma_start(out=wbg.rearrange("p (k h) -> p k h", k=16),
                               in_=wgate_in[n].rearrange("(k p) h -> p k h", p=128)[:, :, hc * 128:(hc + 1) * 128]),
                   waits=[(s_mm, ctx.cache.get(("exw", tt, ci - 2), ctx.cache.get(("exw0", tt), 0)))] +
                         ([(sems["s_gst"], precast_v)] if ci == 0 else []),
                   incs=[(s_w, 16)])
                op("sync", lambda e, wbu=wbu, n=n, hc=hc:
                   e.dma_start(out=wbu.rearrange("p (k h) -> p k h", k=16),
                               in_=wup_in[n].rearrange("(k p) h -> p k h", p=128)[:, :, hc * 128:(hc + 1) * 128]),
                   incs=[(s_w, 16)])
                wv = (s_w, s_w.n)
                bank = (ci % 2) * 1024
                for kc in range(16):
                    op("tensor", lambda e, wbg=wbg, kc=kc, bank=bank:
                       e.matmul(psA[:, bank:bank + 512], wbg[:, kc * 128:(kc + 1) * 128],
                                XNF[:, kc * 512:(kc + 1) * 512], start=(kc == 0), stop=(kc == 15)),
                       waits=([wv, (s_ve, ctx.cache.get(("cbe", tt, ci - 2), 0))] +
                              ([(s_ac, s_ac.n)] if ci < 2 else [])
                              if kc == 0 else []),
                       incs=([(s_mm, 1)] if kc == 15 else []))
                for kc in range(16):
                    op("tensor", lambda e, wbu=wbu, kc=kc, bank=bank:
                       e.matmul(psA[:, bank + 512:bank + 1024], wbu[:, kc * 128:(kc + 1) * 128],
                                XNF[:, kc * 512:(kc + 1) * 512], start=(kc == 0), stop=(kc == 15)),
                       incs=([(s_mm, 1)] if kc == 15 else []))
                ctx.cache[("exw", tt, ci)] = s_mm.n
                umm = s_mm.n
                op("scalar", lambda e, n=n, hc=hc, bank=bank:
                   e.activation(SIL, psA[:, bank:bank + 512], AF.Silu,
                                bias=sb["bgate_sb"][:, n * 8 + hc:n * 8 + hc + 1]),
                   waits=[(s_mm, umm), (s_ve, ctx.cache.get(("cbe", tt, ci - 1), 0))], incs=[(s_ac, 1)])
                op("scalar", lambda e, n=n, hc=hc, bank=bank:
                   e.activation(UPS, psA[:, bank + 512:bank + 1024], AF.Identity,
                                bias=sb["bup_sb"][:, n * 8 + hc:n * 8 + hc + 1]),
                   incs=[(s_ac, 1)])
                acv = s_ac.n
                op("vector", lambda e: e.tensor_mul(SIL, SIL, UPS), waits=[(s_ac, acv)], incs=[(s_ve, 1)])
                op("vector", lambda e, ci=ci, n=n:
                   e.tensor_tensor(CBF[:, ci * 512:(ci + 1) * 512], SIL, WBC[:, n * 512:(n + 1) * 512], ALU.mult),
                   waits=[(s_ve, wbc_v)], incs=[(s_ve, 1)])
                ctx.cache[("cbe", tt, ci)] = s_ve.n
        c_v = s_ve.n

        XH = af32[:, 0:8192]
        for sub in range(4):
            op("sync", lambda e, sub=sub, tt=tt:
               e.dma_start(out=XH[:, sub * 2048:(sub + 1) * 2048], in_=base_t[tt * 4 + sub]),
               waits=([(s_ve, c_v), (s_sB, s_sB.n)] if sub == 0 else []), incs=[(s_x2, 16)])
        bv = s_x2.n

        srcs = [("e", n, q) for n in range(6) for q in range(2)] + \
               [("s", 0, q) for q in range(4)]
        down_guard = s_mm.n
        for og in range(4):
            si_mm = {}
            for si, (kind, n, q) in enumerate(srcs):
                wb = [WBg[0], WBg[1], WBu[0], WBu[1]][si % 4]
                s_w = [s_wA, s_wB][si % 2]
                if kind == "e":
                    src_ap = wdown_in[n][q * 512:(q + 1) * 512, og * 512:(og + 1) * 512]
                else:
                    src_ap = wsdown_in[q * 512:(q + 1) * 512, og * 512:(og + 1) * 512]
                op("sync", lambda e, wb=wb, src_ap=src_ap:
                   e.dma_start(out=wb.rearrange("p (k o) -> p k o", k=4),
                               in_=src_ap.rearrange("(k p) o -> p k o", p=128)),
                   waits=[(s_mm, si_mm.get(si - 2, down_guard))] +
                         ([(sems["s_gst"], precast_v)] if si == 0 else []),
                   incs=[(s_w, 16)])
                wv = (s_w, s_w.n)
                for k in range(4):
                    lhs0 = (CBF[:, (n * 8 + q * 4 + k) * 512:(n * 8 + q * 4 + k) * 512 + 512]
                            if kind == "e" else S1B[:, (q * 4 + k) * 512:(q * 4 + k) * 512 + 512])
                    for sub in range(4):
                        w_ = []
                        if k == 0 and sub == 0:
                            w_ = [wv]
                            if si == 0:
                                w_ += [(s_ve, c_v), (s_ac, s1_v),
                                       (s_ve, ctx.cache.get(("d1ev", tt, og - 1), 0))]
                        op("tensor", lambda e, wb=wb, lhs=lhs0, sub=sub, si=si, k=k:
                           e.matmul(psB[:, sub * 512:(sub + 1) * 512],
                                    lhs[:, sub * 128:sub * 128 + 128], wb[:, k * 512:(k + 1) * 512],
                                    start=(si == 0 and k == 0), stop=False),
                           waits=w_, incs=([(s_mm, 1)] if (k == 3 and sub == 3) else []))
                si_mm[si] = s_mm.n
            for sub in range(4):
                op("tensor", lambda e, sub=sub, og=og, tt=tt:
                   e.matmul(psB[:, sub * 512:(sub + 1) * 512], w_fmb[0:6, tt * 512 + sub * 128: tt * 512 + sub * 128 + 128],
                            sb["bdown_sb"][0:6, og * 512:(og + 1) * 512], start=False, stop=False),
                   incs=[])
                op("tensor", lambda e, sub=sub, og=og:
                   e.matmul(psB[:, sub * 512:(sub + 1) * 512], ones[:, :],
                            sb["bsdown_row"][0:1, og * 512:(og + 1) * 512], start=False, stop=True),
                   incs=[(s_mm, 1)])
            down_guard = s_mm.n
            dn_v = s_mm.n
            for sub in range(4):
                op("vector", lambda e, sub=sub, og=og:
                   e.tensor_tensor(af32[:, 10240 + sub * 512: 10240 + sub * 512 + 512],
                                   psB[:, sub * 512:(sub + 1) * 512],
                                   XH[:, sub * 2048 + og * 512: sub * 2048 + og * 512 + 512], ALU.add),
                   waits=[(s_mm, dn_v), (s_x2, bv), (s_sA, s_sA.n)],
                   incs=[(s_ve, 1)])
                vv = s_ve.n
                op("sync", lambda e, sub=sub, og=og, tt=tt:
                   e.dma_start(out=out_t[tt * 4 + sub][:, og * 512:(og + 1) * 512],
                               in_=af32[:, 10240 + sub * 512: 10240 + sub * 512 + 512]),
                   waits=[(s_ve, vv)], incs=[(s_sA, 16)])
            ctx.cache[("d1ev", tt, og)] = s_ve.n



_CACHED = {}


def _host_inputs(core, I):
    g, a = core // 4, core % 4
    f32 = np.float32
    x = I["L0_attn_x"]
    d = {}
    d["x_full"] = np.ascontiguousarray(x[g]).astype(f32, copy=False)
    d["x_own"] = np.ascontiguousarray(x[g, a * TOK:(a + 1) * TOK])
    d["l0_wq"] = I["L0_attn_Wq"]
    d["l0_bq"] = I["L0_attn_bq"].reshape(1, D)
    cs = slice(4 * a * 128, 4 * a * 128 + 512)
    d["l0_wk"] = np.ascontiguousarray(I["L0_attn_Wk"][:, cs])
    d["l0_bk"] = np.ascontiguousarray(I["L0_attn_bk"][cs].reshape(4, 128).T)
    d["l0_wv"] = np.ascontiguousarray(I["L0_attn_Wv"][:, cs])
    d["l0_bv"] = I["L0_attn_bv"][cs].reshape(1, 512)
    d["l0_wo"] = I["L0_attn_Wo"]
    d["l0_bo"] = I["L0_attn_bo"].reshape(1, D)
    d["l0_su_ug"] = np.ascontiguousarray(I["L0_ffn_SU_ug"].reshape(16, 128).T)
    d["l0_su_dp"] = np.ascontiguousarray(I["L0_ffn_SU_dp"].reshape(64, 128).T)
    d["l0_ws_ug"] = np.ascontiguousarray(np.broadcast_to(I["L0_ffn_Wscale_ug"].reshape(1, 16384), (128, 16384)))
    d["l0_ws_dp"] = np.ascontiguousarray(np.broadcast_to(I["L0_ffn_Wscale_dp"].reshape(1, D), (128, D)))
    d["l0_wupT"] = np.ascontiguousarray(I["L0_ffn_W_up"].T)
    d["l0_wgateT"] = np.ascontiguousarray(I["L0_ffn_W_gate"].T)
    d["l0_wdownT"] = np.ascontiguousarray(I["L0_ffn_W_down"].T)
    qs = slice(4 * a * 128, 4 * a * 128 + 512)
    kvs = slice(2 * a * 128, 2 * a * 128 + 256)
    d["l1_wq"] = np.ascontiguousarray(I["L1_attn_Wq"].T[:, qs])
    d["l1_wk"] = np.ascontiguousarray(I["L1_attn_Wk"].T[:, kvs])
    d["l1_wv"] = np.ascontiguousarray(I["L1_attn_Wv"].T[:, kvs])
    d["l1_wo"] = np.ascontiguousarray(I["L1_attn_Wo"].T)
    d["l1_qnw"] = I["L1_attn_q_norm_w"].reshape(128, 1)
    d["l1_knw"] = I["L1_attn_k_norm_w"].reshape(128, 1)
    d["l1_wg"] = I["L1_ffn_Wg"]
    d["l1_bg"] = I["L1_ffn_bg"].reshape(1, 8)
    d["l1_wgate"] = I["L1_ffn_Wgate"]
    d["l1_bgate"] = np.ascontiguousarray(I["L1_ffn_bgate"].reshape(6, 8, 128).transpose(2, 0, 1).reshape(128, 48))
    d["l1_wup"] = I["L1_ffn_Wup"]
    d["l1_bup"] = np.ascontiguousarray(I["L1_ffn_bup"].reshape(6, 8, 128).transpose(2, 0, 1).reshape(128, 48))
    d["l1_wdown"] = I["L1_ffn_Wdown"]
    d["l1_bdown"] = I["L1_ffn_bdown"]
    d["l1_wsup"] = I["L1_ffn_Ws_up"]
    d["l1_bsup"] = np.ascontiguousarray(I["L1_ffn_bs_up"].reshape(16, 128).T)
    d["l1_wsdown"] = I["L1_ffn_Ws_down"]
    d["l1_bsdown"] = I["L1_ffn_bs_down"].reshape(1, D)
    d["ident"] = np.eye(128, dtype=f32)
    d["h128"] = _hadamard(128)
    d["h64"] = _hadamard(64)
    m0 = np.zeros((4, 128, 512), dtype=f32)
    for i in range(4):
        off = 128 * i
        col = np.arange(512)[None, :]
        row = np.arange(128)[:, None]
        m0[i] = np.where(col <= row + off, 0.0, NEG)
    d["l0_mask"] = m0
    m1 = np.zeros((16, 128, 512), dtype=f32)
    for qb in range(56, 64):
        for st in (14, 15):
            i = (qb - 56) * 2 + (st - 14)
            tau = qb * 128 + np.arange(128)[:, None]
            s = st * 512 + np.arange(512)[None, :]
            m1[i] = np.where((s >= 7168) & (s <= tau), NEG, 0.0)
    d["l1_mask"] = m1
    for k in d:
        if d[k].dtype != np.float32:
            d[k] = d[k].astype(np.float32)
        d[k] = np.ascontiguousarray(d[k])
    return d


def kernel(**I):
    if "nc" not in _CACHED:
        _CACHED["nc"] = build_nc(debug=False)
    nc = _CACHED["nc"]
    in_maps = [_host_inputs(c, I) for c in range(8)]
    res = run_bass_kernel_spmd(nc, in_maps, core_ids=list(range(8)))
    parts = [res.results[c]["out"] for c in range(8)]
    full = np.concatenate([np.concatenate(parts[0:4], axis=0)[None],
                           np.concatenate(parts[4:8], axis=0)[None]], axis=0)
    return full.astype(np.float32)


# version 08
# speedup vs baseline: 1.1995x; 1.1995x over previous
import numpy as np
import ml_dtypes
import concourse.bass as bass
import concourse.mybir as mybir
from concourse.bass_utils import run_bass_kernel_spmd

F32 = mybir.dt.float32
BF16 = mybir.dt.bfloat16
AF = mybir.ActivationFunctionType
ALU = mybir.AluOpType
AX = mybir.AxisListType.X

D = 2048
S = 8192
TOK = 2048
NH = 4
HD = 128
KC = D // 128
SCL = float(HD) ** -0.5
NEG = -1.0e30

ENGS = ("tensor", "vector", "scalar", "gpsimd", "sync")

BUILD_PHASES = 99


class Sem:
    __slots__ = ("h", "n", "name")

    def __init__(self, h, name):
        self.h = h
        self.n = 0
        self.name = name


class Ctx:

    def __init__(self, who, eng, sems):
        self.who = who
        self.eng = eng
        self.sems = sems
        self.cache = {}
        self.own = {"vector": sems["s_ve"], "scalar": sems["s_ac"]}

    def op(self, engine, fn, waits=(), incs=()):
        waits = list(waits)
        incs = list(incs)
        own = self.own.get(engine)
        if own is not None:
            waits.append((own, own.n))
            if not any(s is own for s, _ in incs):
                incs.append((own, 1))
        if self.who == engine:
            for s, v in waits:
                self.eng.wait_ge(s.h, v)
            inst = fn(self.eng)
            for s, n in incs:
                inst = inst.then_inc(s.h, n)
        for s, n in incs:
            s.n += n

    def wait(self, engine, waits):
        if self.who == engine:
            for s, v in waits:
                self.eng.wait_ge(s.h, v)


def _hadamard(n):
    h = np.array([[1.0]], dtype=np.float32)
    while h.shape[0] < n:
        h = np.block([[h, h], [h, -h]])
    return h



def build_nc(debug=False):
    nc = bass.Bass("TRN2", target_bir_lowering=False)

    EIN = dict(kind="ExternalInput")
    inp = {}

    def din(name, shape, dtype=F32):
        inp[name] = nc.dram_tensor(name, list(shape), dtype, **EIN)
        return inp[name]

    din("x_full", (S, D))
    din("x_own", (TOK, D))
    din("l0_wq", (D, D)); din("l0_bq", (1, D))
    din("l0_wk", (D, 512)); din("l0_bk", (128, 4))
    din("l0_wv", (D, 512)); din("l0_bv", (1, 512))
    din("l0_wo", (D, D)); din("l0_bo", (1, D))
    din("l0_su_ug", (128, 16)); din("l0_su_dp", (128, 64))
    din("l0_ws_ug", (128, 16384)); din("l0_ws_dp", (128, D))
    din("l0_wupT", (D, 8192)); din("l0_wgateT", (D, 8192)); din("l0_wdownT", (8192, D))
    din("l1_wq", (D, 512)); din("l1_wk", (D, 256)); din("l1_wv", (D, 256))
    din("l1_wo", (D, D))
    din("l1_qnw", (128, 1)); din("l1_knw", (128, 1))
    din("l1_wg", (D, 8)); din("l1_bg", (1, 8))
    din("l1_wgate", (6, D, 1024)); din("l1_bgate", (128, 48))
    din("l1_wup", (6, D, 1024)); din("l1_bup", (128, 48))
    din("l1_wdown", (6, 1024, D)); din("l1_bdown", (6, D))
    din("l1_wsup", (D, D)); din("l1_bsup", (128, 16))
    din("l1_wsdown", (D, D)); din("l1_bsdown", (1, D))
    din("ident", (128, 128)); din("h128", (128, 128)); din("h64", (64, 64))
    din("l0_mask", (4, 128, 512)); din("l1_mask", (8, 128, 512))

    out = nc.dram_tensor("out", [TOK, D], F32, kind="ExternalOutput")

    dr = {}

    def dint(name, shape, dtype):
        dr[name] = nc.dram_tensor(name, list(shape), dtype)
        return dr[name]

    dint("wupf", (D, 8192), BF16)
    dint("wgatef", (D, 8192), BF16)
    dint("wdtmp", (8192, D), BF16)
    dint("wdownf", (8192, D), BF16)
    dint("q0d", (TOK, D), BF16)
    dint("xn0d", (D, TOK), BF16)
    dint("k0d", (NH * 128, S), BF16)
    dint("v0d", (NH * 128, S), BF16)
    dint("attn_h0a", (S, 256), BF16)
    dint("attn_h0b", (S, 256), BF16)
    dint("ag0a", (4 * S, 256), BF16)
    dint("ag0b", (4 * S, 256), BF16)
    dint("attn_own0", (TOK, D), BF16)
    dint("attn_own1", (TOK, D), BF16)
    dint("base0", (TOK, D), F32)
    dint("x1d", (TOK, D), F32)
    dint("x1ag", (4 * TOK, D), F32)
    dint("xn1d", (D, TOK), BF16)
    dint("q1d", (NH * 128, S), BF16)
    dint("k1d", (2 * 128, S), BF16)
    dint("v1d", (2 * 128, S), BF16)
    dint("attn_h1a", (S, 256), BF16)
    dint("attn_h1b", (S, 256), BF16)
    dint("ag1a", (4 * S, 256), BF16)
    dint("ag1b", (4 * S, 256), BF16)
    dint("base1", (TOK, D), F32)
    dint("wgate_b", (6, D, 1024), BF16)
    dint("wup_b", (6, D, 1024), BF16)
    dint("wdown_b", (6, 1024, D), BF16)
    dint("wsup_b", (D, D), BF16)
    dint("wsdown_b", (D, D), BF16)
    dint("zscr", (2, 512), F32)

    dbg = {}
    if debug:
        for name, shape, dt in [
            ("dbg_wupf", (D, 8192), BF16), ("dbg_wdownf", (8192, D), BF16),
            ("dbg_q0", (TOK, D), BF16), ("dbg_k0", (NH * 128, S), BF16),
            ("dbg_v0", (NH * 128, S), BF16),
            ("dbg_base0", (TOK, D), F32),
            ("dbg_x1", (TOK, D), F32),
            ("dbg_base1", (TOK, D), F32),
        ]:
            dbg[name] = nc.dram_tensor(name, list(shape), dt, kind="ExternalOutput")

    import contextlib
    with contextlib.ExitStack() as stk:
        ent = stk.enter_context

        sb = {}
        def sbuf(name, shape, dtype):
            sb[name] = ent(nc.sbuf_tensor(name, list(shape), dtype))
            return sb[name]

        sbuf("ident_bf", (128, 128), BF16)
        sbuf("h128_bf", (128, 128), BF16)
        sbuf("h64_bf", (64, 64), BF16)
        sbuf("ones_col", (1, 128), BF16)
        sbuf("ones128", (128, 1), BF16)
        sbuf("su_ug", (128, 16), F32)
        sbuf("su_dp", (128, 64), F32)
        sbuf("bk_sb", (128, 4), F32)
        sbuf("bq_row", (1, D), BF16)
        sbuf("bv_row", (1, 512), BF16)
        sbuf("bo_row", (1, D), BF16)
        sbuf("qnw", (128, 1), F32)
        sbuf("knw", (128, 1), F32)
        sbuf("bg_row", (1, 8), BF16)
        sbuf("bgate_sb", (128, 48), F32)
        sbuf("bup_sb", (128, 48), F32)
        sbuf("bdown_sb", (6, D), BF16)
        sbuf("bsup_sb", (128, 16), F32)
        sbuf("bsdown_row", (1, D), BF16)
        sbuf("masks0", (128, 4 * 512), BF16)
        sbuf("masks1", (128, 8 * 512), BF16)

        sbuf("cvals", (128, 4), F32)
        nc.const_aps.register(F32, 1e-5, sb["cvals"][:, 0:1])
        nc.const_aps.register(F32, 1e-6, sb["cvals"][:, 1:2])
        nc.const_aps.register(F32, SCL, sb["cvals"][:, 2:3])
        nc.const_aps.register(F32, 1.0 / HD, sb["cvals"][:, 3:4])

        sbuf("abf", (128, 52224), BF16)
        sbuf("af32", (128, 12288), F32)
        sbuf("stat", (128, 256), F32)
        sbuf("w_fmb", (8, TOK), BF16)
        sbuf("w_row", (1, 6 * 512), BF16)

        psA = ent(nc.psum_tensor("psA", [128, 2048], F32))
        psB = ent(nc.psum_tensor("psB", [128, 2048], F32))

        sems = {}
        for name in ("s_wl", "s_al", "s_st", "s_gst", "s_mm", "s_ve", "s_ac", "s_cc",
                     "s_kv", "s_qA", "s_qB", "s_x2", "s_wA", "s_wB", "s_gA", "s_gB",
                     "s_gb", "s_sA", "s_sB", "s_sC", "s_xA", "s_xB"):
            sems[name] = Sem(ent(nc.semaphore(name)), name)

        block = ent(nc.Block())

        def prog(ctx):
            emit_program(ctx, nc, inp, out, dr, sb, psA, psB, sems, dbg)

        block.tensor(lambda e: prog(Ctx("tensor", e, sems)))
        block.vector(lambda e: prog(Ctx("vector", e, sems)))
        block.scalar(lambda e: prog(Ctx("scalar", e, sems)))
        block.gpsimd(lambda e: prog(Ctx("gpsimd", e, sems)))
        block.sync(lambda e: prog(Ctx("sync", e, sems)))

    return nc



def emit_program(ctx, nc, inp, out, dr, sb, psA, psB, sems, dbg):
    for s in sems.values():
        s.n = 0
    s_wl = sems["s_wl"]
    s_al = sems["s_al"]
    s_st = sems["s_st"]
    s_mm = sems["s_mm"]
    s_ve = sems["s_ve"]
    s_ac = sems["s_ac"]
    s_cc = sems["s_cc"]

    op = ctx.op

    def ld(engine, sem, dst, src):
        op(engine, lambda e, d=dst, s=src: e.dma_start(out=d, in_=s), incs=[(sem, 16)])

    G = "gpsimd"; SY = "sync"
    ld(G, s_wl, sb["ident_bf"][:, :], inp["ident"][:, :])
    ld(G, s_wl, sb["h128_bf"][:, :], inp["h128"][:, :])
    ld(G, s_wl, sb["h64_bf"][:, :], inp["h64"][:, :])
    ld(G, s_wl, sb["su_ug"][:, :], inp["l0_su_ug"][:, :])
    ld(G, s_wl, sb["su_dp"][:, :], inp["l0_su_dp"][:, :])
    ld(G, s_wl, sb["bk_sb"][:, :], inp["l0_bk"][:, :])
    ld(G, s_wl, sb["bq_row"][:, :], inp["l0_bq"][:, :])
    ld(G, s_wl, sb["bv_row"][:, :], inp["l0_bv"][:, :])
    ld(G, s_wl, sb["bo_row"][:, :], inp["l0_bo"][:, :])
    ld(G, s_wl, sb["qnw"][:, :], inp["l1_qnw"][:, :])
    ld(G, s_wl, sb["knw"][:, :], inp["l1_knw"][:, :])
    ld(G, s_wl, sb["bg_row"][:, :], inp["l1_bg"][:, :])
    ld(G, s_wl, sb["bgate_sb"][:, :], inp["l1_bgate"][:, :])
    ld(G, s_wl, sb["bup_sb"][:, :], inp["l1_bup"][:, :])
    ld(G, s_wl, sb["bdown_sb"][:, :], inp["l1_bdown"][:, :])
    ld(G, s_wl, sb["bsup_sb"][:, :], inp["l1_bsup"][:, :])
    ld(G, s_wl, sb["bsdown_row"][:, :], inp["l1_bsdown"][:, :])
    m0 = inp["l0_mask"].ap().rearrange("n p f -> p n f")
    ld(G, s_wl, sb["masks0"][:, :].rearrange("p (n f) -> p n f", n=4), m0)
    m1 = inp["l1_mask"].ap().rearrange("n p f -> p n f")
    ld(G, s_wl, sb["masks1"][:, :].rearrange("p (n f) -> p n f", n=8), m1)
    consts_done = s_wl.n
    op("vector", lambda e: e.memset(sb["ones_col"][:, :], 1.0), incs=[(s_ve, 1)])
    op("vector", lambda e: e.memset(sb["ones128"][:, :], 1.0), incs=[(s_ve, 1)])
    for i, v in enumerate([1e-5, 1e-6, SCL, 1.0 / HD]):
        op("vector", lambda e, i=i, v=v: e.memset(sb["cvals"][:, i:i + 1], v), incs=[(s_ve, 1)])

    if BUILD_PHASES >= 1:
        emit_phaseW(ctx, nc, inp, dr, sb, psA, psB, sems, consts_done, dbg)
    if BUILD_PHASES >= 2:
        emit_layer0(ctx, nc, inp, out, dr, sb, psA, psB, sems, dbg)
    if BUILD_PHASES >= 3:
        emit_layer1(ctx, nc, inp, out, dr, sb, psA, psB, sems, dbg)

    ctx.wait(SY, [(sems["s_st"], sems["s_st"].n), (sems["s_sA"], sems["s_sA"].n),
                  (sems["s_sB"], sems["s_sB"].n), (sems["s_sC"], sems["s_sC"].n)])
    ctx.wait(G, [(sems["s_gst"], sems["s_gst"].n), (sems["s_gb"], sems["s_gb"].n)])



def emit_phaseW(ctx, nc, inp, dr, sb, psA, psB, sems, consts_done, dbg):
    op = ctx.op
    s_wl, s_al, s_st, s_gst, s_mm, s_ve = (sems[k] for k in ("s_wl", "s_al", "s_st", "s_gst", "s_mm", "s_ve"))
    s_wA, s_wB, s_gA, s_gB = (sems[k] for k in ("s_wA", "s_wB", "s_gA", "s_gB"))
    s_sA, s_sB, s_ac = sems["s_sA"], sems["s_sB"], sems["s_ac"]
    abf, af32 = sb["abf"], sb["af32"]
    h128, h64 = sb["h128_bf"], sb["h64_bf"]
    ws_bc = af32[:, 0:8192]
    wsd_bc = af32[:, 8192:10240]

    op("sync", lambda e: e.dma_start(out=wsd_bc, in_=inp["l0_ws_dp"][:, :]), incs=[(s_al, 16)])
    op("vector", lambda e: e.tensor_scalar_mul(wsd_bc, wsd_bc, 2.0 ** -6.5),
       waits=[(s_al, s_al.n)], incs=[(s_ve, 1)])
    wsd_ready = s_ve.n

    for wname, dst in (("l0_wupT", "wupf"), ("l0_wgateT", "wgatef")):
        wsrc = inp[wname].ap().rearrange("(r c) h -> c r h", c=128)
        wdst = dr[dst].ap().rearrange("(r c) h -> c r h", c=128)
        base_off = 0 if wname == "l0_wupT" else 8192
        op("sync", lambda e, bo=base_off:
           e.dma_start(out=ws_bc, in_=inp["l0_ws_ug"][:, bo:bo + 8192]),
           waits=[(s_ve, s_ve.n)], incs=[(s_al, 16)])
        op("vector", lambda e: e.tensor_scalar_mul(ws_bc, ws_bc, 2.0 ** -5.5),
           waits=[(s_al, s_al.n)], incs=[(s_ve, 1)])
        ws_ready = s_ve.n
        for hc4 in range(16):
            h0 = hc4 * 512
            lbuf = abf[:, (hc4 % 2) * 8192:(hc4 % 2) * 8192 + 16 * 512]
            s_g = [s_gA, s_gB][hc4 % 2]
            op("gpsimd", lambda e, lb=lbuf, w=wsrc, h0=h0:
               e.dma_start(out=lb.rearrange("p (r h) -> p r h", r=16), in_=w[:, :, h0:h0 + 512]),
               waits=[(s_mm, ctx.cache.get(("wld1", (hc4 - 2) % 16), 0))], incs=[(s_g, 16)])
            lv = (s_g, s_g.n)
            for r in range(16):
                bank = (r % 2) * 512
                op("tensor", lambda e, lb=lbuf, r=r, bank=bank:
                   e.matmul(psA[:, bank:bank + 512], h128[:, :], lb[:, r * 512:(r + 1) * 512],
                            start=True, stop=True),
                   waits=[lv, (s_wl, consts_done),
                          (s_ve, ctx.cache.get(("wfe", (r - 2) % 16), 0))], incs=[(s_mm, 1)])
                mmv = s_mm.n
                ob = abf[:, 16384 + (r % 2) * 512: 16384 + (r % 2) * 512 + 512]
                s_s = [s_sA, s_sB][r % 2]
                op("vector", lambda e, ob=ob, bank=bank, h0=h0:
                   e.tensor_tensor(ob, psA[:, bank:bank + 512], ws_bc[:, h0:h0 + 512], ALU.mult),
                   waits=[(s_mm, mmv), (s_ve, ws_ready),
                          (s_s, ctx.cache.get(("wst", (r - 2) % 16), 0))], incs=[(s_ve, 1)])
                ctx.cache[("wfe", r)] = s_ve.n
                vv = s_ve.n
                op("sync", lambda e, ob=ob, w=wdst, r=r, h0=h0:
                   e.dma_start(out=w[:, r, h0:h0 + 512], in_=ob),
                   waits=[(s_ve, vv)], incs=[(s_s, 16)])
                ctx.cache[("wst", r)] = s_s.n
            ctx.cache[("wld1", hc4)] = s_mm.n

    ctx.cache[("wfe2", 30)] = (s_ve, s_ve.n)
    ctx.cache[("wfe2", 31)] = (s_ve, s_ve.n)
    ctx.cache[("wld2x",)] = s_mm.n
    wsrc = inp["l0_wdownT"].ap().rearrange("(r c) o -> c r o", c=128)
    wdst = dr["wdtmp"].ap().rearrange("(r c) o -> c r o", c=128)
    for oc in range(4):
        o0 = oc * 512
        for hh in range(2):
            lbuf = abf[:, ((oc * 2 + hh) % 2) * 16384:((oc * 2 + hh) % 2) * 16384 + 32 * 512]
            s_g = [s_gA, s_gB][(oc * 2 + hh) % 2]
            it = oc * 2 + hh
            op("gpsimd", lambda e, lb=lbuf, o0=o0, hh=hh:
               e.dma_start(out=lb.rearrange("p (r o) -> p r o", r=32),
                           in_=wsrc[:, hh * 32:(hh + 1) * 32, o0:o0 + 512]),
               waits=[(s_mm, ctx.cache.get(("wld2", it - 2), ctx.cache.get(("wld2x",), 0)))] +
                     ([(s_sA, s_sA.n), (s_sB, s_sB.n)] if it < 2 else []),
               incs=[(s_g, 16)])
            lv = (s_g, s_g.n)
            for rr in range(32):
                r = hh * 32 + rr
                bank = (rr % 2) * 512
                op("tensor", lambda e, lb=lbuf, rr=rr, bank=bank:
                   e.matmul(psA[:, bank:bank + 512], h128[:, :], lb[:, rr * 512:(rr + 1) * 512],
                            start=True, stop=True),
                   waits=[lv, (s_wl, consts_done),
                          ctx.cache.get(("wfe2", (rr - 2) % 32), (s_ve, 0))], incs=[(s_mm, 1)])
                mmv = s_mm.n
                ob = abf[:, 33280 + (rr % 2) * 512: 33280 + (rr % 2) * 512 + 512]
                s_s = [s_sA, s_sB][rr % 2]
                op("scalar", lambda e, ob=ob, bank=bank: e.activation(ob, psA[:, bank:bank + 512], AF.Copy),
                   waits=[(s_mm, mmv), (s_s, ctx.cache.get(("wst2", (rr - 2) % 32), 0))],
                   incs=[(s_ac, 1)])
                ctx.cache[("wfe2", rr)] = (s_ac, s_ac.n)
                vv = (s_ac, s_ac.n)
                op("sync", lambda e, ob=ob, r=r, o0=o0:
                   e.dma_start(out=wdst[:, r, o0:o0 + 512], in_=ob),
                   waits=[vv], incs=[(s_s, 16)])
                ctx.cache[("wst2", rr)] = s_s.n
            ctx.cache[("wld2", it)] = s_mm.n
    f1_done_sA, f1_done_sB = s_sA.n, s_sB.n

    w2 = dr["wdtmp"].ap().rearrange("(r c) o -> r c o", c=128)
    w2dst = dr["wdownf"].ap().rearrange("(r c) o -> c r o", c=128)
    for c in range(128):
        lbuf = abf[0:64, (c % 2) * 2048:(c % 2) * 2048 + 2048]
        s_w = [s_wA, s_wB][c % 2]
        op("sync", lambda e, lb=lbuf, c=c: e.dma_start(out=lb, in_=w2[:, c, :]),
           waits=[(s_sA, f1_done_sA), (s_sB, f1_done_sB),
                  (s_mm, ctx.cache.get(("wld3", c - 2), 0))], incs=[(s_w, 16)])
        lv = (s_w, s_w.n)
        for oc in range(4):
            bank = 1024 + (oc % 2) * 512
            op("tensor", lambda e, lb=lbuf, oc=oc, bank=bank:
               e.matmul(psA[0:64, bank:bank + 512], h64[:, :], lb[:, oc * 512:(oc + 1) * 512],
                        start=True, stop=True),
               waits=[lv, (s_wl, consts_done),
                      (s_ve, ctx.cache.get(("wfe3", c, oc - 2),
                                           ctx.cache.get(("wfe3", c - 1, oc + 2), 0)))],
               incs=[(s_mm, 1)])
            mmv = s_mm.n
            ob = abf[0:64, 4096 + (oc % 2) * 512: 4096 + (oc % 2) * 512 + 512]
            s_s2 = [s_gst, sems["s_gb"]][oc % 2]
            op("vector", lambda e, ob=ob, bank=bank, oc=oc:
               e.tensor_tensor(ob, psA[0:64, bank:bank + 512], wsd_bc[0:64, oc * 512:(oc + 1) * 512], ALU.mult),
               waits=[(s_mm, mmv), (s_ve, wsd_ready),
                      (s_s2, ctx.cache.get(("wst3", c, oc - 2),
                                           ctx.cache.get(("wst3", c - 1, oc + 2), 0)))],
               incs=[(s_ve, 1)])
            ctx.cache[("wfe3", c, oc)] = s_ve.n
            vv = s_ve.n
            op("gpsimd", lambda e, ob=ob, c=c, oc=oc:
               e.dma_start(out=w2dst[c, :, oc * 512:(oc + 1) * 512], in_=ob),
               waits=[(s_ve, vv)], incs=[(s_s2, 16)])
            ctx.cache[("wst3", c, oc)] = s_s2.n
        ctx.cache[("wld3", c)] = s_mm.n

    for nm, dst in (("l1_wgate", "wgate_b"), ("l1_wup", "wup_b"), ("l1_wdown", "wdown_b"),
                    ("l1_wsup", "wsup_b"), ("l1_wsdown", "wsdown_b")):
        srcap = inp[nm].ap()
        dstap = dr[dst].ap()
        if len(srcap.shape) == 3:
            for n in range(srcap.shape[0]):
                op("gpsimd", lambda e, srcap=srcap, dstap=dstap, n=n:
                   e.dma_start(out=dstap[n], in_=srcap[n]), incs=[(s_gst, 16)])
        else:
            op("gpsimd", lambda e, srcap=srcap, dstap=dstap:
               e.dma_start(out=dstap[:, :], in_=srcap[:, :]), incs=[(s_gst, 16)])
    if dbg:
        wbar = [(s_st, s_st.n), (s_gst, s_gst.n), (sems["s_gb"], sems["s_gb"].n),
                (s_sA, s_sA.n), (s_sB, s_sB.n)]
        op("sync", lambda e: e.dma_start(out=dbg["dbg_wupf"][:, :], in_=dr["wupf"][:, :]),
           waits=wbar, incs=[(s_st, 16)])
        op("sync", lambda e: e.dma_start(out=dbg["dbg_wdownf"][:, :], in_=dr["wdownf"][:, :]),
           waits=wbar, incs=[(s_st, 16)])


def emit_ln(ctx, sems, sb, xf32, sq_scr, stat, xn_bf, cb=0):
    op = ctx.op
    s_ve, s_ac = sems["s_ve"], sems["s_ac"]
    op("vector", lambda e: e.reduce_sum(stat[:, cb:cb + 1], xf32, axis=AX), incs=[(s_ve, 1)])
    op("scalar", lambda e: e.activation(sq_scr, xf32, AF.Square, accum_out=stat[:, cb + 1:cb + 2]),
       incs=[(s_ac, 1)])
    sqv = s_ac.n
    op("vector", lambda e: e.tensor_scalar_mul(stat[:, cb + 2:cb + 3], stat[:, cb:cb + 1], 1.0 / D),
       incs=[(s_ve, 1)])
    op("vector", lambda e: e.tensor_mul(stat[:, cb + 3:cb + 4], stat[:, cb + 2:cb + 3], stat[:, cb + 2:cb + 3]),
       incs=[(s_ve, 1)])
    op("vector", lambda e: e.scalar_tensor_tensor(stat[:, cb + 4:cb + 5], stat[:, cb + 1:cb + 2], 1.0 / D,
                                                  stat[:, cb + 3:cb + 4], ALU.mult, ALU.subtract),
       waits=[(s_ac, sqv)], incs=[(s_ve, 1)])
    vv = s_ve.n
    op("scalar", lambda e: e.activation(stat[:, cb + 5:cb + 6], stat[:, cb + 4:cb + 5], AF.Ln, bias=1e-5),
       waits=[(s_ve, vv)], incs=[(s_ac, 1)])
    op("scalar", lambda e: e.activation(stat[:, cb + 6:cb + 7], stat[:, cb + 5:cb + 6], AF.Exp, scale=-0.5),
       incs=[(s_ac, 1)])
    sqv2 = s_ac.n
    op("vector", lambda e: e.tensor_mul(stat[:, cb + 7:cb + 8], stat[:, cb + 2:cb + 3], stat[:, cb + 6:cb + 7]),
       waits=[(s_ac, sqv2)], incs=[(s_ve, 1)])
    op("vector", lambda e: e.tensor_scalar_mul(stat[:, cb + 8:cb + 9], stat[:, cb + 7:cb + 8], -1.0),
       incs=[(s_ve, 1)])
    vv = s_ve.n
    op("scalar", lambda e: e.activation(xn_bf, xf32, AF.Identity, scale=stat[:, cb + 6:cb + 7],
                                        bias=stat[:, cb + 8:cb + 9]),
       waits=[(s_ve, vv)], incs=[(s_ac, 1)])
    return s_ve.n, s_ac.n


def emit_layer0(ctx, nc, inp, out, dr, sb, psA, psB, sems, dbg):
    op = ctx.op
    s_wl, s_al, s_st, s_gst, s_mm, s_ve, s_ac, s_cc = (
        sems[k] for k in ("s_wl", "s_al", "s_st", "s_gst", "s_mm", "s_ve", "s_ac", "s_cc"))
    s_sA, s_sB = sems["s_sA"], sems["s_sB"]
    abf, af32, stat = sb["abf"], sb["af32"], sb["stat"]
    ident, ones = sb["ident_bf"], sb["ones_col"]

    wq_src = inp["l0_wq"].ap().rearrange("(k p) o -> p k o", p=128)
    x_own_t = inp["x_own"].ap().rearrange("(n p) d -> n p d", p=128)
    x_full_t = inp["x_full"].ap().rearrange("(n p) d -> n p d", p=128)
    xn0_v = dr["xn0d"].ap().rearrange("(k p) t -> p k t", p=128)
    q0_v = dr["q0d"].ap()

    phase_barrier = [(s_st, s_st.n), (s_gst, s_gst.n), (s_ve, s_ve.n), (s_mm, s_mm.n)]
    XT = [af32[:, 0:2048], af32[:, 2048:4096]]
    SQp = [af32[:, 4096:6144], af32[:, 6144:8192]]
    XNT = [abf[:, 0:2048], abf[:, 2048:4096]]
    XNF = abf[:, 4096:36864]

    def xnf_tile(kc, t0, w=128):
        return XNF[:, kc * 2048 + t0: kc * 2048 + t0 + w]

    ln_acv = 0
    for ti in range(16):
        xt = XT[ti % 2]
        s_x = [sems["s_xA"], sems["s_xB"]][ti % 2]
        op("sync", lambda e, xt=xt, ti=ti: e.dma_start(out=xt, in_=x_own_t[ti]),
           waits=phase_barrier if ti == 0 else [(s_ac, ctx.cache.get(("a0ln", ti - 2), 0))],
           incs=[(s_x, 16)])
        lv = s_x.n
        ctx.wait("vector", [(s_x, lv)])
        ctx.wait("scalar", [(s_x, lv)])
        if ti >= 2:
            ctx.wait("scalar", [(s_mm, ctx.cache.get(("a0tr", ti - 2), 0))])
        _, ln_acv = emit_ln(ctx, sems, sb, xt, SQp[ti % 2], stat, XNT[ti % 2], cb=(ti % 2) * 40)
        ctx.cache[("a0ln", ti)] = ln_acv
        acv = s_ac.n
        for kc in range(16):
            op("tensor", lambda e, ti=ti, kc=kc:
               e.matmul(psB[:, kc * 128:(kc + 1) * 128], XNT[ti % 2][:, kc * 128:(kc + 1) * 128],
                        ident[:, :], start=True, stop=True),
               waits=([(s_ac, acv), (s_ve, ctx.cache.get(("a0ev", ti - 1), 0))] if kc == 0 else []),
               incs=([(s_mm, 1)] if kc == 15 else []))
        ctx.cache[("a0tr", ti)] = s_mm.n
        mmv = s_mm.n
        dst = XNF[:, :].rearrange("p (k t) -> p k t", k=16)[:, :, ti * 128:(ti + 1) * 128]
        op("vector", lambda e, dst=dst:
           e.tensor_copy(dst, psB[:, :].rearrange("p (k t) -> p k t", k=16)),
           waits=[(s_mm, mmv)], incs=[(s_ve, 1)])
        ctx.cache[("a0ev", ti)] = s_ve.n
    xnf_ready = s_ve.n
    op("sync", lambda e: e.dma_start(out=xn0_v[:, :, :],
                                     in_=XNF[:, :].rearrange("p (k t) -> p k t", k=16)),
       waits=[(s_ve, xnf_ready)], incs=[(s_st, 16)])

    WQB = abf[:, 36864:45056]
    qev = 0
    for og in range(4):
        op("gpsimd", lambda e, og=og:
           e.dma_start(out=WQB.rearrange("p (k o) -> p k o", k=16),
                       in_=wq_src[:, :, og * 512:(og + 1) * 512]),
           waits=[(s_mm, s_mm.n)], incs=[(s_wl, 16)])
        wv = s_wl.n
        for ti in range(16):
            bank = (ti % 2) * 512
            for kc in range(16):
                op("tensor", lambda e, ti=ti, kc=kc, bank=bank:
                   e.matmul(psA[:, bank:bank + 512], xnf_tile(kc, ti * 128),
                            WQB[:, kc * 512:(kc + 1) * 512],
                            start=(kc == 0), stop=False),
                   waits=([(s_wl, wv), (s_ve, xnf_ready), (s_ac, qev)] if kc == 0 else []),
                   incs=[])
            op("tensor", lambda e, og=og, bank=bank:
               e.matmul(psA[:, bank:bank + 512], ones[:, :],
                        sb["bq_row"][0:1, og * 512:(og + 1) * 512], start=False, stop=True),
               incs=[(s_mm, 1)])
            mmv = s_mm.n
            qst = abf[:, 45056 + (ti % 2) * 512: 45056 + (ti % 2) * 512 + 512]
            op("scalar", lambda e, qst=qst, bank=bank: e.activation(qst, psA[:, bank:bank + 512], AF.Copy),
               waits=[(s_mm, mmv), (s_sA, s_sA.n)], incs=[(s_ac, 1)])
            qev = s_ac.n
            op("sync", lambda e, qst=qst, ti=ti, og=og:
               e.dma_start(out=q0_v[ti * 128:(ti + 1) * 128, og * 512:(og + 1) * 512], in_=qst),
               waits=[(s_ac, qev)], incs=[(s_sA, 16)])
    a0_done = (s_st.n, s_gst.n)

    WKB = abf[:, 20480:28672]
    WVB = abf[:, 28672:36864]
    op("gpsimd", lambda e: e.dma_start(out=WKB.rearrange("p (k o) -> p k o", k=16),
                                       in_=inp["l0_wk"].ap().rearrange("(k p) o -> p k o", p=128)),
       waits=[(s_mm, s_mm.n), (s_st, s_st.n), (s_sA, s_sA.n)], incs=[(s_wl, 16)])
    op("gpsimd", lambda e: e.dma_start(out=WVB.rearrange("p (k o) -> p k o", k=16),
                                       in_=inp["l0_wv"].ap().rearrange("(k p) o -> p k o", p=128)),
       incs=[(s_wl, 16)])
    wkv_v = s_wl.n
    k0_v = dr["k0d"].ap()
    v0_v = dr["v0d"].ap().rearrange("(h p) s -> p h s", h=NH)
    ln_acv2 = 0
    kst_ve = {}
    for mac in range(16):
        XF = abf[:, 4096 + (mac % 2) * 8192: 4096 + (mac % 2) * 8192 + 8192]
        for sub in range(4):
            ti = mac * 4 + sub
            xt = XT[ti % 2]
            s_x = [sems["s_xA"], sems["s_xB"]][ti % 2]
            op("sync", lambda e, xt=xt, ti=ti: e.dma_start(out=xt, in_=x_full_t[ti]),
               waits=[(s_ac, ctx.cache.get(("a1ln", ti - 2), 0))] if ti else
                     [(s_ac, s_ac.n), (s_ve, s_ve.n)],
               incs=[(s_x, 16)])
            lv = s_x.n
            ctx.wait("vector", [(s_x, lv)])
            ctx.wait("scalar", [(s_x, lv)])
            if ti >= 2:
                ctx.wait("scalar", [(s_mm, ctx.cache.get(("a1tr", ti - 2), 0))])
            _, ln_acv2 = emit_ln(ctx, sems, sb, xt, SQp[ti % 2], stat, XNT[ti % 2], cb=(ti % 2) * 40)
            ctx.cache[("a1ln", ti)] = ln_acv2
            acv = s_ac.n
            for kc in range(16):
                op("tensor", lambda e, ti=ti, kc=kc:
                   e.matmul(psB[:, kc * 128:(kc + 1) * 128], XNT[ti % 2][:, kc * 128:(kc + 1) * 128],
                            ident[:, :], start=True, stop=True),
                   waits=([(s_ac, acv), (s_ve, ctx.cache.get(("a1ev", ti - 2), 0))] if kc == 0 else []),
                   incs=([(s_mm, 1)] if kc == 15 else []))
            ctx.cache[("a1tr", ti)] = s_mm.n
            mmv = s_mm.n
            dst = XF.rearrange("p (k t) -> p k t", k=16)[:, :, sub * 128:(sub + 1) * 128]
            op("vector", lambda e, dst=dst:
               e.tensor_copy(dst, psB[:, :].rearrange("p (k t) -> p k t", k=16)),
               waits=([(s_mm, mmv)] + ([(s_st, s_st.n), (s_sA, s_sA.n)] if ti == 0 else [])),
               incs=[(s_ve, 1)])
            ctx.cache[("a1ev", ti)] = s_ve.n
        xf_ready = s_ve.n

        for h in range(NH):
            bank = (h % 2) * 512
            for kc in range(16):
                op("tensor", lambda e, h=h, kc=kc, bank=bank, XF=XF:
                   e.matmul(psA[:, bank:bank + 512], WKB[:, kc * 512 + h * 128: kc * 512 + h * 128 + 128],
                            XF[:, kc * 512:(kc + 1) * 512], start=(kc == 0), stop=(kc == 15)),
                   waits=([(s_wl, wkv_v), (s_ve, xf_ready), (s_ac, kst_ve.get(("k", h % 2), 0))]
                          if kc == 0 else []),
                   incs=([(s_mm, 1)] if kc == 15 else []))
            mmv = s_mm.n
            kst = abf[:, 36864 + (h % 2) * 512: 36864 + (h % 2) * 512 + 512]
            op("scalar", lambda e, kst=kst, bank=bank, h=h:
               e.activation(kst, psA[:, bank:bank + 512], AF.Identity, bias=sb["bk_sb"][:, h:h + 1]),
               waits=[(s_mm, mmv), (s_sA, s_sA.n)], incs=[(s_ac, 1)])
            kst_ve[("k", h % 2)] = s_ac.n
            acv = s_ac.n
            op("sync", lambda e, kst=kst, h=h, mac=mac:
               e.dma_start(out=k0_v[h * 128:(h + 1) * 128, mac * 512:(mac + 1) * 512], in_=kst),
               waits=[(s_ac, acv)], incs=[(s_sA, 16)])
        for sub in range(4):
            bank = 1024 + (sub % 2) * 512
            for kc in range(16):
                op("tensor", lambda e, sub=sub, kc=kc, bank=bank, XF=XF:
                   e.matmul(psA[:, bank:bank + 512], XF[:, kc * 512 + sub * 128: kc * 512 + sub * 128 + 128],
                            WVB[:, kc * 512:(kc + 1) * 512], start=(kc == 0), stop=False),
                   waits=([(s_ve, xf_ready), (s_ac, kst_ve.get(("v", sub % 2), 0))] if kc == 0 else []),
                   incs=[])
            op("tensor", lambda e, bank=bank:
               e.matmul(psA[:, bank:bank + 512], ones[:, :], sb["bv_row"][0:1, :], start=False, stop=True),
               incs=[(s_mm, 1)])
            mmv = s_mm.n
            vst = abf[:, 37888 + (sub % 2) * 512: 37888 + (sub % 2) * 512 + 512]
            op("scalar", lambda e, vst=vst, bank=bank:
               e.activation(vst, psA[:, bank:bank + 512], AF.Copy),
               waits=[(s_mm, mmv), (s_sB, s_sB.n)], incs=[(s_ac, 1)])
            kst_ve[("v", sub % 2)] = s_ac.n
            acv = s_ac.n
            n = mac * 4 + sub
            op("sync", lambda e, vst=vst, n=n:
               e.dma_start(out=v0_v[:, :, n * 128:(n + 1) * 128],
                           in_=vst.rearrange("p (h j) -> p h j", h=4)),
               waits=[(s_ac, acv)], incs=[(s_sB, 16)])
    a1_done = s_st.n

    emit_attention(ctx, nc, sems, sb, psA, psB, layer=0,
                   kd=dr["k0d"], vd=dr["v0d"], qd=dr["q0d"],
                   attn_out=(dr["attn_h0a"], dr["attn_h0b"]),
                   proj_done=s_st.n, zscr=dr["zscr"])

    barrier = [(s_st, s_st.n), (s_gst, s_gst.n), (s_sA, s_sA.n), (s_sB, s_sB.n)]
    op("gpsimd", lambda e: e.collective_compute(
        "AllGather", ALU.bypass, replica_groups=[[0, 1, 2, 3], [4, 5, 6, 7]],
        ins=[dr["attn_h0a"].ap().opt()], outs=[dr["ag0a"].ap().opt()]),
       waits=[(s_sA, ctx.cache[("hdone", 0, 1)])], incs=[(s_cc, 1)])
    op("gpsimd", lambda e: e.collective_compute(
        "AllGather", ALU.bypass, replica_groups=[[0, 1, 2, 3], [4, 5, 6, 7]],
        ins=[dr["attn_h0b"].ap().opt()], outs=[dr["ag0b"].ap().opt()]),
       waits=[(s_sB, ctx.cache[("hdone", 0, 3)])], incs=[(s_cc, 1)])
    ccv = s_cc.n

    if dbg:
        for nm, srcn in (("dbg_q0", "q0d"), ("dbg_k0", "k0d"), ("dbg_v0", "v0d")):
            op("sync", lambda e, nm=nm, srcn=srcn: e.dma_start(out=dbg[nm][:, :], in_=dr[srcn][:, :]),
               waits=barrier, incs=[(s_st, 16)])

    emit_wo_base(ctx, nc, sems, sb, psA, psB, ag=(dr["ag0a"], dr["ag0b"]), w_in=inp["l0_wo"],
                 bias_row=sb["bo_row"], x_res=inp["x_own"], base=dr["base0"], cc_wait=ccv,
                 own_scr=dr["attn_own0"])
    if dbg:
        op("sync", lambda e: e.dma_start(out=dbg["dbg_base0"][:, :], in_=dr["base0"][:, :]),
           waits=[(s_st, s_st.n), (s_sB, s_sB.n), (s_ve, s_ve.n)], incs=[(s_st, 16)])

    emit_ffn0(ctx, nc, sems, sb, psA, psB, dr, inp)
    if dbg:
        op("sync", lambda e: e.dma_start(out=dbg["dbg_x1"][:, :], in_=dr["x1d"][:, :]),
           waits=[(s_st, s_st.n), (s_sA, s_sA.n)], incs=[(s_st, 16)])

    for tt in range(4):
        op("gpsimd", lambda e, tt=tt: e.collective_compute(
            "AllGather", ALU.bypass, replica_groups=[[0, 1, 2, 3], [4, 5, 6, 7]],
            ins=[dr["x1d"].ap()[tt * 512:(tt + 1) * 512, :]],
            outs=[dr["x1ag"].ap()[tt * 2048:(tt + 1) * 2048, :]]),
           waits=[(s_sA, ctx.cache.get(("x1st", tt), s_sA.n))],
           incs=[(s_cc, 1)])
        ctx.cache[("x1cc", tt)] = s_cc.n


def emit_attention(ctx, nc, sems, sb, psA, psB, layer, kd, vd, qd, attn_out, proj_done,
                   zscr):
    op = ctx.op
    s_al, s_st, s_mm, s_ve, s_ac = (sems[k] for k in ("s_al", "s_st", "s_mm", "s_ve", "s_ac"))
    s_kv, s_qA, s_qB, s_sA, s_sB = (sems[k] for k in ("s_kv", "s_qA", "s_qB", "s_sA", "s_sB"))
    s_gst, s_xA, s_xB = sems["s_gst"], sems["s_xA"], sems["s_xB"]
    abf, af32, stat = sb["abf"], sb["af32"], sb["stat"]
    ident = sb["ident_bf"]
    onesr, onesc = sb["ones_col"], sb["ones128"]
    KF = abf[:, 0:8192]
    VT = abf[:, 8192:16384]
    KSUB = abf[:, 16384:17408]
    QT = abf[:, 17408:25600]
    PRB = [abf[:, 25600:26624], abf[:, 26624:27648]]
    ATT = [abf[:, 27648:28160], abf[:, 28160:28672]]
    AST = [abf[:, 28672:28800], abf[:, 28800:28928]]
    NEGBIG = abf[:, 30720:47104]
    NEGROWB = abf[0:1, 47104:48128]
    NEGCB = abf[:, 48128:48192]
    NEGC = af32[:, 0:64]
    ZCOL = [af32[:, 72:76], af32[:, 76:80]]
    ZINV = [af32[:, 80:84], af32[:, 84:88]]
    ZROW = [af32[0:1, 128:640], af32[0:1, 640:1152]]
    masks = sb["masks0"] if layer == 0 else sb["masks1"]

    if layer == 0:
        q_view = qd.ap().rearrange("(j r) (m d) -> j (r m) d", j=4, m=16)
    else:
        q_view = qd.ap()

    for h in range(NH):
        kvh = h if layer == 0 else h // 2
        op("sync", lambda e, kvh=kvh, kd=kd: e.dma_start(out=KF, in_=kd.ap()[kvh * 128:(kvh + 1) * 128, :]),
           waits=[(s_st, s_st.n), (s_sA, s_sA.n), (s_sB, s_sB.n), (s_gst, s_gst.n),
                  (sems["s_sC"], sems["s_sC"].n),
                  (s_mm, s_mm.n), (s_ve, s_ve.n), (s_ac, s_ac.n)],
           incs=[(s_kv, 16)])
        op("sync", lambda e, kvh=kvh, vd=vd: e.dma_start(out=VT, in_=vd.ap()[kvh * 128:(kvh + 1) * 128, :]),
           incs=[(s_kv, 16)])
        kv_v = s_kv.n
        op("vector", lambda e: e.tensor_copy(KSUB, KF.rearrange("p (n e) -> p n e", e=8)[:, :, 0]),
           waits=[(s_kv, kv_v)], incs=[(s_ve, 1)])
        ksub_v = s_ve.n

        for qb in range(64):
            s_q = [s_qA, s_qB][qb % 2]
            if layer == 0:
                qn = abf[:, 28928 + (qb % 2) * 128: 28928 + (qb % 2) * 128 + 128]
                op("sync", lambda e, qn=qn, h=h, qb=qb:
                   e.dma_start(out=qn, in_=q_view[h, qb * 128:(qb + 1) * 128, :]),
                   waits=[(s_mm, ctx.cache.get(("qtr", h, qb - 2), 0))], incs=[(s_q, 16)])
                qlv = s_q.n
                tb = 1024 + (qb % 2) * 512
                op("tensor", lambda e, qn=qn, tb=tb:
                   e.matmul(psB[:, tb:tb + 128], qn, ident[:, :], start=True, stop=True),
                   waits=[(s_q, qlv), (s_ve, ctx.cache.get(("qtc", h, qb - 2), 0))],
                   incs=[(s_mm, 1)])
                ctx.cache[("qtr", h, qb)] = s_mm.n
                mmv = s_mm.n
                op("vector", lambda e, qb=qb, tb=tb:
                   e.tensor_copy(QT[:, qb * 128:(qb + 1) * 128], psB[:, tb:tb + 128]),
                   waits=[(s_mm, mmv)], incs=[(s_ve, 1)])
                ctx.cache[("qtc", h, qb)] = s_ve.n
                qt_w = (s_ve, s_ve.n)
            else:
                op("sync", lambda e, h=h, qb=qb:
                   e.dma_start(out=QT[:, qb * 128:(qb + 1) * 128],
                               in_=q_view[h * 128:(h + 1) * 128, qb * 128:(qb + 1) * 128]),
                   waits=[(s_mm, ctx.cache.get(("sqk", h, qb - 2), 0))], incs=[(s_q, 16)])
                qt_w = (s_q, s_q.n)
            sb0 = (qb % 2) * 1024
            for half in range(2):
                op("tensor", lambda e, qb=qb, half=half, sb0=sb0:
                   e.matmul(psA[:, sb0 + half * 512: sb0 + half * 512 + 512],
                            QT[:, qb * 128:(qb + 1) * 128], KSUB[:, half * 512:(half + 1) * 512],
                            start=True, stop=True),
                   waits=([qt_w, (s_ve, ksub_v),
                           (s_ve, ctx.cache.get(("ngc", h, qb - 2), 0))] if half == 0 else []),
                   incs=([(s_mm, 1)] if half == 1 else []))
            ctx.cache[("sqk", h, qb)] = s_mm.n
            mmv = s_mm.n
            nst = (qb // 4 + 1) if layer == 0 else 16
            ncols = min(nst * 512, 8192) // 8
            op("vector", lambda e, qb=qb, ncols=ncols, sb0=sb0:
               e.reduce_max(NEGC[:, qb:qb + 1], psA[:, sb0:sb0 + ncols], axis=AX, negate=True),
               waits=[(s_mm, mmv)], incs=[(s_ve, 1)])
            ctx.cache[("ngc", h, qb)] = s_ve.n

        op("vector", lambda e: e.tensor_copy(NEGCB, NEGC[:, :]), incs=[(s_ve, 1)])
        negcb_v = s_ve.n
        rowev = 0
        nbev = {}
        for g in range(16):
            for i in range(4):
                w = []
                if i == 0:
                    w = [(s_ve, negcb_v)] if g == 0 else [(s_ac, rowev)]
                op("tensor", lambda e, g=g, i=i:
                   e.matmul(psB[0:1, 1536 + i * 128: 1536 + (i + 1) * 128],
                            NEGCB[:, (4 * g + i):(4 * g + i + 1)], ident[:, :],
                            start=True, stop=True),
                   waits=w, incs=([(s_mm, 1)] if i == 3 else []))
            rmv = s_mm.n
            rsl = (g % 2) * 512
            op("scalar", lambda e, rsl=rsl:
               e.activation(NEGROWB[:, rsl:rsl + 512], psB[0:1, 1536:2048], AF.Copy),
               waits=[(s_mm, rmv)], incs=[(s_ac, 1)])
            rowev = s_ac.n
            bw = [(s_ac, rowev)]
            if g >= 2:
                bw.append((s_ve, nbev[g - 2]))
            op("tensor", lambda e, g=g, rsl=rsl:
               e.matmul(psB[:, (g % 2) * 512:(g % 2) * 512 + 512], onesr[:, :],
                        NEGROWB[:, rsl:rsl + 512], start=True, stop=True),
               waits=bw, incs=[(s_mm, 1)])
            bcv = s_mm.n
            for rep in range(2):
                op("vector", lambda e, g=g, rep=rep:
                   e.tensor_copy(NEGBIG[:, g * 1024 + rep * 512: g * 1024 + rep * 512 + 512],
                                 psB[:, (g % 2) * 512:(g % 2) * 512 + 512]),
                   waits=([(s_mm, bcv)] if rep == 0 else []), incs=[(s_ve, 1)])
            nbev[g] = s_ve.n
        negb_done = s_ve.n
        rowev_last = rowev

        pv_war = {0: (s_ve, negb_done), 1: (s_ve, negb_done)}
        dn_war = (s_ac, rowev_last)
        slotg = {}
        prbg = {0: 0, 1: 0}
        nrm_prev = 0
        zprev = {0: 0, 1: 0}
        rinv_prev = {0: 0, 1: 0}
        for g in range(16):
            if layer == 0:
                chunks = list(range(4 * g + 4))
            else:
                chunks = [kc for kc in range(64) if not (g == 15 and 56 <= kc <= 59)]
            npairs = len(chunks) // 2
            pvb0 = (g % 2) * 512
            expv = {}
            pvstate = {}

            def pv_dn(p, g=g, chunks=chunks, pvb0=pvb0, expv=expv, pvstate=pvstate):
                for half in (0, 1):
                    idx2 = 2 * p + half
                    kc2 = chunks[idx2]
                    first = (idx2 == 0)
                    last = (idx2 == len(chunks) - 1)
                    w = [(s_ac, expv[p])] if half == 0 else []
                    if first:
                        w.append(pv_war[g % 2])
                    op("tensor", lambda e, kc2=kc2, first=first, last=last, p=p, half=half, pvb0=pvb0:
                       e.matmul(psB[:, pvb0:pvb0 + 512], VT[:, kc2 * 128:(kc2 + 1) * 128],
                                PRB[p % 2][:, half * 512:(half + 1) * 512],
                                start=first, stop=last),
                       waits=w, incs=([(s_mm, 1)] if last else []))
                    if last:
                        pvstate["stop"] = s_mm.n
                    w2 = [dn_war] if first else []
                    op("tensor", lambda e, first=first, last=last, p=p, half=half:
                       e.matmul(psB[0:1, 1536:2048], onesc[:, :],
                                PRB[p % 2][:, half * 512:(half + 1) * 512],
                                start=first, stop=last),
                       waits=w2, incs=([(s_mm, 1)] if half == 1 else []))
                prbg[p % 2] = s_mm.n

            for idx, kc in enumerate(chunks):
                p = idx // 2
                si = idx % 4
                w = []
                if si in slotg:
                    w.append((s_ac, slotg.pop(si)))
                if g == 0 and idx == 0:
                    w.append((s_ve, negb_done))
                op("tensor", lambda e, kc=kc, si=si, g=g:
                   e.matmul(psA[:, si * 512:(si + 1) * 512], KF[:, kc * 128:(kc + 1) * 128],
                            QT[:, g * 512:(g + 1) * 512], start=True, stop=True),
                   waits=w, incs=[(s_mm, 1)])
                if idx % 2 == 1:
                    qkv = s_mm.n
                    op("vector", lambda e, p=p, g=g:
                       e.tensor_tensor(psA[:, (p % 2) * 1024:(p % 2) * 1024 + 1024],
                                       psA[:, (p % 2) * 1024:(p % 2) * 1024 + 1024],
                                       NEGBIG[:, g * 1024:(g + 1) * 1024], ALU.add),
                       waits=[(s_mm, qkv)], incs=[(s_ve, 1)])
                    for idx2 in (idx - 1, idx):
                        kc2 = chunks[idx2]
                        mi = -1
                        if layer == 0 and kc2 // 4 == g:
                            mi = kc2 % 4
                        if layer == 1 and kc2 >= 56 and ((g == 14 and kc2 <= 59) or (g == 15 and kc2 >= 60)):
                            mi = kc2 - 56
                        if mi >= 0:
                            si2 = idx2 % 4
                            op("vector", lambda e, si2=si2, mi=mi:
                               e.tensor_tensor(psA[:, si2 * 512:(si2 + 1) * 512],
                                               psA[:, si2 * 512:(si2 + 1) * 512],
                                               masks[:, mi * 512:(mi + 1) * 512], ALU.add),
                               incs=[(s_ve, 1)])
                    bias_v = s_ve.n
                    if p >= 2:
                        pv_dn(p - 2)
                    op("scalar", lambda e, p=p:
                       e.activation(PRB[p % 2], psA[:, (p % 2) * 1024:(p % 2) * 1024 + 1024],
                                    AF.Exp, scale=SCL),
                       waits=[(s_ve, bias_v), (s_mm, prbg[p % 2])], incs=[(s_ac, 1)])
                    expv[p] = s_ac.n
                    slotg[(2 * p) % 4] = s_ac.n
                    slotg[(2 * p + 1) % 4] = s_ac.n
            if npairs >= 2:
                pv_dn(npairs - 2)
            pv_dn(npairs - 1)
            dn_stop = prbg[(npairs - 1) % 2]

            par = g % 2
            op("vector", lambda e, par=par, pvb0=pvb0: e.tensor_copy(ATT[par], psB[:, pvb0:pvb0 + 512]),
               waits=[(s_mm, pvstate["stop"])], incs=[(s_ve, 1)])
            att_v = s_ve.n
            pv_war[par] = (s_ve, att_v)
            op("scalar", lambda e, par=par: e.activation(ZROW[par], psB[0:1, 1536:2048], AF.Copy),
               waits=[(s_mm, dn_stop)], incs=[(s_ac, 1)])
            zrow_v = s_ac.n
            dn_war = (s_ac, zrow_v)
            s_z = s_xA if par == 0 else s_xB
            op("sync", lambda e, par=par: e.dma_start(out=zscr.ap()[par:par + 1, :], in_=ZROW[par]),
               waits=[(s_ac, zrow_v), (s_z, zprev[par])], incs=[(s_z, 16)])
            z1 = s_z.n
            op("sync", lambda e, par=par: e.dma_start(
                out=ZCOL[par],
                in_=zscr.ap()[par:par + 1, :].rearrange("o (c p) -> (o p) c", p=128),
                allow_slow_non_contiguous=True),
               waits=[(s_z, z1), (s_ve, rinv_prev[par])], incs=[(s_z, 16)])
            z2 = s_z.n
            zprev[par] = z2
            op("vector", lambda e, par=par: e.reciprocal(ZINV[par], ZCOL[par]),
               waits=[(s_z, z2)], incs=[(s_ve, 1)])
            rinv_v = s_ve.n
            rinv_prev[par] = rinv_v
            trv = []
            for i in range(4):
                w = [(s_ve, att_v), (s_ac, nrm_prev)] if i == 0 else []
                op("tensor", lambda e, i=i, par=par:
                   e.matmul(psB[:, 1024 + i * 128:1024 + (i + 1) * 128],
                            ATT[par][:, i * 128:(i + 1) * 128], ident[:, :],
                            start=True, stop=True),
                   waits=w, incs=[(s_mm, 1)])
                trv.append(s_mm.n)
            s_so = s_sA if h < 2 else s_sB
            aout = attn_out[h // 2]
            for i in range(4):
                ast = AST[i % 2]
                op("scalar", lambda e, i=i, ast=ast, par=par:
                   e.activation(ast, psB[:, 1024 + i * 128:1024 + (i + 1) * 128],
                                AF.Copy, scale=ZINV[par][:, i:i + 1]),
                   waits=[(s_mm, trv[i]), (s_ve, rinv_v), (s_so, s_so.n)], incs=[(s_ac, 1)])
                nv = s_ac.n
                qb2 = 4 * g + i
                op("sync", lambda e, ast=ast, qb2=qb2, aout=aout, h=h:
                   e.dma_start(out=aout.ap()[qb2 * 128:(qb2 + 1) * 128, (h % 2) * 128:(h % 2) * 128 + 128],
                               in_=ast),
                   waits=[(s_ac, nv)], incs=[(s_so, 16)])
            nrm_prev = s_ac.n
        ctx.cache[("hdone", layer, h)] = (s_sA if h < 2 else s_sB).n


def emit_wo_base(ctx, nc, sems, sb, psA, psB, ag, w_in, bias_row, x_res, base, cc_wait, own_scr):
    op = ctx.op
    s_wl, s_al, s_st, s_mm, s_ve, s_ac, s_cc = (
        sems[k] for k in ("s_wl", "s_al", "s_st", "s_mm", "s_ve", "s_ac", "s_cc"))
    s_gb, s_x2, s_sB = sems["s_gb"], sems["s_x2"], sems["s_sB"]
    abf, af32 = sb["abf"], sb["af32"]
    ident, ones = sb["ident_bf"], sb["ones_col"]
    WOB = abf[:, 0:32768]
    ATMp = [abf[:, 32768:34816], abf[:, 38912:40960]]
    AFM = [abf[:, 34816:36864], abf[:, 36864:38912]]
    x_res_t = x_res.ap().rearrange("(n p) d -> n p d", p=128)
    base_t = base.ap().rearrange("(n p) d -> n p d", p=128)

    op("gpsimd", lambda e: e.dma_start(out=WOB.rearrange("p (k o) -> p k o", k=16),
                                       in_=w_in.ap().rearrange("(k p) o -> p k o", p=128)),
       waits=[(s_mm, s_mm.n), (s_ve, s_ve.n)], incs=[(s_wl, 16)])
    wo_v = s_wl.n

    if ctx.who == "gpsimd" and "my_off" not in ctx.cache:
        pid = ctx.eng.partition_id()
        ctx.cache["my_off"] = ctx.eng.snap((pid % 4) * TOK)
    for half in range(2):
        agx = ag[half]
        for r in range(4):
            def gfn(e, r=r, half=half, agx=agx):
                mo = ctx.cache["my_off"]
                return e.dma_start(out=own_scr.ap()[:, r * 512 + half * 256: r * 512 + half * 256 + 256],
                                   in_=agx.ap()[r * S:, :][bass.ds(mo, TOK), :])
            op("gpsimd", gfn,
               waits=([(s_cc, cc_wait - 1 + half)] if r == 0 else []), incs=[(s_gb, 16)])
    own_v = s_gb.n
    own_t = own_scr.ap().rearrange("(n p) d -> n p d", p=128)

    for t in range(16):
        ATM = ATMp[t % 2]
        s_x = [sems["s_xA"], sems["s_xB"]][t % 2]
        op("sync", lambda e, t=t, ATM=ATM: e.dma_start(out=ATM, in_=own_t[t]),
           waits=[(s_gb, own_v), (s_mm, ctx.cache.get(("wotr", t - 2), 0))],
           incs=[(s_x, 16)])
        gv = (s_x, s_x.n)
        op("sync", lambda e, t=t: e.dma_start(out=af32[:, 0:2048], in_=x_res_t[t]),
           waits=[(s_ve, ctx.cache.get(("wocb", t - 1), 0))], incs=[(s_x2, 16)])
        xv = s_x2.n
        for kc in range(16):
            op("tensor", lambda e, kc=kc, ATM=ATM:
               e.matmul(psB[:, kc * 128:(kc + 1) * 128], ATM[:, kc * 128:(kc + 1) * 128],
                        ident[:, :], start=True, stop=True),
               waits=([gv, (s_wl, wo_v), (s_ve, ctx.cache.get(("woev", t - 1), 0))]
                      if kc == 0 else []),
               incs=([(s_mm, 1)] if kc == 15 else []))
        ctx.cache[("wotr", t)] = s_mm.n
        mmv = s_mm.n
        afm = AFM[t % 2]
        op("vector", lambda e, afm=afm: e.tensor_copy(afm, psB[:, :]),
           waits=[(s_mm, mmv)], incs=[(s_ve, 1)])
        ctx.cache[("woev", t)] = s_ve.n
        afm_v = s_ve.n
        for og in range(4):
            bank = (og % 2) * 512
            for kc in range(16):
                op("tensor", lambda e, afm=afm, kc=kc, og=og, bank=bank:
                   e.matmul(psA[:, bank:bank + 512], afm[:, kc * 128:(kc + 1) * 128],
                            WOB[:, kc * 2048 + og * 512: kc * 2048 + og * 512 + 512],
                            start=(kc == 0), stop=(kc == 15 and bias_row is None)),
                   waits=([(s_ve, afm_v), (s_ve, ctx.cache.get(("wocbv", t, og - 2), 0))]
                          if kc == 0 else []),
                   incs=([(s_mm, 1)] if (kc == 15 and bias_row is None) else []))
            if bias_row is not None:
                op("tensor", lambda e, og=og, bank=bank:
                   e.matmul(psA[:, bank:bank + 512], ones[:, :], bias_row[0:1, og * 512:(og + 1) * 512],
                            start=False, stop=True),
                   incs=[(s_mm, 1)])
            mmv = s_mm.n
            op("vector", lambda e, og=og, bank=bank:
               e.tensor_tensor(af32[:, 2048 + og * 512: 2048 + og * 512 + 512],
                               psA[:, bank:bank + 512], af32[:, og * 512:(og + 1) * 512], ALU.add),
               waits=[(s_mm, mmv), (s_x2, xv), (s_sB, s_sB.n)],
               incs=[(s_ve, 1)])
            ctx.cache[("wocbv", t, og)] = s_ve.n
        ctx.cache[("wocb", t)] = s_ve.n
        cbv = s_ve.n
        op("sync", lambda e, t=t: e.dma_start(out=base_t[t], in_=af32[:, 2048:4096]),
           waits=[(s_ve, cbv)], incs=[(s_sB, 16)])


def emit_ffn0(ctx, nc, sems, sb, psA, psB, dr, inp):
    op = ctx.op
    s_wl, s_al, s_st, s_mm, s_ve, s_ac = (
        sems[k] for k in ("s_wl", "s_al", "s_st", "s_mm", "s_ve", "s_ac"))
    s_wA, s_wB, s_x2, s_sA, s_sB = (sems[k] for k in ("s_wA", "s_wB", "s_x2", "s_sA", "s_sB"))
    abf, af32 = sb["abf"], sb["af32"]
    su_ug, su_dp = sb["su_ug"], sb["su_dp"]
    XNF = abf[:, 0:8192]
    UBF = abf[:, 0:8192]
    CBF = abf[:, 8192:40960]
    WUBp = [abf[:, 40960:43008], abf[:, 48128:50176]]
    WGBp = [abf[:, 43008:45056], abf[:, 50176:52224]]
    XH = af32[:, 0:8192]
    SIL = af32[:, 8192:8704]
    xn0_v = dr["xn0d"].ap().rearrange("(k p) t -> p k t", p=128)
    wupf_v = dr["wupf"].ap().rearrange("(k p) h -> p k h", p=128)
    wgatef_v = dr["wgatef"].ap().rearrange("(k p) h -> p k h", p=128)
    base_t = dr["base0"].ap().rearrange("(n p) d -> n p d", p=128)
    x1_t = dr["x1d"].ap().rearrange("(n p) d -> n p d", p=128)

    for tt in range(4):
        op("sync", lambda e, tt=tt:
           e.dma_start(out=XNF.rearrange("p (k t) -> p k t", k=16), in_=xn0_v[:, :, tt * 512:(tt + 1) * 512]),
           waits=[(s_mm, s_mm.n), (s_ve, s_ve.n), (s_st, s_st.n),
                  (sems["s_gst"], sems["s_gst"].n), (sems["s_gb"], sems["s_gb"].n)],
           incs=[(s_al, 16)])
        lv = s_al.n
        ctx.cache[("f0mm", tt, -1)] = s_mm.n
        for kc in range(16):
            op("vector", lambda e, kc=kc:
               e.tensor_scalar_mul(XH[:, kc * 512:(kc + 1) * 512], XNF[:, kc * 512:(kc + 1) * 512],
                                   su_ug[:, kc:kc + 1]),
               waits=([(s_al, lv)] if kc == 0 else []), incs=([(s_ve, 1)] if kc == 15 else []))
        for stg in range(4):
            b = 1 << stg
            for i in range(16):
                if i & b:
                    continue
                p = i | b
                op("vector", lambda e, i=i, p=p:
                   e.tensor_add(XH[:, i * 512:(i + 1) * 512], XH[:, i * 512:(i + 1) * 512],
                                XH[:, p * 512:(p + 1) * 512]), incs=[])
                op("vector", lambda e, i=i, p=p:
                   e.scalar_tensor_tensor(XH[:, p * 512:(p + 1) * 512], XH[:, p * 512:(p + 1) * 512],
                                          -2.0, XH[:, i * 512:(i + 1) * 512], ALU.mult, ALU.add),
                   incs=[])
        for kc in range(16):
            op("vector", lambda e, kc=kc:
               e.tensor_copy(UBF[:, kc * 512:(kc + 1) * 512], XH[:, kc * 512:(kc + 1) * 512]),
               incs=([(s_ve, 1)] if kc == 15 else []))
        u_v = s_ve.n
        for hc in range(64):
            WUB, WGB = WUBp[hc % 2], WGBp[hc % 2]
            s_w = [s_wA, s_wB][hc % 2]
            op("sync", lambda e, hc=hc, WUB=WUB:
               e.dma_start(out=WUB.rearrange("p (k h) -> p k h", k=16), in_=wupf_v[:, :, hc * 128:(hc + 1) * 128]),
               waits=[(s_mm, ctx.cache.get(("f0mm", tt, hc - 2), ctx.cache.get(("f0mm", tt, -1), 0)))],
               incs=[(s_w, 16)])
            op("sync", lambda e, hc=hc, WGB=WGB:
               e.dma_start(out=WGB.rearrange("p (k h) -> p k h", k=16), in_=wgatef_v[:, :, hc * 128:(hc + 1) * 128]),
               incs=[(s_w, 16)])
            wv = (s_w, s_w.n)
            bu = (hc % 2) * 1024
            for kc in range(16):
                op("tensor", lambda e, kc=kc, bu=bu, WUB=WUB:
                   e.matmul(psA[:, bu:bu + 512], WUB[:, kc * 128:(kc + 1) * 128],
                            UBF[:, kc * 512:(kc + 1) * 512], start=(kc == 0), stop=(kc == 15)),
                   waits=([wv, (s_ve, u_v), (s_ve, ctx.cache.get(("f0c", tt, hc - 2), 0))]
                          if kc == 0 else []), incs=[])
            for kc in range(16):
                op("tensor", lambda e, kc=kc, bu=bu, WGB=WGB:
                   e.matmul(psA[:, bu + 512:bu + 1024], WGB[:, kc * 128:(kc + 1) * 128],
                            UBF[:, kc * 512:(kc + 1) * 512], start=(kc == 0), stop=(kc == 15)),
                   incs=([(s_mm, 1)] if kc == 15 else []))
            ctx.cache[("f0mm", tt, hc)] = s_mm.n
            mmv = s_mm.n
            op("scalar", lambda e, bu=bu: e.activation(SIL, psA[:, bu + 512:bu + 1024], AF.Silu),
               waits=[(s_mm, mmv), (s_ve, ctx.cache.get(("f0c", tt, hc - 1), 0))], incs=[(s_ac, 1)])
            acv = s_ac.n
            op("vector", lambda e, hc=hc, bu=bu:
               e.scalar_tensor_tensor(CBF[:, hc * 512:(hc + 1) * 512], SIL, su_dp[:, hc:hc + 1],
                                      psA[:, bu:bu + 512], ALU.mult, ALU.mult),
               waits=[(s_ac, acv)], incs=[(s_ve, 1)])
            ctx.cache[("f0c", tt, hc)] = s_ve.n
        c_v = s_ve.n
        for sub in range(4):
            op("sync", lambda e, sub=sub, tt=tt:
               e.dma_start(out=XH[:, sub * 2048:(sub + 1) * 2048], in_=base_t[tt * 4 + sub]),
               waits=([(s_ve, c_v), (s_sB, s_sB.n)] if sub == 0 else []), incs=[(s_x2, 16)])
        bv = s_x2.n
        ctx.cache[("f0dmm0", tt, 0)] = s_mm.n
        wdownf_v = dr["wdownf"].ap()
        WDB = [abf[:, 40960:43008], abf[:, 43008:45056]]
        for og in range(4):
            for hc4 in range(16):
                wb = WDB[hc4 % 2]
                s_w = [s_wA, s_wB][hc4 % 2]
                op("sync", lambda e, wb=wb, hc4=hc4, og=og:
                   e.dma_start(out=wb.rearrange("p (k o) -> p k o", k=4),
                               in_=wdownf_v[hc4 * 512:(hc4 + 1) * 512, og * 512:(og + 1) * 512]
                               .rearrange("(k p) o -> p k o", p=128)),
                   waits=[(s_mm, ctx.cache.get(("f0dmm", tt, og, hc4 - 2),
                                               ctx.cache.get(("f0dmm0", tt, og), 0)))],
                   incs=[(s_w, 16)])
                wv = (s_w, s_w.n)
                for k in range(4):
                    hc = hc4 * 4 + k
                    for sub in range(4):
                        w_ = []
                        if k == 0 and sub == 0:
                            w_ = [wv, (s_ve, c_v)]
                            if hc == 0:
                                w_.append((s_ve, ctx.cache.get(("f0ev", tt, og - 1), 0)))
                        op("tensor", lambda e, wb=wb, k=k, hc=hc, sub=sub:
                           e.matmul(psB[:, sub * 512:(sub + 1) * 512],
                                    CBF[:, hc * 512 + sub * 128: hc * 512 + sub * 128 + 128],
                                    wb[:, k * 512:(k + 1) * 512],
                                    start=(hc == 0), stop=(hc == 63)),
                           waits=w_,
                           incs=([(s_mm, 1)] if (k == 3 and sub == 3) else []))
                ctx.cache[("f0dmm", tt, og, hc4)] = s_mm.n
            ctx.cache[("f0dmm0", tt, og + 1)] = s_mm.n
            dn_v = s_mm.n
            for sub in range(4):
                op("vector", lambda e, sub=sub, og=og:
                   e.tensor_tensor(af32[:, 10240 + sub * 512: 10240 + sub * 512 + 512],
                                   psB[:, sub * 512:(sub + 1) * 512],
                                   XH[:, sub * 2048 + og * 512: sub * 2048 + og * 512 + 512], ALU.add),
                   waits=[(s_mm, dn_v), (s_x2, bv), (s_sA, s_sA.n)],
                   incs=[(s_ve, 1)])
                vv = s_ve.n
                op("sync", lambda e, sub=sub, og=og, tt=tt:
                   e.dma_start(out=x1_t[tt * 4 + sub][:, og * 512:(og + 1) * 512],
                               in_=af32[:, 10240 + sub * 512: 10240 + sub * 512 + 512]),
                   waits=[(s_ve, vv)], incs=[(s_sA, 16)])
            ctx.cache[("f0ev", tt, og)] = s_ve.n
            ctx.cache[("x1st", tt)] = s_sA.n


def emit_layer1(ctx, nc, inp, out, dr, sb, psA, psB, sems, dbg):
    op = ctx.op
    s_wl, s_al, s_st, s_gst, s_mm, s_ve, s_ac, s_cc = (
        sems[k] for k in ("s_wl", "s_al", "s_st", "s_gst", "s_mm", "s_ve", "s_ac", "s_cc"))
    s_sA, s_sB = sems["s_sA"], sems["s_sB"]
    abf, af32, stat = sb["abf"], sb["af32"], sb["stat"]
    ident, ones = sb["ident_bf"], sb["ones_col"]
    x1ag_cc = s_cc.n

    XT = [af32[:, 0:2048], af32[:, 2048:4096]]
    SQp = [af32[:, 4096:6144], af32[:, 6144:8192]]
    SQ = af32[:, 8704:9728]
    RMS = af32[:, 8192:8704]
    XNT = [abf[:, 0:2048], abf[:, 2048:4096]]
    WQB = abf[:, 20480:28672]
    WKB = abf[:, 28672:32768]
    WVB = abf[:, 32768:36864]
    x1ag_t = dr["x1ag"].ap().rearrange("(n p) d -> n p d", p=128)
    q1_v = dr["q1d"].ap().rearrange("(h p) s -> p h s", h=NH)
    k1_v = dr["k1d"].ap().rearrange("(h p) s -> p h s", h=2)
    v1_v = dr["v1d"].ap().rearrange("(h p) s -> p h s", h=2)

    def perm(j):
        return ((j % 16) // 4) * 2048 + (j // 16) * 512 + (j % 4) * 128

    op("gpsimd", lambda e: e.dma_start(out=WQB.rearrange("p (k o) -> p k o", k=16),
                                       in_=inp["l1_wq"].ap().rearrange("(k p) o -> p k o", p=128)),
       waits=[(s_mm, s_mm.n), (s_ve, s_ve.n)], incs=[(s_wl, 16)])
    op("gpsimd", lambda e: e.dma_start(out=WKB.rearrange("p (k o) -> p k o", k=16),
                                       in_=inp["l1_wk"].ap().rearrange("(k p) o -> p k o", p=128)),
       incs=[(s_wl, 16)])
    op("gpsimd", lambda e: e.dma_start(out=WVB.rearrange("p (k o) -> p k o", k=16),
                                       in_=inp["l1_wv"].ap().rearrange("(k p) o -> p k o", p=128)),
       incs=[(s_wl, 16)])
    w_v = s_wl.n

    ln_acv = 0
    for mac in range(16):
        XF = abf[:, 4096 + (mac % 2) * 8192: 4096 + (mac % 2) * 8192 + 8192]
        for sub in range(4):
            ti = mac * 4 + sub
            xt = XT[ti % 2]
            s_x = [sems["s_xA"], sems["s_xB"]][ti % 2]
            ccw = [(s_cc, ctx.cache.get(("x1cc", ti // 16), x1ag_cc))] if ti % 16 == 0 else []
            op("sync", lambda e, xt=xt, ti=ti: e.dma_start(out=xt, in_=x1ag_t[ti]),
               waits=ccw + ([(s_ac, ctx.cache.get(("b1ln", ti - 2), 0))] if ti else
                            [(s_ac, s_ac.n), (s_ve, s_ve.n)]),
               incs=[(s_x, 16)])
            lv = s_x.n
            ctx.wait("vector", [(s_x, lv)])
            ctx.wait("scalar", [(s_x, lv)])
            if ti >= 2:
                ctx.wait("scalar", [(s_mm, ctx.cache.get(("b1tr", ti - 2), 0))])
            _, ln_acv = emit_ln(ctx, sems, sb, xt, SQp[ti % 2], stat, XNT[ti % 2], cb=(ti % 2) * 40)
            ctx.cache[("b1ln", ti)] = ln_acv
            acv = s_ac.n
            for kc in range(16):
                op("tensor", lambda e, ti=ti, kc=kc:
                   e.matmul(psB[:, kc * 128:(kc + 1) * 128], XNT[ti % 2][:, kc * 128:(kc + 1) * 128],
                            ident[:, :], start=True, stop=True),
                   waits=([(s_ac, acv), (s_ve, ctx.cache.get(("b1ev", ti - 2), 0))] if kc == 0 else []),
                   incs=([(s_mm, 1)] if kc == 15 else []))
            ctx.cache[("b1tr", ti)] = s_mm.n
            mmv = s_mm.n
            dst = XF.rearrange("p (k t) -> p k t", k=16)[:, :, sub * 128:(sub + 1) * 128]
            op("vector", lambda e, dst=dst:
               e.tensor_copy(dst, psB[:, :].rearrange("p (k t) -> p k t", k=16)),
               waits=[(s_mm, mmv)], incs=[(s_ve, 1)])
            ctx.cache[("b1ev", ti)] = s_ve.n
        xf_ready = s_ve.n

        for sub in range(4):
            ti = mac * 4 + sub
            bank = (sub % 2) * 512
            for kc in range(16):
                op("tensor", lambda e, kc=kc, sub=sub, bank=bank, XF=XF:
                   e.matmul(psA[:, bank:bank + 512], XF[:, kc * 512 + sub * 128: kc * 512 + sub * 128 + 128],
                            WQB[:, kc * 512:(kc + 1) * 512], start=(kc == 0), stop=(kc == 15)),
                   waits=([(s_wl, w_v), (s_ve, xf_ready),
                           (s_ac, ctx.cache.get(("qrm", ti - 2), 0))] if kc == 0 else []),
                   incs=([(s_mm, 1)] if kc == 15 else []))
            qmm = s_mm.n
            op("scalar", lambda e, bank=bank: e.activation(SQ[:, 0:512], psA[:, bank:bank + 512],
                                                           AF.Square),
               waits=[(s_mm, qmm), (s_ve, ctx.cache.get(("qsq", ti - 1), 0))], incs=[(s_ac, 1)])
            sqv = s_ac.n
            op("vector", lambda e: e.reduce_sum(RMS[:, 0:4],
                                                SQ[:, 0:512].rearrange("p (h j) -> p h j", h=4), axis=AX),
               waits=[(s_ac, sqv)], incs=[(s_ve, 1)])
            ctx.cache[("qsq", ti)] = s_ve.n
            rsv = s_ve.n
            op("scalar", lambda e: e.activation(RMS[:, 4:8], RMS[:, 0:4], AF.Sqrt,
                                                scale=1.0 / HD, bias=1e-6),
               waits=[(s_ve, rsv)], incs=[(s_ac, 1)])
            acv = s_ac.n
            op("vector", lambda e: e.reciprocal(RMS[:, 8:12], RMS[:, 4:8]),
               waits=[(s_ac, acv)], incs=[(s_ve, 1)])
            rzv = s_ve.n
            qn = abf[:, 36864 + (sub % 2) * 512: 36864 + (sub % 2) * 512 + 512]
            for hh in range(4):
                op("vector", lambda e, hh=hh, qn=qn, bank=bank:
                   e.tensor_scalar_mul(qn[:, hh * 128:(hh + 1) * 128],
                                       psA[:, bank + hh * 128: bank + hh * 128 + 128],
                                       RMS[:, 8 + hh:9 + hh]),
                   waits=([(s_ve, rzv), (s_mm, ctx.cache.get(("qtr1", ti - 2), 0))] if hh == 0 else []),
                   incs=([(s_ve, 1)] if hh == 3 else []))
            ctx.cache[("qrm", ti)] = s_ac.n
            qnv = s_ve.n
            for hh in range(4):
                op("tensor", lambda e, hh=hh, qn=qn:
                   e.matmul(psB[:, 1024 + hh * 128: 1024 + hh * 128 + 128],
                            qn[:, hh * 128:(hh + 1) * 128], ident[:, :], start=True, stop=True),
                   waits=([(s_ve, qnv), (s_ve, ctx.cache.get(("qfe", ti - 1), 0))] if hh == 0 else []),
                   incs=([(s_mm, 1)] if hh == 3 else []))
            ctx.cache[("qtr1", ti)] = s_mm.n
            mmv = s_mm.n
            qf = abf[:, 37888 + (sub % 2) * 512: 37888 + (sub % 2) * 512 + 512]
            op("vector", lambda e, qf=qf: e.tensor_scalar_mul(qf, psB[:, 1024:1536], sb["qnw"][:, 0:1]),
               waits=[(s_mm, mmv), (s_sA, s_sA.n)], incs=[(s_ve, 1)])
            ctx.cache[("qfe", ti)] = s_ve.n
            qfv = s_ve.n
            op("sync", lambda e, qf=qf, ps=perm(ti):
               e.dma_start(out=q1_v[:, :, ps:ps + 128],
                           in_=qf.rearrange("p (h t) -> p h t", h=4)),
               waits=[(s_ve, qfv)], incs=[(s_sA, 16)])

            for kc in range(16):
                op("tensor", lambda e, kc=kc, sub=sub, XF=XF:
                   e.matmul(psA[:, 1024:1280], XF[:, kc * 512 + sub * 128: kc * 512 + sub * 128 + 128],
                            WKB[:, kc * 256:(kc + 1) * 256], start=(kc == 0), stop=(kc == 15)),
                   waits=([(s_ve, ctx.cache.get(("krm", ti - 1), 0))] if kc == 0 else []),
                   incs=([(s_mm, 1)] if kc == 15 else []))
            kmm = s_mm.n
            op("scalar", lambda e: e.activation(SQ[:, 512:768], psA[:, 1024:1280], AF.Square),
               waits=[(s_mm, kmm), (s_ve, ctx.cache.get(("ksq", ti - 1), 0))], incs=[(s_ac, 1)])
            sqv = s_ac.n
            op("vector", lambda e: e.reduce_sum(RMS[:, 12:14],
                                                SQ[:, 512:768].rearrange("p (h j) -> p h j", h=2), axis=AX),
               waits=[(s_ac, sqv)], incs=[(s_ve, 1)])
            ctx.cache[("ksq", ti)] = s_ve.n
            rsv = s_ve.n
            op("scalar", lambda e: e.activation(RMS[:, 14:16], RMS[:, 12:14], AF.Sqrt,
                                                scale=1.0 / HD, bias=1e-6),
               waits=[(s_ve, rsv)], incs=[(s_ac, 1)])
            acv = s_ac.n
            op("vector", lambda e: e.reciprocal(RMS[:, 16:18], RMS[:, 14:16]),
               waits=[(s_ac, acv)], incs=[(s_ve, 1)])
            rzv = s_ve.n
            kn = abf[:, 38912 + (sub % 2) * 256: 38912 + (sub % 2) * 256 + 256]
            for hh in range(2):
                op("vector", lambda e, hh=hh, kn=kn:
                   e.tensor_scalar_mul(kn[:, hh * 128:(hh + 1) * 128],
                                       psA[:, 1024 + hh * 128: 1024 + hh * 128 + 128],
                                       RMS[:, 16 + hh:17 + hh]),
                   waits=([(s_ve, rzv), (s_mm, ctx.cache.get(("ktr1", ti - 2), 0))] if hh == 0 else []),
                   incs=([(s_ve, 1)] if hh == 1 else []))
            ctx.cache[("krm", ti)] = s_ve.n
            knv = s_ve.n
            for hh in range(2):
                op("tensor", lambda e, hh=hh, kn=kn:
                   e.matmul(psB[:, 1536 + hh * 128: 1536 + hh * 128 + 128],
                            kn[:, hh * 128:(hh + 1) * 128], ident[:, :], start=True, stop=True),
                   waits=([(s_ve, knv), (s_ve, ctx.cache.get(("kfe", ti - 1), 0))] if hh == 0 else []),
                   incs=([(s_mm, 1)] if hh == 1 else []))
            ctx.cache[("ktr1", ti)] = s_mm.n
            mmv = s_mm.n
            kf = abf[:, 39424 + (sub % 2) * 256: 39424 + (sub % 2) * 256 + 256]
            op("vector", lambda e, kf=kf: e.tensor_scalar_mul(kf, psB[:, 1536:1792], sb["knw"][:, 0:1]),
               waits=[(s_mm, mmv), (s_sB, s_sB.n)], incs=[(s_ve, 1)])
            ctx.cache[("kfe", ti)] = s_ve.n
            kfv = s_ve.n
            op("sync", lambda e, kf=kf, ps=perm(ti):
               e.dma_start(out=k1_v[:, :, ps:ps + 128],
                           in_=kf.rearrange("p (h t) -> p h t", h=2)),
               waits=[(s_ve, kfv)], incs=[(s_sB, 16)])

            for kc in range(16):
                op("tensor", lambda e, kc=kc, sub=sub, XF=XF:
                   e.matmul(psA[:, 1536:1792], XF[:, kc * 512 + sub * 128: kc * 512 + sub * 128 + 128],
                            WVB[:, kc * 256:(kc + 1) * 256], start=(kc == 0), stop=(kc == 15)),
                   waits=([(s_ac, ctx.cache.get(("vev", ti - 1), 0))] if kc == 0 else []),
                   incs=([(s_mm, 1)] if kc == 15 else []))
            vmm = s_mm.n
            vst = abf[:, 39936 + (sub % 2) * 256: 39936 + (sub % 2) * 256 + 256]
            op("scalar", lambda e, vst=vst: e.activation(vst, psA[:, 1536:1792], AF.Copy),
               waits=[(s_mm, vmm), (sems["s_sC"], sems["s_sC"].n)], incs=[(s_ac, 1)])
            ctx.cache[("vev", ti)] = s_ac.n
            acv = s_ac.n
            op("sync", lambda e, vst=vst, ps=perm(ti):
               e.dma_start(out=v1_v[:, :, ps:ps + 128],
                           in_=vst.rearrange("p (h t) -> p h t", h=2)),
               waits=[(s_ac, acv)], incs=[(sems["s_sC"], 16)])
    a1q_done = s_st.n

    x1_t = dr["x1d"].ap().rearrange("(n p) d -> n p d", p=128)
    xn1_v = dr["xn1d"].ap().rearrange("(k p) t -> p k t", p=128)
    XNF = abf[:, 4096:36864]
    ln_acv2 = 0
    for ti in range(16):
        xt = XT[ti % 2]
        s_x = [sems["s_xA"], sems["s_xB"]][ti % 2]
        op("sync", lambda e, xt=xt, ti=ti: e.dma_start(out=xt, in_=x1_t[ti]),
           waits=([(s_ac, ctx.cache.get(("o1ln", ti - 2), 0))] if ti else
                  [(s_ac, s_ac.n), (s_ve, s_ve.n)]),
           incs=[(s_x, 16)])
        lv = s_x.n
        ctx.wait("vector", [(s_x, lv)])
        ctx.wait("scalar", [(s_x, lv)])
        if ti >= 2:
            ctx.wait("scalar", [(s_mm, ctx.cache.get(("o1tr", ti - 2), 0))])
        _, ln_acv2 = emit_ln(ctx, sems, sb, xt, SQp[ti % 2], stat, XNT[ti % 2], cb=(ti % 2) * 40)
        ctx.cache[("o1ln", ti)] = ln_acv2
        acv = s_ac.n
        for kc in range(16):
            op("tensor", lambda e, ti=ti, kc=kc:
               e.matmul(psB[:, kc * 128:(kc + 1) * 128], XNT[ti % 2][:, kc * 128:(kc + 1) * 128],
                        ident[:, :], start=True, stop=True),
               waits=([(s_ac, acv), (s_ve, ctx.cache.get(("o1ev", ti - 1), 0))] if kc == 0 else []),
               incs=([(s_mm, 1)] if kc == 15 else []))
        ctx.cache[("o1tr", ti)] = s_mm.n
        mmv = s_mm.n
        dst = XNF[:, :].rearrange("p (k t) -> p k t", k=16)[:, :, ti * 128:(ti + 1) * 128]
        op("vector", lambda e, dst=dst:
           e.tensor_copy(dst, psB[:, :].rearrange("p (k t) -> p k t", k=16)),
           waits=[(s_mm, mmv)], incs=[(s_ve, 1)])
        ctx.cache[("o1ev", ti)] = s_ve.n
    xnf_ready = s_ve.n
    op("sync", lambda e: e.dma_start(out=xn1_v[:, :, :],
                                     in_=XNF[:, :].rearrange("p (k t) -> p k t", k=16)),
       waits=[(s_ve, xnf_ready)], incs=[(s_st, 16)])

    emit_attention(ctx, nc, sems, sb, psA, psB, layer=1,
                   kd=dr["k1d"], vd=dr["v1d"], qd=dr["q1d"],
                   attn_out=(dr["attn_h1a"], dr["attn_h1b"]),
                   proj_done=s_st.n, zscr=dr["zscr"])

    op("gpsimd", lambda e: e.collective_compute(
        "AllGather", ALU.bypass, replica_groups=[[0, 1, 2, 3], [4, 5, 6, 7]],
        ins=[dr["attn_h1a"].ap().opt()], outs=[dr["ag1a"].ap().opt()]),
       waits=[(s_sA, ctx.cache[("hdone", 1, 1)])], incs=[(s_cc, 1)])
    op("gpsimd", lambda e: e.collective_compute(
        "AllGather", ALU.bypass, replica_groups=[[0, 1, 2, 3], [4, 5, 6, 7]],
        ins=[dr["attn_h1b"].ap().opt()], outs=[dr["ag1b"].ap().opt()]),
       waits=[(s_sB, ctx.cache[("hdone", 1, 3)])], incs=[(s_cc, 1)])
    ccv = s_cc.n


    emit_wo_base(ctx, nc, sems, sb, psA, psB, ag=(dr["ag1a"], dr["ag1b"]), w_in=inp["l1_wo"],
                 bias_row=None, x_res=dr["x1d"], base=dr["base1"], cc_wait=ccv,
                 own_scr=dr["attn_own1"])
    if dbg:
        op("sync", lambda e: e.dma_start(out=dbg["dbg_base1"][:, :], in_=dr["base1"][:, :]),
           waits=[(s_st, s_st.n), (s_sB, s_sB.n), (s_ve, s_ve.n)], incs=[(s_st, 16)])

    emit_ffn1(ctx, nc, sems, sb, psA, psB, dr, inp, out)


def emit_ffn1(ctx, nc, sems, sb, psA, psB, dr, inp, out):
    op = ctx.op
    s_wl, s_al, s_st, s_mm, s_ve, s_ac = (
        sems[k] for k in ("s_wl", "s_al", "s_st", "s_mm", "s_ve", "s_ac"))
    s_gA, s_gB, s_x2, s_sA, s_sB = (sems[k] for k in ("s_gA", "s_gB", "s_x2", "s_sA", "s_sB"))
    s_wA, s_wB = sems["s_wA"], sems["s_wB"]
    abf, af32, stat = sb["abf"], sb["af32"], sb["stat"]
    ident, ones = sb["ident_bf"], sb["ones_col"]
    xn1_v = dr["xn1d"].ap().rearrange("(k p) t -> p k t", p=128)
    base_t = dr["base1"].ap().rearrange("(n p) d -> n p d", p=128)
    out_t = out.ap().rearrange("(n p) d -> n p d", p=128)
    w_fmb = sb["w_fmb"]

    WGS = abf[:, 2048:2176]
    op("gpsimd", lambda e: e.dma_start(out=WGS.rearrange("p (k o) -> p k o", k=16),
                                       in_=inp["l1_wg"].ap().rearrange("(k p) o -> p k o", p=128)),
       waits=[(s_mm, s_mm.n), (s_ve, s_ve.n)], incs=[(s_wl, 16)])
    wg_v = s_wl.n
    G8 = af32[:, 0:8]
    GT = af32[:, 8:16]
    GK = af32[:, 16:24]
    GE = af32[:, 24:32]
    for t in range(16):
        XC = abf[:, 0:2048]
        op("sync", lambda e, t=t: e.dma_start(out=XC.rearrange("p (k t) -> p k t", k=16),
                                              in_=xn1_v[:, :, t * 128:(t + 1) * 128]),
           waits=[(s_mm, ctx.cache.get(("g8mm", t - 1), 0))], incs=[(s_al, 16)])
        lv = s_al.n
        for kc in range(16):
            op("tensor", lambda e, kc=kc:
               e.matmul(psA[:, 0:8], XC[:, kc * 128:(kc + 1) * 128], WGS[:, kc * 8:(kc + 1) * 8],
                        start=(kc == 0), stop=False),
               waits=([(s_al, lv), (s_wl, wg_v), (s_ve, ctx.cache.get(("g8ev", t - 1), 0))]
                      if kc == 0 else []), incs=[])
        op("tensor", lambda e: e.matmul(psA[:, 0:8], ones[:, :], sb["bg_row"][0:1, :],
                                        start=False, stop=True), incs=[(s_mm, 1)])
        ctx.cache[("g8mm", t)] = s_mm.n
        mmv = s_mm.n
        op("vector", lambda e: e.tensor_copy(G8, psA[:, 0:8]), waits=[(s_mm, mmv), (s_ac, s_ac.n)],
           incs=[(s_ve, 1)])
        ctx.cache[("g8ev", t)] = s_ve.n
        op("vector", lambda e: e.tensor_reduce(stat[:, 20:21], G8, AX, ALU.min), incs=[(s_ve, 1)])
        op("vector", lambda e: e.tensor_single_scalar(GT, G8, stat[:, 20:21], ALU.is_equal), incs=[(s_ve, 1)])
        op("vector", lambda e: e.scalar_tensor_tensor(GT, GT, 1e30, G8, ALU.mult, ALU.add), incs=[(s_ve, 1)])
        op("vector", lambda e: e.tensor_reduce(stat[:, 21:22], GT, AX, ALU.min), incs=[(s_ve, 1)])
        op("vector", lambda e: e.tensor_single_scalar(GK, G8, stat[:, 21:22], ALU.is_gt), incs=[(s_ve, 1)])
        op("vector", lambda e: e.reduce_max(stat[:, 22:23], G8, axis=AX, negate=True), incs=[(s_ve, 1)])
        vv = s_ve.n
        op("scalar", lambda e: e.activation(GE, G8, AF.Exp, bias=stat[:, 22:23]),
           waits=[(s_ve, vv)], incs=[(s_ac, 1)])
        acv = s_ac.n
        op("vector", lambda e: e.tensor_mul(GE, GE, GK), waits=[(s_ac, acv)], incs=[(s_ve, 1)])
        op("vector", lambda e: e.reduce_sum(stat[:, 23:24], GE, axis=AX), incs=[(s_ve, 1)])
        op("vector", lambda e: e.reciprocal(stat[:, 24:25], stat[:, 23:24]), incs=[(s_ve, 1)])
        op("vector", lambda e: e.tensor_scalar_mul(GE, GE, stat[:, 24:25]), incs=[(s_ve, 1)])
        GEB = abf[:, 2176:2184]
        op("vector", lambda e: e.tensor_copy(GEB, GE), waits=[(s_mm, s_mm.n)], incs=[(s_ve, 1)])
        gv = s_ve.n
        op("tensor", lambda e: e.matmul(psB[0:8, 0:128], GEB, ident[:, :], start=True, stop=True),
           waits=[(s_ve, gv), (s_ve, ctx.cache.get(("gwe", t - 1), 0))], incs=[(s_mm, 1)])
        mmv = s_mm.n
        op("vector", lambda e, t=t: e.tensor_copy(w_fmb[0:8, t * 128:(t + 1) * 128], psB[0:8, 0:128]),
           waits=[(s_mm, mmv)], incs=[(s_ve, 1)])
        ctx.cache[("gwe", t)] = s_ve.n
    gate_done_ve = s_ve.n

    XNF = abf[:, 0:8192]
    S1B = abf[:, 8192:16384]
    CBF = abf[:, 16384:40960]
    WBg = [abf[:, 40960:43008], abf[:, 43008:45056]]
    WBu = [abf[:, 48128:50176], abf[:, 50176:52224]]
    WBC = abf[:, 45056:48128]
    SIL = af32[:, 8192:8704]
    UPS = af32[:, 8704:9216]
    wgate_in = dr["wgate_b"].ap()
    wup_in = dr["wup_b"].ap()
    wdown_in = dr["wdown_b"].ap()
    wsup_in = dr["wsup_b"].ap().rearrange("(k p) o -> p k o", p=128)
    wsdown_in = dr["wsdown_b"].ap()
    precast_v = sems["s_gst"].n

    for tt in range(4):
        op("sync", lambda e, tt=tt:
           e.dma_start(out=XNF.rearrange("p (k t) -> p k t", k=16), in_=xn1_v[:, :, tt * 512:(tt + 1) * 512]),
           waits=[(s_mm, s_mm.n), (s_ve, s_ve.n)], incs=[(s_al, 16)])
        xv = s_al.n
        for n in range(6):
            op("sync", lambda e, n=n, tt=tt:
               e.dma_start(out=sb["w_row"][0:1, n * 512:(n + 1) * 512],
                           in_=w_fmb[n:n + 1, tt * 512:(tt + 1) * 512]),
               waits=([(s_ve, gate_done_ve), (s_mm, ctx.cache.get(("wrow_mm", tt - 1), 0))]
                      if n == 0 else []),
               incs=[(s_x2, 16)])
        wrow_v = s_x2.n
        for n in range(6):
            op("tensor", lambda e, n=n:
               e.matmul(psB[:, 1024 + (n % 2) * 512: 1024 + (n % 2) * 512 + 512],
                        ones[:, :], sb["w_row"][0:1, n * 512:(n + 1) * 512],
                        start=True, stop=True),
               waits=[(s_x2, wrow_v), (s_ve, s_ve.n)],
               incs=[(s_mm, 1)])
            ctx.cache[("wrow_mm", tt)] = s_mm.n
            mmv = s_mm.n
            op("vector", lambda e, n=n:
               e.tensor_copy(WBC[:, n * 512:(n + 1) * 512], psB[:, 1024 + (n % 2) * 512: 1024 + (n % 2) * 512 + 512]),
               waits=[(s_mm, mmv)], incs=[(s_ve, 1)])
        wbc_v = s_ve.n

        tt_start_mm = s_mm.n
        for hc2 in range(16):
            wb = WBg[hc2 % 2]
            s_w = [s_wA, s_wB][hc2 % 2]
            op("sync", lambda e, wb=wb, hc2=hc2:
               e.dma_start(out=wb.rearrange("p (k h) -> p k h", k=16),
                           in_=wsup_in[:, :, hc2 * 128:(hc2 + 1) * 128]),
               waits=[(s_mm, ctx.cache.get(("sup", tt, hc2 - 2), tt_start_mm)),
                      (sems["s_gst"], precast_v)],
               incs=[(s_w, 16)])
            wv = (s_w, s_w.n)
            bank = (hc2 % 2) * 512
            for kc in range(16):
                op("tensor", lambda e, wb=wb, kc=kc, bank=bank:
                   e.matmul(psA[:, bank:bank + 512], wb[:, kc * 128:(kc + 1) * 128],
                            XNF[:, kc * 512:(kc + 1) * 512], start=(kc == 0), stop=(kc == 15)),
                   waits=([wv, (s_al, xv), (s_ac, ctx.cache.get(("s1e", tt, hc2 - 2), 0))]
                          if kc == 0 else []),
                   incs=([(s_mm, 1)] if kc == 15 else []))
            ctx.cache[("sup", tt, hc2)] = s_mm.n
            mmv = s_mm.n
            op("scalar", lambda e, hc2=hc2, bank=bank:
               e.activation(S1B[:, hc2 * 512:(hc2 + 1) * 512], psA[:, bank:bank + 512], AF.Silu,
                            bias=sb["bsup_sb"][:, hc2:hc2 + 1]),
               waits=[(s_mm, mmv)], incs=[(s_ac, 1)])
            ctx.cache[("s1e", tt, hc2)] = s_ac.n
        s1_v = s_ac.n

        ctx.cache[("exw0", tt)] = s_mm.n
        for n in range(6):
            for hc in range(8):
                ci = n * 8 + hc
                wbg, wbu = WBg[ci % 2], WBu[ci % 2]
                s_w = [s_wA, s_wB][ci % 2]
                op("sync", lambda e, wbg=wbg, n=n, hc=hc:
                   e.dma_start(out=wbg.rearrange("p (k h) -> p k h", k=16),
                               in_=wgate_in[n].rearrange("(k p) h -> p k h", p=128)[:, :, hc * 128:(hc + 1) * 128]),
                   waits=[(s_mm, ctx.cache.get(("exw", tt, ci - 2), ctx.cache.get(("exw0", tt), 0)))] +
                         ([(sems["s_gst"], precast_v)] if ci == 0 else []),
                   incs=[(s_w, 16)])
                op("sync", lambda e, wbu=wbu, n=n, hc=hc:
                   e.dma_start(out=wbu.rearrange("p (k h) -> p k h", k=16),
                               in_=wup_in[n].rearrange("(k p) h -> p k h", p=128)[:, :, hc * 128:(hc + 1) * 128]),
                   incs=[(s_w, 16)])
                wv = (s_w, s_w.n)
                bank = (ci % 2) * 1024
                for kc in range(16):
                    op("tensor", lambda e, wbg=wbg, kc=kc, bank=bank:
                       e.matmul(psA[:, bank:bank + 512], wbg[:, kc * 128:(kc + 1) * 128],
                                XNF[:, kc * 512:(kc + 1) * 512], start=(kc == 0), stop=(kc == 15)),
                       waits=([wv, (s_ve, ctx.cache.get(("cbe", tt, ci - 2), 0))] +
                              ([(s_ac, s_ac.n)] if ci < 2 else [])
                              if kc == 0 else []),
                       incs=([(s_mm, 1)] if kc == 15 else []))
                for kc in range(16):
                    op("tensor", lambda e, wbu=wbu, kc=kc, bank=bank:
                       e.matmul(psA[:, bank + 512:bank + 1024], wbu[:, kc * 128:(kc + 1) * 128],
                                XNF[:, kc * 512:(kc + 1) * 512], start=(kc == 0), stop=(kc == 15)),
                       incs=([(s_mm, 1)] if kc == 15 else []))
                ctx.cache[("exw", tt, ci)] = s_mm.n
                umm = s_mm.n
                op("scalar", lambda e, n=n, hc=hc, bank=bank:
                   e.activation(SIL, psA[:, bank:bank + 512], AF.Silu,
                                bias=sb["bgate_sb"][:, n * 8 + hc:n * 8 + hc + 1]),
                   waits=[(s_mm, umm), (s_ve, ctx.cache.get(("cbe", tt, ci - 1), 0))], incs=[(s_ac, 1)])
                op("scalar", lambda e, n=n, hc=hc, bank=bank:
                   e.activation(UPS, psA[:, bank + 512:bank + 1024], AF.Identity,
                                bias=sb["bup_sb"][:, n * 8 + hc:n * 8 + hc + 1]),
                   incs=[(s_ac, 1)])
                acv = s_ac.n
                op("vector", lambda e: e.tensor_mul(SIL, SIL, UPS), waits=[(s_ac, acv)], incs=[(s_ve, 1)])
                op("vector", lambda e, ci=ci, n=n:
                   e.tensor_tensor(CBF[:, ci * 512:(ci + 1) * 512], SIL, WBC[:, n * 512:(n + 1) * 512], ALU.mult),
                   waits=[(s_ve, wbc_v)], incs=[(s_ve, 1)])
                ctx.cache[("cbe", tt, ci)] = s_ve.n
        c_v = s_ve.n

        XH = af32[:, 0:8192]
        for sub in range(4):
            op("sync", lambda e, sub=sub, tt=tt:
               e.dma_start(out=XH[:, sub * 2048:(sub + 1) * 2048], in_=base_t[tt * 4 + sub]),
               waits=([(s_ve, c_v), (s_sB, s_sB.n)] if sub == 0 else []), incs=[(s_x2, 16)])
        bv = s_x2.n

        srcs = [("e", n, q) for n in range(6) for q in range(2)] + \
               [("s", 0, q) for q in range(4)]
        down_guard = s_mm.n
        for og in range(4):
            si_mm = {}
            for si, (kind, n, q) in enumerate(srcs):
                wb = [WBg[0], WBg[1], WBu[0], WBu[1]][si % 4]
                s_w = [s_wA, s_wB][si % 2]
                if kind == "e":
                    src_ap = wdown_in[n][q * 512:(q + 1) * 512, og * 512:(og + 1) * 512]
                else:
                    src_ap = wsdown_in[q * 512:(q + 1) * 512, og * 512:(og + 1) * 512]
                op("sync", lambda e, wb=wb, src_ap=src_ap:
                   e.dma_start(out=wb.rearrange("p (k o) -> p k o", k=4),
                               in_=src_ap.rearrange("(k p) o -> p k o", p=128)),
                   waits=[(s_mm, si_mm.get(si - 2, down_guard))] +
                         ([(sems["s_gst"], precast_v)] if si == 0 else []),
                   incs=[(s_w, 16)])
                wv = (s_w, s_w.n)
                for k in range(4):
                    lhs0 = (CBF[:, (n * 8 + q * 4 + k) * 512:(n * 8 + q * 4 + k) * 512 + 512]
                            if kind == "e" else S1B[:, (q * 4 + k) * 512:(q * 4 + k) * 512 + 512])
                    for sub in range(4):
                        w_ = []
                        if k == 0 and sub == 0:
                            w_ = [wv]
                            if si == 0:
                                w_ += [(s_ve, c_v), (s_ac, s1_v),
                                       (s_ve, ctx.cache.get(("d1ev", tt, og - 1), 0))]
                        op("tensor", lambda e, wb=wb, lhs=lhs0, sub=sub, si=si, k=k:
                           e.matmul(psB[:, sub * 512:(sub + 1) * 512],
                                    lhs[:, sub * 128:sub * 128 + 128], wb[:, k * 512:(k + 1) * 512],
                                    start=(si == 0 and k == 0), stop=False),
                           waits=w_, incs=([(s_mm, 1)] if (k == 3 and sub == 3) else []))
                si_mm[si] = s_mm.n
            for sub in range(4):
                op("tensor", lambda e, sub=sub, og=og, tt=tt:
                   e.matmul(psB[:, sub * 512:(sub + 1) * 512], w_fmb[0:6, tt * 512 + sub * 128: tt * 512 + sub * 128 + 128],
                            sb["bdown_sb"][0:6, og * 512:(og + 1) * 512], start=False, stop=False),
                   incs=[])
                op("tensor", lambda e, sub=sub, og=og:
                   e.matmul(psB[:, sub * 512:(sub + 1) * 512], ones[:, :],
                            sb["bsdown_row"][0:1, og * 512:(og + 1) * 512], start=False, stop=True),
                   incs=[(s_mm, 1)])
            down_guard = s_mm.n
            dn_v = s_mm.n
            for sub in range(4):
                op("vector", lambda e, sub=sub, og=og:
                   e.tensor_tensor(af32[:, 10240 + sub * 512: 10240 + sub * 512 + 512],
                                   psB[:, sub * 512:(sub + 1) * 512],
                                   XH[:, sub * 2048 + og * 512: sub * 2048 + og * 512 + 512], ALU.add),
                   waits=[(s_mm, dn_v), (s_x2, bv), (s_sA, s_sA.n)],
                   incs=[(s_ve, 1)])
                vv = s_ve.n
                op("sync", lambda e, sub=sub, og=og, tt=tt:
                   e.dma_start(out=out_t[tt * 4 + sub][:, og * 512:(og + 1) * 512],
                               in_=af32[:, 10240 + sub * 512: 10240 + sub * 512 + 512]),
                   waits=[(s_ve, vv)], incs=[(s_sA, 16)])
            ctx.cache[("d1ev", tt, og)] = s_ve.n



_CACHED = {}


def _host_inputs(core, I):
    g, a = core // 4, core % 4
    f32 = np.float32
    x = I["L0_attn_x"]
    d = {}
    d["x_full"] = np.ascontiguousarray(x[g]).astype(f32, copy=False)
    d["x_own"] = np.ascontiguousarray(x[g, a * TOK:(a + 1) * TOK])
    d["l0_wq"] = I["L0_attn_Wq"]
    d["l0_bq"] = I["L0_attn_bq"].reshape(1, D)
    cs = slice(4 * a * 128, 4 * a * 128 + 512)
    d["l0_wk"] = np.ascontiguousarray(I["L0_attn_Wk"][:, cs])
    d["l0_bk"] = np.ascontiguousarray(I["L0_attn_bk"][cs].reshape(4, 128).T)
    d["l0_wv"] = np.ascontiguousarray(I["L0_attn_Wv"][:, cs])
    d["l0_bv"] = I["L0_attn_bv"][cs].reshape(1, 512)
    d["l0_wo"] = I["L0_attn_Wo"]
    d["l0_bo"] = I["L0_attn_bo"].reshape(1, D)
    d["l0_su_ug"] = np.ascontiguousarray(I["L0_ffn_SU_ug"].reshape(16, 128).T)
    d["l0_su_dp"] = np.ascontiguousarray(I["L0_ffn_SU_dp"].reshape(64, 128).T)
    d["l0_ws_ug"] = np.ascontiguousarray(np.broadcast_to(I["L0_ffn_Wscale_ug"].reshape(1, 16384), (128, 16384)))
    d["l0_ws_dp"] = np.ascontiguousarray(np.broadcast_to(I["L0_ffn_Wscale_dp"].reshape(1, D), (128, D)))
    d["l0_wupT"] = np.ascontiguousarray(I["L0_ffn_W_up"].T)
    d["l0_wgateT"] = np.ascontiguousarray(I["L0_ffn_W_gate"].T)
    d["l0_wdownT"] = np.ascontiguousarray(I["L0_ffn_W_down"].T)
    qs = slice(4 * a * 128, 4 * a * 128 + 512)
    kvs = slice(2 * a * 128, 2 * a * 128 + 256)
    d["l1_wq"] = np.ascontiguousarray(I["L1_attn_Wq"].T[:, qs])
    d["l1_wk"] = np.ascontiguousarray(I["L1_attn_Wk"].T[:, kvs])
    d["l1_wv"] = np.ascontiguousarray(I["L1_attn_Wv"].T[:, kvs])
    d["l1_wo"] = np.ascontiguousarray(I["L1_attn_Wo"].T)
    d["l1_qnw"] = I["L1_attn_q_norm_w"].reshape(128, 1)
    d["l1_knw"] = I["L1_attn_k_norm_w"].reshape(128, 1)
    d["l1_wg"] = I["L1_ffn_Wg"]
    d["l1_bg"] = I["L1_ffn_bg"].reshape(1, 8)
    d["l1_wgate"] = I["L1_ffn_Wgate"]
    d["l1_bgate"] = np.ascontiguousarray(I["L1_ffn_bgate"].reshape(6, 8, 128).transpose(2, 0, 1).reshape(128, 48))
    d["l1_wup"] = I["L1_ffn_Wup"]
    d["l1_bup"] = np.ascontiguousarray(I["L1_ffn_bup"].reshape(6, 8, 128).transpose(2, 0, 1).reshape(128, 48))
    d["l1_wdown"] = I["L1_ffn_Wdown"]
    d["l1_bdown"] = I["L1_ffn_bdown"]
    d["l1_wsup"] = I["L1_ffn_Ws_up"]
    d["l1_bsup"] = np.ascontiguousarray(I["L1_ffn_bs_up"].reshape(16, 128).T)
    d["l1_wsdown"] = I["L1_ffn_Ws_down"]
    d["l1_bsdown"] = I["L1_ffn_bs_down"].reshape(1, D)
    d["ident"] = np.eye(128, dtype=f32)
    d["h128"] = _hadamard(128)
    d["h64"] = _hadamard(64)
    m0 = np.zeros((4, 128, 512), dtype=f32)
    krow = np.arange(128)[:, None]
    qcol = np.arange(512)[None, :]
    for i in range(4):
        m0[i] = np.where(i * 128 + krow <= qcol, 0.0, NEG)
    d["l0_mask"] = m0
    m1 = np.zeros((8, 128, 512), dtype=f32)
    for i in range(8):
        kc = 56 + i
        qbase = 7168 if i < 4 else 7680
        key = kc * 128 + krow
        m1[i] = np.where((key >= 7168) & (key <= qbase + qcol), NEG, 0.0)
    d["l1_mask"] = m1
    for k in d:
        if d[k].dtype != np.float32:
            d[k] = d[k].astype(np.float32)
        d[k] = np.ascontiguousarray(d[k])
    return d


def kernel(**I):
    if "nc" not in _CACHED:
        _CACHED["nc"] = build_nc(debug=False)
    nc = _CACHED["nc"]
    in_maps = [_host_inputs(c, I) for c in range(8)]
    res = run_bass_kernel_spmd(nc, in_maps, core_ids=list(range(8)))
    parts = [res.results[c]["out"] for c in range(8)]
    full = np.concatenate([np.concatenate(parts[0:4], axis=0)[None],
                           np.concatenate(parts[4:8], axis=0)[None]], axis=0)
    return full.astype(np.float32)


# version 09
# speedup vs baseline: 1.2109x; 1.0095x over previous
import numpy as np
import ml_dtypes
import concourse.bass as bass
import concourse.mybir as mybir
from concourse.bass_utils import run_bass_kernel_spmd

F32 = mybir.dt.float32
BF16 = mybir.dt.bfloat16
AF = mybir.ActivationFunctionType
ALU = mybir.AluOpType
AX = mybir.AxisListType.X

D = 2048
S = 8192
TOK = 2048
NH = 4
HD = 128
KC = D // 128
SCL = float(HD) ** -0.5
NEG = -1.0e30

ENGS = ("tensor", "vector", "scalar", "gpsimd", "sync")

BUILD_PHASES = 99


class Sem:
    __slots__ = ("h", "n", "name")

    def __init__(self, h, name):
        self.h = h
        self.n = 0
        self.name = name


class Ctx:

    def __init__(self, who, eng, sems):
        self.who = who
        self.eng = eng
        self.sems = sems
        self.cache = {}
        self.own = {"vector": sems["s_ve"], "scalar": sems["s_ac"]}

    def op(self, engine, fn, waits=(), incs=()):
        waits = list(waits)
        incs = list(incs)
        own = self.own.get(engine)
        if own is not None:
            waits.append((own, own.n))
            if not any(s is own for s, _ in incs):
                incs.append((own, 1))
        if self.who == engine:
            for s, v in waits:
                self.eng.wait_ge(s.h, v)
            inst = fn(self.eng)
            for s, n in incs:
                inst = inst.then_inc(s.h, n)
        for s, n in incs:
            s.n += n

    def wait(self, engine, waits):
        if self.who == engine:
            for s, v in waits:
                self.eng.wait_ge(s.h, v)


def _hadamard(n):
    h = np.array([[1.0]], dtype=np.float32)
    while h.shape[0] < n:
        h = np.block([[h, h], [h, -h]])
    return h



def build_nc(debug=False):
    nc = bass.Bass("TRN2", target_bir_lowering=False)

    EIN = dict(kind="ExternalInput")
    inp = {}

    def din(name, shape, dtype=F32):
        inp[name] = nc.dram_tensor(name, list(shape), dtype, **EIN)
        return inp[name]

    din("x_full", (S, D))
    din("x_own", (TOK, D))
    din("l0_wq", (D, D)); din("l0_bq", (1, D))
    din("l0_wk", (D, 512)); din("l0_bk", (128, 4))
    din("l0_wv", (D, 512)); din("l0_bv", (1, 512))
    din("l0_wo", (D, D)); din("l0_bo", (1, D))
    din("l0_su_ug", (128, 16)); din("l0_su_dp", (128, 64))
    din("l0_ws_ug", (128, 16384)); din("l0_ws_dp", (128, D))
    din("l0_wupT", (D, 8192)); din("l0_wgateT", (D, 8192)); din("l0_wdownT", (8192, D))
    din("l1_wq", (D, 512)); din("l1_wk", (D, 256)); din("l1_wv", (D, 256))
    din("l1_wo", (D, D))
    din("l1_qnw", (128, 1)); din("l1_knw", (128, 1))
    din("l1_wg", (D, 8)); din("l1_bg", (1, 8))
    din("l1_wgate", (6, D, 1024)); din("l1_bgate", (128, 48))
    din("l1_wup", (6, D, 1024)); din("l1_bup", (128, 48))
    din("l1_wdown", (6, 1024, D)); din("l1_bdown", (6, D))
    din("l1_wsup", (D, D)); din("l1_bsup", (128, 16))
    din("l1_wsdown", (D, D)); din("l1_bsdown", (1, D))
    din("ident", (128, 128)); din("h128", (128, 128)); din("h64", (64, 64))
    din("l0_mask", (4, 128, 512)); din("l1_mask", (8, 128, 512))

    out = nc.dram_tensor("out", [TOK, D], F32, kind="ExternalOutput")

    dr = {}

    def dint(name, shape, dtype):
        dr[name] = nc.dram_tensor(name, list(shape), dtype)
        return dr[name]

    dint("wupf", (D, 8192), BF16)
    dint("wgatef", (D, 8192), BF16)
    dint("wdtmp", (8192, D), BF16)
    dint("wdownf", (8192, D), BF16)
    dint("q0d", (TOK, D), BF16)
    dint("xn0d", (D, TOK), BF16)
    dint("k0d", (NH * 128, S), BF16)
    dint("v0d", (NH * 128, S), BF16)
    dint("attn_h0a", (S, 256), BF16)
    dint("attn_h0b", (S, 256), BF16)
    dint("ag0a", (4 * S, 256), BF16)
    dint("ag0b", (4 * S, 256), BF16)
    dint("attn_own0", (TOK, D), BF16)
    dint("attn_own1", (TOK, D), BF16)
    dint("base0", (TOK, D), F32)
    dint("x1d", (TOK, D), F32)
    dint("x1ag", (4 * TOK, D), F32)
    dint("xn1d", (D, TOK), BF16)
    dint("q1d", (NH * 128, S), BF16)
    dint("k1d", (2 * 128, S), BF16)
    dint("v1d", (2 * 128, S), BF16)
    dint("attn_h1a", (S, 256), BF16)
    dint("attn_h1b", (S, 256), BF16)
    dint("ag1a", (4 * S, 256), BF16)
    dint("ag1b", (4 * S, 256), BF16)
    dint("base1", (TOK, D), F32)
    dint("wgate_b", (6, D, 1024), BF16)
    dint("wup_b", (6, D, 1024), BF16)
    dint("wdown_b", (6, 1024, D), BF16)
    dint("wsup_b", (D, D), BF16)
    dint("wsdown_b", (D, D), BF16)
    dint("zscr", (2, 512), F32)

    dbg = {}
    if debug:
        for name, shape, dt in [
            ("dbg_wupf", (D, 8192), BF16), ("dbg_wdownf", (8192, D), BF16),
            ("dbg_q0", (TOK, D), BF16), ("dbg_k0", (NH * 128, S), BF16),
            ("dbg_v0", (NH * 128, S), BF16),
            ("dbg_base0", (TOK, D), F32),
            ("dbg_x1", (TOK, D), F32),
            ("dbg_base1", (TOK, D), F32),
        ]:
            dbg[name] = nc.dram_tensor(name, list(shape), dt, kind="ExternalOutput")

    import contextlib
    with contextlib.ExitStack() as stk:
        ent = stk.enter_context

        sb = {}
        def sbuf(name, shape, dtype):
            sb[name] = ent(nc.sbuf_tensor(name, list(shape), dtype))
            return sb[name]

        sbuf("ident_bf", (128, 128), BF16)
        sbuf("h128_bf", (128, 128), BF16)
        sbuf("h64_bf", (64, 64), BF16)
        sbuf("ones_col", (1, 128), BF16)
        sbuf("ones128", (128, 1), BF16)
        sbuf("su_ug", (128, 16), F32)
        sbuf("su_dp", (128, 64), F32)
        sbuf("bk_sb", (128, 4), F32)
        sbuf("bq_row", (1, D), BF16)
        sbuf("bv_row", (1, 512), BF16)
        sbuf("bo_row", (1, D), BF16)
        sbuf("qnw", (128, 1), F32)
        sbuf("knw", (128, 1), F32)
        sbuf("bg_row", (1, 8), BF16)
        sbuf("bgate_sb", (128, 48), F32)
        sbuf("bup_sb", (128, 48), F32)
        sbuf("bdown_sb", (6, D), BF16)
        sbuf("bsup_sb", (128, 16), F32)
        sbuf("bsdown_row", (1, D), BF16)
        sbuf("masks0", (128, 4 * 512), BF16)
        sbuf("masks1", (128, 8 * 512), BF16)

        sbuf("cvals", (128, 4), F32)
        nc.const_aps.register(F32, 1e-5, sb["cvals"][:, 0:1])
        nc.const_aps.register(F32, 1e-6, sb["cvals"][:, 1:2])
        nc.const_aps.register(F32, SCL, sb["cvals"][:, 2:3])
        nc.const_aps.register(F32, 1.0 / HD, sb["cvals"][:, 3:4])

        sbuf("abf", (128, 52224), BF16)
        sbuf("af32", (128, 12288), F32)
        sbuf("stat", (128, 256), F32)
        sbuf("w_fmb", (8, TOK), BF16)
        sbuf("w_row", (1, 6 * 512), BF16)

        psA = ent(nc.psum_tensor("psA", [128, 2048], F32))
        psB = ent(nc.psum_tensor("psB", [128, 2048], F32))

        sems = {}
        for name in ("s_wl", "s_al", "s_st", "s_gst", "s_mm", "s_ve", "s_ac", "s_cc",
                     "s_kv", "s_qA", "s_qB", "s_x2", "s_wA", "s_wB", "s_gA", "s_gB",
                     "s_gb", "s_sA", "s_sB", "s_sC", "s_xA", "s_xB", "s_xC", "s_xD"):
            sems[name] = Sem(ent(nc.semaphore(name)), name)

        block = ent(nc.Block())

        def prog(ctx):
            emit_program(ctx, nc, inp, out, dr, sb, psA, psB, sems, dbg)

        block.tensor(lambda e: prog(Ctx("tensor", e, sems)))
        block.vector(lambda e: prog(Ctx("vector", e, sems)))
        block.scalar(lambda e: prog(Ctx("scalar", e, sems)))
        block.gpsimd(lambda e: prog(Ctx("gpsimd", e, sems)))
        block.sync(lambda e: prog(Ctx("sync", e, sems)))

    return nc



def emit_program(ctx, nc, inp, out, dr, sb, psA, psB, sems, dbg):
    for s in sems.values():
        s.n = 0
    s_wl = sems["s_wl"]
    s_al = sems["s_al"]
    s_st = sems["s_st"]
    s_mm = sems["s_mm"]
    s_ve = sems["s_ve"]
    s_ac = sems["s_ac"]
    s_cc = sems["s_cc"]

    op = ctx.op

    def ld(engine, sem, dst, src):
        op(engine, lambda e, d=dst, s=src: e.dma_start(out=d, in_=s), incs=[(sem, 16)])

    G = "gpsimd"; SY = "sync"
    ld(G, s_wl, sb["ident_bf"][:, :], inp["ident"][:, :])
    ld(G, s_wl, sb["h128_bf"][:, :], inp["h128"][:, :])
    ld(G, s_wl, sb["h64_bf"][:, :], inp["h64"][:, :])
    ld(G, s_wl, sb["su_ug"][:, :], inp["l0_su_ug"][:, :])
    ld(G, s_wl, sb["su_dp"][:, :], inp["l0_su_dp"][:, :])
    ld(G, s_wl, sb["bk_sb"][:, :], inp["l0_bk"][:, :])
    ld(G, s_wl, sb["bq_row"][:, :], inp["l0_bq"][:, :])
    ld(G, s_wl, sb["bv_row"][:, :], inp["l0_bv"][:, :])
    ld(G, s_wl, sb["bo_row"][:, :], inp["l0_bo"][:, :])
    ld(G, s_wl, sb["qnw"][:, :], inp["l1_qnw"][:, :])
    ld(G, s_wl, sb["knw"][:, :], inp["l1_knw"][:, :])
    ld(G, s_wl, sb["bg_row"][:, :], inp["l1_bg"][:, :])
    ld(G, s_wl, sb["bgate_sb"][:, :], inp["l1_bgate"][:, :])
    ld(G, s_wl, sb["bup_sb"][:, :], inp["l1_bup"][:, :])
    ld(G, s_wl, sb["bdown_sb"][:, :], inp["l1_bdown"][:, :])
    ld(G, s_wl, sb["bsup_sb"][:, :], inp["l1_bsup"][:, :])
    ld(G, s_wl, sb["bsdown_row"][:, :], inp["l1_bsdown"][:, :])
    m0 = inp["l0_mask"].ap().rearrange("n p f -> p n f")
    ld(G, s_wl, sb["masks0"][:, :].rearrange("p (n f) -> p n f", n=4), m0)
    m1 = inp["l1_mask"].ap().rearrange("n p f -> p n f")
    ld(G, s_wl, sb["masks1"][:, :].rearrange("p (n f) -> p n f", n=8), m1)
    consts_done = s_wl.n
    op("vector", lambda e: e.memset(sb["ones_col"][:, :], 1.0), incs=[(s_ve, 1)])
    op("vector", lambda e: e.memset(sb["ones128"][:, :], 1.0), incs=[(s_ve, 1)])
    for i, v in enumerate([1e-5, 1e-6, SCL, 1.0 / HD]):
        op("vector", lambda e, i=i, v=v: e.memset(sb["cvals"][:, i:i + 1], v), incs=[(s_ve, 1)])

    if BUILD_PHASES >= 1:
        emit_phaseW(ctx, nc, inp, dr, sb, psA, psB, sems, consts_done, dbg)
    if BUILD_PHASES >= 2:
        emit_layer0(ctx, nc, inp, out, dr, sb, psA, psB, sems, dbg)
    if BUILD_PHASES >= 3:
        emit_layer1(ctx, nc, inp, out, dr, sb, psA, psB, sems, dbg)

    ctx.wait(SY, [(sems["s_st"], sems["s_st"].n), (sems["s_sA"], sems["s_sA"].n),
                  (sems["s_sB"], sems["s_sB"].n), (sems["s_sC"], sems["s_sC"].n)])
    ctx.wait(G, [(sems["s_gst"], sems["s_gst"].n), (sems["s_gb"], sems["s_gb"].n)])



def emit_phaseW(ctx, nc, inp, dr, sb, psA, psB, sems, consts_done, dbg):
    op = ctx.op
    s_wl, s_al, s_st, s_gst, s_mm, s_ve = (sems[k] for k in ("s_wl", "s_al", "s_st", "s_gst", "s_mm", "s_ve"))
    s_wA, s_wB, s_gA, s_gB = (sems[k] for k in ("s_wA", "s_wB", "s_gA", "s_gB"))
    s_sA, s_sB, s_ac = sems["s_sA"], sems["s_sB"], sems["s_ac"]
    abf, af32 = sb["abf"], sb["af32"]
    h128, h64 = sb["h128_bf"], sb["h64_bf"]
    ws_bc = af32[:, 0:8192]
    wsd_bc = af32[:, 8192:10240]

    op("sync", lambda e: e.dma_start(out=wsd_bc, in_=inp["l0_ws_dp"][:, :]), incs=[(s_al, 16)])
    op("vector", lambda e: e.tensor_scalar_mul(wsd_bc, wsd_bc, 2.0 ** -6.5),
       waits=[(s_al, s_al.n)], incs=[(s_ve, 1)])
    wsd_ready = s_ve.n

    for wname, dst in (("l0_wupT", "wupf"), ("l0_wgateT", "wgatef")):
        wsrc = inp[wname].ap().rearrange("(r c) h -> c r h", c=128)
        wdst = dr[dst].ap().rearrange("(r c) h -> c r h", c=128)
        base_off = 0 if wname == "l0_wupT" else 8192
        op("sync", lambda e, bo=base_off:
           e.dma_start(out=ws_bc, in_=inp["l0_ws_ug"][:, bo:bo + 8192]),
           waits=[(s_ve, s_ve.n)], incs=[(s_al, 16)])
        op("vector", lambda e: e.tensor_scalar_mul(ws_bc, ws_bc, 2.0 ** -5.5),
           waits=[(s_al, s_al.n)], incs=[(s_ve, 1)])
        ws_ready = s_ve.n
        for hc4 in range(16):
            h0 = hc4 * 512
            lbuf = abf[:, (hc4 % 2) * 8192:(hc4 % 2) * 8192 + 16 * 512]
            s_g = [s_gA, s_gB][hc4 % 2]
            op("gpsimd", lambda e, lb=lbuf, w=wsrc, h0=h0:
               e.dma_start(out=lb.rearrange("p (r h) -> p r h", r=16), in_=w[:, :, h0:h0 + 512]),
               waits=[(s_mm, ctx.cache.get(("wld1", (hc4 - 2) % 16), 0))], incs=[(s_g, 16)])
            lv = (s_g, s_g.n)
            for r in range(16):
                bank = (r % 2) * 512
                op("tensor", lambda e, lb=lbuf, r=r, bank=bank:
                   e.matmul(psA[:, bank:bank + 512], h128[:, :], lb[:, r * 512:(r + 1) * 512],
                            start=True, stop=True),
                   waits=[lv, (s_wl, consts_done),
                          (s_ve, ctx.cache.get(("wfe", (r - 2) % 16), 0))], incs=[(s_mm, 1)])
                mmv = s_mm.n
                ob = abf[:, 16384 + (r % 2) * 512: 16384 + (r % 2) * 512 + 512]
                s_s = [s_sA, s_sB][r % 2]
                op("vector", lambda e, ob=ob, bank=bank, h0=h0:
                   e.tensor_tensor(ob, psA[:, bank:bank + 512], ws_bc[:, h0:h0 + 512], ALU.mult),
                   waits=[(s_mm, mmv), (s_ve, ws_ready),
                          (s_s, ctx.cache.get(("wst", (r - 2) % 16), 0))], incs=[(s_ve, 1)])
                ctx.cache[("wfe", r)] = s_ve.n
                vv = s_ve.n
                op("sync", lambda e, ob=ob, w=wdst, r=r, h0=h0:
                   e.dma_start(out=w[:, r, h0:h0 + 512], in_=ob),
                   waits=[(s_ve, vv)], incs=[(s_s, 16)])
                ctx.cache[("wst", r)] = s_s.n
            ctx.cache[("wld1", hc4)] = s_mm.n

    ctx.cache[("wfe2", 30)] = (s_ve, s_ve.n)
    ctx.cache[("wfe2", 31)] = (s_ve, s_ve.n)
    ctx.cache[("wld2x",)] = s_mm.n
    wsrc = inp["l0_wdownT"].ap().rearrange("(r c) o -> c r o", c=128)
    wdst = dr["wdtmp"].ap().rearrange("(r c) o -> c r o", c=128)
    for oc in range(4):
        o0 = oc * 512
        for hh in range(2):
            lbuf = abf[:, ((oc * 2 + hh) % 2) * 16384:((oc * 2 + hh) % 2) * 16384 + 32 * 512]
            s_g = [s_gA, s_gB][(oc * 2 + hh) % 2]
            it = oc * 2 + hh
            op("gpsimd", lambda e, lb=lbuf, o0=o0, hh=hh:
               e.dma_start(out=lb.rearrange("p (r o) -> p r o", r=32),
                           in_=wsrc[:, hh * 32:(hh + 1) * 32, o0:o0 + 512]),
               waits=[(s_mm, ctx.cache.get(("wld2", it - 2), ctx.cache.get(("wld2x",), 0)))] +
                     ([(s_sA, s_sA.n), (s_sB, s_sB.n)] if it < 2 else []),
               incs=[(s_g, 16)])
            lv = (s_g, s_g.n)
            for rr in range(32):
                r = hh * 32 + rr
                bank = (rr % 2) * 512
                op("tensor", lambda e, lb=lbuf, rr=rr, bank=bank:
                   e.matmul(psA[:, bank:bank + 512], h128[:, :], lb[:, rr * 512:(rr + 1) * 512],
                            start=True, stop=True),
                   waits=[lv, (s_wl, consts_done),
                          ctx.cache.get(("wfe2", (rr - 2) % 32), (s_ve, 0))], incs=[(s_mm, 1)])
                mmv = s_mm.n
                ob = abf[:, 33280 + (rr % 2) * 512: 33280 + (rr % 2) * 512 + 512]
                s_s = [s_sA, s_sB][rr % 2]
                op("scalar", lambda e, ob=ob, bank=bank: e.activation(ob, psA[:, bank:bank + 512], AF.Copy),
                   waits=[(s_mm, mmv), (s_s, ctx.cache.get(("wst2", (rr - 2) % 32), 0))],
                   incs=[(s_ac, 1)])
                ctx.cache[("wfe2", rr)] = (s_ac, s_ac.n)
                vv = (s_ac, s_ac.n)
                op("sync", lambda e, ob=ob, r=r, o0=o0:
                   e.dma_start(out=wdst[:, r, o0:o0 + 512], in_=ob),
                   waits=[vv], incs=[(s_s, 16)])
                ctx.cache[("wst2", rr)] = s_s.n
            ctx.cache[("wld2", it)] = s_mm.n
    f1_done_sA, f1_done_sB = s_sA.n, s_sB.n

    w2 = dr["wdtmp"].ap().rearrange("(r c) o -> r c o", c=128)
    w2dst = dr["wdownf"].ap().rearrange("(r c) o -> c r o", c=128)
    for c in range(128):
        lbuf = abf[0:64, (c % 2) * 2048:(c % 2) * 2048 + 2048]
        s_w = [s_wA, s_wB][c % 2]
        op("sync", lambda e, lb=lbuf, c=c: e.dma_start(out=lb, in_=w2[:, c, :]),
           waits=[(s_sA, f1_done_sA), (s_sB, f1_done_sB),
                  (s_mm, ctx.cache.get(("wld3", c - 2), 0))], incs=[(s_w, 16)])
        lv = (s_w, s_w.n)
        for oc in range(4):
            bank = 1024 + (oc % 2) * 512
            op("tensor", lambda e, lb=lbuf, oc=oc, bank=bank:
               e.matmul(psA[0:64, bank:bank + 512], h64[:, :], lb[:, oc * 512:(oc + 1) * 512],
                        start=True, stop=True),
               waits=[lv, (s_wl, consts_done),
                      (s_ve, ctx.cache.get(("wfe3", c, oc - 2),
                                           ctx.cache.get(("wfe3", c - 1, oc + 2), 0)))],
               incs=[(s_mm, 1)])
            mmv = s_mm.n
            ob = abf[0:64, 4096 + (oc % 2) * 512: 4096 + (oc % 2) * 512 + 512]
            s_s2 = [s_gst, sems["s_gb"]][oc % 2]
            op("vector", lambda e, ob=ob, bank=bank, oc=oc:
               e.tensor_tensor(ob, psA[0:64, bank:bank + 512], wsd_bc[0:64, oc * 512:(oc + 1) * 512], ALU.mult),
               waits=[(s_mm, mmv), (s_ve, wsd_ready),
                      (s_s2, ctx.cache.get(("wst3", c, oc - 2),
                                           ctx.cache.get(("wst3", c - 1, oc + 2), 0)))],
               incs=[(s_ve, 1)])
            ctx.cache[("wfe3", c, oc)] = s_ve.n
            vv = s_ve.n
            op("gpsimd", lambda e, ob=ob, c=c, oc=oc:
               e.dma_start(out=w2dst[c, :, oc * 512:(oc + 1) * 512], in_=ob),
               waits=[(s_ve, vv)], incs=[(s_s2, 16)])
            ctx.cache[("wst3", c, oc)] = s_s2.n
        ctx.cache[("wld3", c)] = s_mm.n

    for nm, dst in (("l1_wgate", "wgate_b"), ("l1_wup", "wup_b"), ("l1_wdown", "wdown_b"),
                    ("l1_wsup", "wsup_b"), ("l1_wsdown", "wsdown_b")):
        srcap = inp[nm].ap()
        dstap = dr[dst].ap()
        if len(srcap.shape) == 3:
            for n in range(srcap.shape[0]):
                op("gpsimd", lambda e, srcap=srcap, dstap=dstap, n=n:
                   e.dma_start(out=dstap[n], in_=srcap[n]), incs=[(s_gst, 16)])
        else:
            op("gpsimd", lambda e, srcap=srcap, dstap=dstap:
               e.dma_start(out=dstap[:, :], in_=srcap[:, :]), incs=[(s_gst, 16)])
    if dbg:
        wbar = [(s_st, s_st.n), (s_gst, s_gst.n), (sems["s_gb"], sems["s_gb"].n),
                (s_sA, s_sA.n), (s_sB, s_sB.n)]
        op("sync", lambda e: e.dma_start(out=dbg["dbg_wupf"][:, :], in_=dr["wupf"][:, :]),
           waits=wbar, incs=[(s_st, 16)])
        op("sync", lambda e: e.dma_start(out=dbg["dbg_wdownf"][:, :], in_=dr["wdownf"][:, :]),
           waits=wbar, incs=[(s_st, 16)])


def emit_ln(ctx, sems, sb, xf32, sq_scr, stat, xn_bf, cb=0):
    op = ctx.op
    s_ve, s_ac = sems["s_ve"], sems["s_ac"]
    op("vector", lambda e: e.reduce_sum(stat[:, cb:cb + 1], xf32, axis=AX), incs=[(s_ve, 1)])
    op("scalar", lambda e: e.activation(sq_scr, xf32, AF.Square, accum_out=stat[:, cb + 1:cb + 2]),
       incs=[(s_ac, 1)])
    sqv = s_ac.n
    op("vector", lambda e: e.tensor_scalar_mul(stat[:, cb + 2:cb + 3], stat[:, cb:cb + 1], 1.0 / D),
       incs=[(s_ve, 1)])
    op("vector", lambda e: e.tensor_mul(stat[:, cb + 3:cb + 4], stat[:, cb + 2:cb + 3], stat[:, cb + 2:cb + 3]),
       incs=[(s_ve, 1)])
    op("vector", lambda e: e.scalar_tensor_tensor(stat[:, cb + 4:cb + 5], stat[:, cb + 1:cb + 2], 1.0 / D,
                                                  stat[:, cb + 3:cb + 4], ALU.mult, ALU.subtract),
       waits=[(s_ac, sqv)], incs=[(s_ve, 1)])
    vv = s_ve.n
    op("scalar", lambda e: e.activation(stat[:, cb + 5:cb + 6], stat[:, cb + 4:cb + 5], AF.Ln, bias=1e-5),
       waits=[(s_ve, vv)], incs=[(s_ac, 1)])
    op("scalar", lambda e: e.activation(stat[:, cb + 6:cb + 7], stat[:, cb + 5:cb + 6], AF.Exp, scale=-0.5),
       incs=[(s_ac, 1)])
    sqv2 = s_ac.n
    op("vector", lambda e: e.tensor_mul(stat[:, cb + 7:cb + 8], stat[:, cb + 2:cb + 3], stat[:, cb + 6:cb + 7]),
       waits=[(s_ac, sqv2)], incs=[(s_ve, 1)])
    op("vector", lambda e: e.tensor_scalar_mul(stat[:, cb + 8:cb + 9], stat[:, cb + 7:cb + 8], -1.0),
       incs=[(s_ve, 1)])
    vv = s_ve.n
    op("scalar", lambda e: e.activation(xn_bf, xf32, AF.Identity, scale=stat[:, cb + 6:cb + 7],
                                        bias=stat[:, cb + 8:cb + 9]),
       waits=[(s_ve, vv)], incs=[(s_ac, 1)])
    return s_ve.n, s_ac.n


def emit_layer0(ctx, nc, inp, out, dr, sb, psA, psB, sems, dbg):
    op = ctx.op
    s_wl, s_al, s_st, s_gst, s_mm, s_ve, s_ac, s_cc = (
        sems[k] for k in ("s_wl", "s_al", "s_st", "s_gst", "s_mm", "s_ve", "s_ac", "s_cc"))
    s_sA, s_sB = sems["s_sA"], sems["s_sB"]
    abf, af32, stat = sb["abf"], sb["af32"], sb["stat"]
    ident, ones = sb["ident_bf"], sb["ones_col"]

    wq_src = inp["l0_wq"].ap().rearrange("(k p) o -> p k o", p=128)
    x_own_t = inp["x_own"].ap().rearrange("(n p) d -> n p d", p=128)
    x_full_t = inp["x_full"].ap().rearrange("(n p) d -> n p d", p=128)
    xn0_v = dr["xn0d"].ap().rearrange("(k p) t -> p k t", p=128)
    q0_v = dr["q0d"].ap()

    phase_barrier = [(s_st, s_st.n), (s_gst, s_gst.n), (s_ve, s_ve.n), (s_mm, s_mm.n)]
    XT = [af32[:, 0:2048], af32[:, 2048:4096], af32[:, 4096:6144], af32[:, 6144:8192]]
    SQp = [af32[:, 8192:10240]] * 4
    XNT = [abf[:, 0:2048], abf[:, 2048:4096]]
    XNF = abf[:, 4096:36864]

    def xnf_tile(kc, t0, w=128):
        return XNF[:, kc * 2048 + t0: kc * 2048 + t0 + w]

    ln_acv = 0
    for ti in range(16):
        xt = XT[ti % 4]
        s_x = [sems["s_xA"], sems["s_xB"], sems["s_xC"], sems["s_xD"]][ti % 4]
        op("sync", lambda e, xt=xt, ti=ti: e.dma_start(out=xt, in_=x_own_t[ti]),
           waits=phase_barrier if ti == 0 else [(s_ac, ctx.cache.get(("a0ln", ti - 4), 0))],
           incs=[(s_x, 16)])
        lv = s_x.n
        ctx.wait("vector", [(s_x, lv)])
        ctx.wait("scalar", [(s_x, lv)])
        if ti >= 2:
            ctx.wait("scalar", [(s_mm, ctx.cache.get(("a0tr", ti - 2), 0))])
        _, ln_acv = emit_ln(ctx, sems, sb, xt, SQp[ti % 4], stat, XNT[ti % 2], cb=(ti % 4) * 10)
        ctx.cache[("a0ln", ti)] = ln_acv
        acv = s_ac.n
        for kc in range(16):
            op("tensor", lambda e, ti=ti, kc=kc:
               e.matmul(psB[:, kc * 128:(kc + 1) * 128], XNT[ti % 2][:, kc * 128:(kc + 1) * 128],
                        ident[:, :], start=True, stop=True),
               waits=([(s_ac, acv), (s_ve, ctx.cache.get(("a0ev", ti - 1), 0))] if kc == 0 else []),
               incs=([(s_mm, 1)] if kc == 15 else []))
        ctx.cache[("a0tr", ti)] = s_mm.n
        mmv = s_mm.n
        dst = XNF[:, :].rearrange("p (k t) -> p k t", k=16)[:, :, ti * 128:(ti + 1) * 128]
        op("vector", lambda e, dst=dst:
           e.tensor_copy(dst, psB[:, :].rearrange("p (k t) -> p k t", k=16)),
           waits=[(s_mm, mmv)], incs=[(s_ve, 1)])
        ctx.cache[("a0ev", ti)] = s_ve.n
    xnf_ready = s_ve.n
    op("sync", lambda e: e.dma_start(out=xn0_v[:, :, :],
                                     in_=XNF[:, :].rearrange("p (k t) -> p k t", k=16)),
       waits=[(s_ve, xnf_ready)], incs=[(s_st, 16)])

    WQB = abf[:, 36864:45056]
    qev = 0
    for og in range(4):
        op("gpsimd", lambda e, og=og:
           e.dma_start(out=WQB.rearrange("p (k o) -> p k o", k=16),
                       in_=wq_src[:, :, og * 512:(og + 1) * 512]),
           waits=[(s_mm, s_mm.n)], incs=[(s_wl, 16)])
        wv = s_wl.n
        for ti in range(16):
            bank = (ti % 2) * 512
            for kc in range(16):
                op("tensor", lambda e, ti=ti, kc=kc, bank=bank:
                   e.matmul(psA[:, bank:bank + 512], xnf_tile(kc, ti * 128),
                            WQB[:, kc * 512:(kc + 1) * 512],
                            start=(kc == 0), stop=False),
                   waits=([(s_wl, wv), (s_ve, xnf_ready), (s_ac, qev)] if kc == 0 else []),
                   incs=[])
            op("tensor", lambda e, og=og, bank=bank:
               e.matmul(psA[:, bank:bank + 512], ones[:, :],
                        sb["bq_row"][0:1, og * 512:(og + 1) * 512], start=False, stop=True),
               incs=[(s_mm, 1)])
            mmv = s_mm.n
            qst = abf[:, 45056 + (ti % 2) * 512: 45056 + (ti % 2) * 512 + 512]
            op("scalar", lambda e, qst=qst, bank=bank: e.activation(qst, psA[:, bank:bank + 512], AF.Copy),
               waits=[(s_mm, mmv), (s_sA, s_sA.n)], incs=[(s_ac, 1)])
            qev = s_ac.n
            op("sync", lambda e, qst=qst, ti=ti, og=og:
               e.dma_start(out=q0_v[ti * 128:(ti + 1) * 128, og * 512:(og + 1) * 512], in_=qst),
               waits=[(s_ac, qev)], incs=[(s_sA, 16)])
    a0_done = (s_st.n, s_gst.n)

    WKB = abf[:, 20480:28672]
    WVB = abf[:, 28672:36864]
    op("gpsimd", lambda e: e.dma_start(out=WKB.rearrange("p (k o) -> p k o", k=16),
                                       in_=inp["l0_wk"].ap().rearrange("(k p) o -> p k o", p=128)),
       waits=[(s_mm, s_mm.n), (s_st, s_st.n), (s_sA, s_sA.n)], incs=[(s_wl, 16)])
    op("gpsimd", lambda e: e.dma_start(out=WVB.rearrange("p (k o) -> p k o", k=16),
                                       in_=inp["l0_wv"].ap().rearrange("(k p) o -> p k o", p=128)),
       incs=[(s_wl, 16)])
    wkv_v = s_wl.n
    k0_v = dr["k0d"].ap()
    v0_v = dr["v0d"].ap().rearrange("(h p) s -> p h s", h=NH)
    ln_acv2 = 0
    kst_ve = {}
    for mac in range(16):
        XF = abf[:, 4096 + (mac % 2) * 8192: 4096 + (mac % 2) * 8192 + 8192]
        for sub in range(4):
            ti = mac * 4 + sub
            xt = XT[ti % 4]
            s_x = [sems["s_xA"], sems["s_xB"], sems["s_xC"], sems["s_xD"]][ti % 4]
            op("sync", lambda e, xt=xt, ti=ti: e.dma_start(out=xt, in_=x_full_t[ti]),
               waits=[(s_ac, ctx.cache.get(("a1ln", ti - 4), 0))] if ti else
                     [(s_ac, s_ac.n), (s_ve, s_ve.n)],
               incs=[(s_x, 16)])
            lv = s_x.n
            ctx.wait("vector", [(s_x, lv)])
            ctx.wait("scalar", [(s_x, lv)])
            if ti >= 2:
                ctx.wait("scalar", [(s_mm, ctx.cache.get(("a1tr", ti - 2), 0))])
            _, ln_acv2 = emit_ln(ctx, sems, sb, xt, SQp[ti % 4], stat, XNT[ti % 2], cb=(ti % 4) * 10)
            ctx.cache[("a1ln", ti)] = ln_acv2
            acv = s_ac.n
            for kc in range(16):
                op("tensor", lambda e, ti=ti, kc=kc:
                   e.matmul(psB[:, kc * 128:(kc + 1) * 128], XNT[ti % 2][:, kc * 128:(kc + 1) * 128],
                            ident[:, :], start=True, stop=True),
                   waits=([(s_ac, acv), (s_ve, ctx.cache.get(("a1ev", ti - 2), 0))] if kc == 0 else []),
                   incs=([(s_mm, 1)] if kc == 15 else []))
            ctx.cache[("a1tr", ti)] = s_mm.n
            mmv = s_mm.n
            dst = XF.rearrange("p (k t) -> p k t", k=16)[:, :, sub * 128:(sub + 1) * 128]
            op("vector", lambda e, dst=dst:
               e.tensor_copy(dst, psB[:, :].rearrange("p (k t) -> p k t", k=16)),
               waits=([(s_mm, mmv)] + ([(s_st, s_st.n), (s_sA, s_sA.n)] if ti == 0 else [])),
               incs=[(s_ve, 1)])
            ctx.cache[("a1ev", ti)] = s_ve.n
        xf_ready = s_ve.n

        for h in range(NH):
            bank = (h % 2) * 512
            for kc in range(16):
                op("tensor", lambda e, h=h, kc=kc, bank=bank, XF=XF:
                   e.matmul(psA[:, bank:bank + 512], WKB[:, kc * 512 + h * 128: kc * 512 + h * 128 + 128],
                            XF[:, kc * 512:(kc + 1) * 512], start=(kc == 0), stop=(kc == 15)),
                   waits=([(s_wl, wkv_v), (s_ve, xf_ready), (s_ac, kst_ve.get(("k", h % 2), 0))]
                          if kc == 0 else []),
                   incs=([(s_mm, 1)] if kc == 15 else []))
            mmv = s_mm.n
            kst = abf[:, 36864 + (h % 2) * 512: 36864 + (h % 2) * 512 + 512]
            op("scalar", lambda e, kst=kst, bank=bank, h=h:
               e.activation(kst, psA[:, bank:bank + 512], AF.Identity, bias=sb["bk_sb"][:, h:h + 1]),
               waits=[(s_mm, mmv), (s_sA, s_sA.n)], incs=[(s_ac, 1)])
            kst_ve[("k", h % 2)] = s_ac.n
            acv = s_ac.n
            op("sync", lambda e, kst=kst, h=h, mac=mac:
               e.dma_start(out=k0_v[h * 128:(h + 1) * 128, mac * 512:(mac + 1) * 512], in_=kst),
               waits=[(s_ac, acv)], incs=[(s_sA, 16)])
        for sub in range(4):
            bank = 1024 + (sub % 2) * 512
            for kc in range(16):
                op("tensor", lambda e, sub=sub, kc=kc, bank=bank, XF=XF:
                   e.matmul(psA[:, bank:bank + 512], XF[:, kc * 512 + sub * 128: kc * 512 + sub * 128 + 128],
                            WVB[:, kc * 512:(kc + 1) * 512], start=(kc == 0), stop=False),
                   waits=([(s_ve, xf_ready), (s_ac, kst_ve.get(("v", sub % 2), 0))] if kc == 0 else []),
                   incs=[])
            op("tensor", lambda e, bank=bank:
               e.matmul(psA[:, bank:bank + 512], ones[:, :], sb["bv_row"][0:1, :], start=False, stop=True),
               incs=[(s_mm, 1)])
            mmv = s_mm.n
            vst = abf[:, 37888 + (sub % 2) * 512: 37888 + (sub % 2) * 512 + 512]
            op("scalar", lambda e, vst=vst, bank=bank:
               e.activation(vst, psA[:, bank:bank + 512], AF.Copy),
               waits=[(s_mm, mmv), (s_sB, s_sB.n)], incs=[(s_ac, 1)])
            kst_ve[("v", sub % 2)] = s_ac.n
            acv = s_ac.n
            n = mac * 4 + sub
            op("sync", lambda e, vst=vst, n=n:
               e.dma_start(out=v0_v[:, :, n * 128:(n + 1) * 128],
                           in_=vst.rearrange("p (h j) -> p h j", h=4)),
               waits=[(s_ac, acv)], incs=[(s_sB, 16)])
    a1_done = s_st.n

    emit_attention(ctx, nc, sems, sb, psA, psB, layer=0,
                   kd=dr["k0d"], vd=dr["v0d"], qd=dr["q0d"],
                   attn_out=(dr["attn_h0a"], dr["attn_h0b"]),
                   proj_done=s_st.n, zscr=dr["zscr"])

    barrier = [(s_st, s_st.n), (s_gst, s_gst.n), (s_sA, s_sA.n), (s_sB, s_sB.n)]
    op("gpsimd", lambda e: e.collective_compute(
        "AllGather", ALU.bypass, replica_groups=[[0, 1, 2, 3], [4, 5, 6, 7]],
        ins=[dr["attn_h0a"].ap().opt()], outs=[dr["ag0a"].ap().opt()]),
       waits=[(s_sA, ctx.cache[("hdone", 0, 1)])], incs=[(s_cc, 1)])
    op("gpsimd", lambda e: e.collective_compute(
        "AllGather", ALU.bypass, replica_groups=[[0, 1, 2, 3], [4, 5, 6, 7]],
        ins=[dr["attn_h0b"].ap().opt()], outs=[dr["ag0b"].ap().opt()]),
       waits=[(s_sB, ctx.cache[("hdone", 0, 3)])], incs=[(s_cc, 1)])
    ccv = s_cc.n

    if dbg:
        for nm, srcn in (("dbg_q0", "q0d"), ("dbg_k0", "k0d"), ("dbg_v0", "v0d")):
            op("sync", lambda e, nm=nm, srcn=srcn: e.dma_start(out=dbg[nm][:, :], in_=dr[srcn][:, :]),
               waits=barrier, incs=[(s_st, 16)])

    emit_wo_base(ctx, nc, sems, sb, psA, psB, ag=(dr["ag0a"], dr["ag0b"]), w_in=inp["l0_wo"],
                 bias_row=sb["bo_row"], x_res=inp["x_own"], base=dr["base0"], cc_wait=ccv,
                 own_scr=dr["attn_own0"])
    if dbg:
        op("sync", lambda e: e.dma_start(out=dbg["dbg_base0"][:, :], in_=dr["base0"][:, :]),
           waits=[(s_st, s_st.n), (s_sB, s_sB.n), (s_ve, s_ve.n)], incs=[(s_st, 16)])

    emit_ffn0(ctx, nc, sems, sb, psA, psB, dr, inp)
    if dbg:
        op("sync", lambda e: e.dma_start(out=dbg["dbg_x1"][:, :], in_=dr["x1d"][:, :]),
           waits=[(s_st, s_st.n), (s_sA, s_sA.n), (s_gst, s_gst.n)], incs=[(s_st, 16)])


def emit_attention(ctx, nc, sems, sb, psA, psB, layer, kd, vd, qd, attn_out, proj_done,
                   zscr):
    op = ctx.op
    s_al, s_st, s_mm, s_ve, s_ac = (sems[k] for k in ("s_al", "s_st", "s_mm", "s_ve", "s_ac"))
    s_kv, s_qA, s_qB, s_sA, s_sB = (sems[k] for k in ("s_kv", "s_qA", "s_qB", "s_sA", "s_sB"))
    s_gst, s_xA, s_xB = sems["s_gst"], sems["s_xA"], sems["s_xB"]
    abf, af32, stat = sb["abf"], sb["af32"], sb["stat"]
    ident = sb["ident_bf"]
    onesr, onesc = sb["ones_col"], sb["ones128"]
    KF = abf[:, 0:8192]
    VT = abf[:, 8192:16384]
    KSUB = abf[:, 16384:17408]
    QT = abf[:, 17408:25600]
    PRB = [abf[:, 25600:26624], abf[:, 26624:27648]]
    ATT = [abf[:, 27648:28160], abf[:, 28160:28672]]
    AST = [abf[:, 28672:28800], abf[:, 28800:28928]]
    NEGBIG = abf[:, 30720:47104]
    NEGROWB = abf[0:1, 47104:48128]
    NEGCB = abf[:, 48128:48192]
    NEGC = af32[:, 0:64]
    ZCOL = [af32[:, 72:76], af32[:, 76:80]]
    ZINV = [af32[:, 80:84], af32[:, 84:88]]
    ZROW = [af32[0:1, 128:640], af32[0:1, 640:1152]]
    masks = sb["masks0"] if layer == 0 else sb["masks1"]

    if layer == 0:
        q_view = qd.ap().rearrange("(j r) (m d) -> j (r m) d", j=4, m=16)
    else:
        q_view = qd.ap()

    for h in range(NH):
        kvh = h if layer == 0 else h // 2
        op("sync", lambda e, kvh=kvh, kd=kd: e.dma_start(out=KF, in_=kd.ap()[kvh * 128:(kvh + 1) * 128, :]),
           waits=[(s_st, s_st.n), (s_sA, s_sA.n), (s_sB, s_sB.n), (s_gst, s_gst.n),
                  (sems["s_sC"], sems["s_sC"].n),
                  (s_mm, s_mm.n), (s_ve, s_ve.n), (s_ac, s_ac.n)],
           incs=[(s_kv, 16)])
        op("sync", lambda e, kvh=kvh, vd=vd: e.dma_start(out=VT, in_=vd.ap()[kvh * 128:(kvh + 1) * 128, :]),
           incs=[(s_kv, 16)])
        kv_v = s_kv.n
        op("vector", lambda e: e.tensor_copy(KSUB, KF.rearrange("p (n e) -> p n e", e=8)[:, :, 0]),
           waits=[(s_kv, kv_v)], incs=[(s_ve, 1)])
        ksub_v = s_ve.n

        for qb in range(64):
            s_q = [s_qA, s_qB][qb % 2]
            if layer == 0:
                qn = abf[:, 28928 + (qb % 2) * 128: 28928 + (qb % 2) * 128 + 128]
                op("sync", lambda e, qn=qn, h=h, qb=qb:
                   e.dma_start(out=qn, in_=q_view[h, qb * 128:(qb + 1) * 128, :]),
                   waits=[(s_mm, ctx.cache.get(("qtr", h, qb - 2), 0))], incs=[(s_q, 16)])
                qlv = s_q.n
                tb = 1024 + (qb % 2) * 512
                op("tensor", lambda e, qn=qn, tb=tb:
                   e.matmul(psB[:, tb:tb + 128], qn, ident[:, :], start=True, stop=True),
                   waits=[(s_q, qlv), (s_ve, ctx.cache.get(("qtc", h, qb - 2), 0))],
                   incs=[(s_mm, 1)])
                ctx.cache[("qtr", h, qb)] = s_mm.n
                mmv = s_mm.n
                op("vector", lambda e, qb=qb, tb=tb:
                   e.tensor_copy(QT[:, qb * 128:(qb + 1) * 128], psB[:, tb:tb + 128]),
                   waits=[(s_mm, mmv)], incs=[(s_ve, 1)])
                ctx.cache[("qtc", h, qb)] = s_ve.n
                qt_w = (s_ve, s_ve.n)
            else:
                op("sync", lambda e, h=h, qb=qb:
                   e.dma_start(out=QT[:, qb * 128:(qb + 1) * 128],
                               in_=q_view[h * 128:(h + 1) * 128, qb * 128:(qb + 1) * 128]),
                   waits=[(s_mm, ctx.cache.get(("sqk", h, qb - 2), 0))], incs=[(s_q, 16)])
                qt_w = (s_q, s_q.n)
            sb0 = (qb % 2) * 1024
            for half in range(2):
                op("tensor", lambda e, qb=qb, half=half, sb0=sb0:
                   e.matmul(psA[:, sb0 + half * 512: sb0 + half * 512 + 512],
                            QT[:, qb * 128:(qb + 1) * 128], KSUB[:, half * 512:(half + 1) * 512],
                            start=True, stop=True),
                   waits=([qt_w, (s_ve, ksub_v),
                           (s_ve, ctx.cache.get(("ngc", h, qb - 2), 0))] if half == 0 else []),
                   incs=([(s_mm, 1)] if half == 1 else []))
            ctx.cache[("sqk", h, qb)] = s_mm.n
            mmv = s_mm.n
            nst = (qb // 4 + 1) if layer == 0 else 16
            ncols = min(nst * 512, 8192) // 8
            op("vector", lambda e, qb=qb, ncols=ncols, sb0=sb0:
               e.reduce_max(NEGC[:, qb:qb + 1], psA[:, sb0:sb0 + ncols], axis=AX, negate=True),
               waits=[(s_mm, mmv)], incs=[(s_ve, 1)])
            ctx.cache[("ngc", h, qb)] = s_ve.n

        op("vector", lambda e: e.tensor_copy(NEGCB, NEGC[:, :]), incs=[(s_ve, 1)])
        negcb_v = s_ve.n
        rowev = 0
        nbev = {}
        for g in range(16):
            for i in range(4):
                w = []
                if i == 0:
                    w = [(s_ve, negcb_v)] if g == 0 else [(s_ac, rowev)]
                op("tensor", lambda e, g=g, i=i:
                   e.matmul(psB[0:1, 1536 + i * 128: 1536 + (i + 1) * 128],
                            NEGCB[:, (4 * g + i):(4 * g + i + 1)], ident[:, :],
                            start=True, stop=True),
                   waits=w, incs=([(s_mm, 1)] if i == 3 else []))
            rmv = s_mm.n
            rsl = (g % 2) * 512
            op("scalar", lambda e, rsl=rsl:
               e.activation(NEGROWB[:, rsl:rsl + 512], psB[0:1, 1536:2048], AF.Copy),
               waits=[(s_mm, rmv)], incs=[(s_ac, 1)])
            rowev = s_ac.n
            bw = [(s_ac, rowev)]
            if g >= 2:
                bw.append((s_ve, nbev[g - 2]))
            op("tensor", lambda e, g=g, rsl=rsl:
               e.matmul(psB[:, (g % 2) * 512:(g % 2) * 512 + 512], onesr[:, :],
                        NEGROWB[:, rsl:rsl + 512], start=True, stop=True),
               waits=bw, incs=[(s_mm, 1)])
            bcv = s_mm.n
            for rep in range(2):
                op("vector", lambda e, g=g, rep=rep:
                   e.tensor_copy(NEGBIG[:, g * 1024 + rep * 512: g * 1024 + rep * 512 + 512],
                                 psB[:, (g % 2) * 512:(g % 2) * 512 + 512]),
                   waits=([(s_mm, bcv)] if rep == 0 else []), incs=[(s_ve, 1)])
            nbev[g] = s_ve.n
        negb_done = s_ve.n
        rowev_last = rowev

        pv_war = {0: (s_ve, negb_done), 1: (s_ve, negb_done)}
        dn_war = (s_ac, rowev_last)
        slotg = {}
        prbg = {0: 0, 1: 0}
        nrm_prev = 0
        zprev = {0: 0, 1: 0}
        rinv_prev = {0: 0, 1: 0}
        for g in range(16):
            if layer == 0:
                chunks = list(range(4 * g + 4))
            else:
                chunks = [kc for kc in range(64) if not (g == 15 and 56 <= kc <= 59)]
            npairs = len(chunks) // 2
            pvb0 = (g % 2) * 512
            expv = {}
            pvstate = {}

            def pv_dn(p, g=g, chunks=chunks, pvb0=pvb0, expv=expv, pvstate=pvstate):
                for half in (0, 1):
                    idx2 = 2 * p + half
                    kc2 = chunks[idx2]
                    first = (idx2 == 0)
                    last = (idx2 == len(chunks) - 1)
                    w = [(s_ac, expv[p])] if half == 0 else []
                    if first:
                        w.append(pv_war[g % 2])
                    op("tensor", lambda e, kc2=kc2, first=first, last=last, p=p, half=half, pvb0=pvb0:
                       e.matmul(psB[:, pvb0:pvb0 + 512], VT[:, kc2 * 128:(kc2 + 1) * 128],
                                PRB[p % 2][:, half * 512:(half + 1) * 512],
                                start=first, stop=last),
                       waits=w, incs=([(s_mm, 1)] if last else []))
                    if last:
                        pvstate["stop"] = s_mm.n
                    w2 = [dn_war] if first else []
                    op("tensor", lambda e, first=first, last=last, p=p, half=half:
                       e.matmul(psB[0:1, 1536:2048], onesc[:, :],
                                PRB[p % 2][:, half * 512:(half + 1) * 512],
                                start=first, stop=last),
                       waits=w2, incs=([(s_mm, 1)] if half == 1 else []))
                prbg[p % 2] = s_mm.n

            for idx, kc in enumerate(chunks):
                p = idx // 2
                si = idx % 4
                w = []
                if si in slotg:
                    w.append((s_ac, slotg.pop(si)))
                if g == 0 and idx == 0:
                    w.append((s_ve, negb_done))
                op("tensor", lambda e, kc=kc, si=si, g=g:
                   e.matmul(psA[:, si * 512:(si + 1) * 512], KF[:, kc * 128:(kc + 1) * 128],
                            QT[:, g * 512:(g + 1) * 512], start=True, stop=True),
                   waits=w, incs=[(s_mm, 1)])
                if idx % 2 == 1:
                    qkv = s_mm.n
                    op("vector", lambda e, p=p, g=g:
                       e.tensor_tensor(psA[:, (p % 2) * 1024:(p % 2) * 1024 + 1024],
                                       psA[:, (p % 2) * 1024:(p % 2) * 1024 + 1024],
                                       NEGBIG[:, g * 1024:(g + 1) * 1024], ALU.add),
                       waits=[(s_mm, qkv)], incs=[(s_ve, 1)])
                    for idx2 in (idx - 1, idx):
                        kc2 = chunks[idx2]
                        mi = -1
                        if layer == 0 and kc2 // 4 == g:
                            mi = kc2 % 4
                        if layer == 1 and kc2 >= 56 and ((g == 14 and kc2 <= 59) or (g == 15 and kc2 >= 60)):
                            mi = kc2 - 56
                        if mi >= 0:
                            si2 = idx2 % 4
                            op("vector", lambda e, si2=si2, mi=mi:
                               e.tensor_tensor(psA[:, si2 * 512:(si2 + 1) * 512],
                                               psA[:, si2 * 512:(si2 + 1) * 512],
                                               masks[:, mi * 512:(mi + 1) * 512], ALU.add),
                               incs=[(s_ve, 1)])
                    bias_v = s_ve.n
                    if p >= 2:
                        pv_dn(p - 2)
                    op("scalar", lambda e, p=p:
                       e.activation(PRB[p % 2], psA[:, (p % 2) * 1024:(p % 2) * 1024 + 1024],
                                    AF.Exp, scale=SCL),
                       waits=[(s_ve, bias_v), (s_mm, prbg[p % 2])], incs=[(s_ac, 1)])
                    expv[p] = s_ac.n
                    slotg[(2 * p) % 4] = s_ac.n
                    slotg[(2 * p + 1) % 4] = s_ac.n
            if npairs >= 2:
                pv_dn(npairs - 2)
            pv_dn(npairs - 1)
            dn_stop = prbg[(npairs - 1) % 2]

            par = g % 2
            op("vector", lambda e, par=par, pvb0=pvb0: e.tensor_copy(ATT[par], psB[:, pvb0:pvb0 + 512]),
               waits=[(s_mm, pvstate["stop"])], incs=[(s_ve, 1)])
            att_v = s_ve.n
            pv_war[par] = (s_ve, att_v)
            op("scalar", lambda e, par=par: e.activation(ZROW[par], psB[0:1, 1536:2048], AF.Copy),
               waits=[(s_mm, dn_stop)], incs=[(s_ac, 1)])
            zrow_v = s_ac.n
            dn_war = (s_ac, zrow_v)
            s_z = s_xA if par == 0 else s_xB
            op("sync", lambda e, par=par: e.dma_start(out=zscr.ap()[par:par + 1, :], in_=ZROW[par]),
               waits=[(s_ac, zrow_v), (s_z, zprev[par])], incs=[(s_z, 16)])
            z1 = s_z.n
            op("sync", lambda e, par=par: e.dma_start(
                out=ZCOL[par],
                in_=zscr.ap()[par:par + 1, :].rearrange("o (c p) -> (o p) c", p=128),
                allow_slow_non_contiguous=True),
               waits=[(s_z, z1), (s_ve, rinv_prev[par])], incs=[(s_z, 16)])
            z2 = s_z.n
            zprev[par] = z2
            op("vector", lambda e, par=par: e.reciprocal(ZINV[par], ZCOL[par]),
               waits=[(s_z, z2)], incs=[(s_ve, 1)])
            rinv_v = s_ve.n
            rinv_prev[par] = rinv_v
            trv = []
            for i in range(4):
                w = [(s_ve, att_v), (s_ac, nrm_prev)] if i == 0 else []
                op("tensor", lambda e, i=i, par=par:
                   e.matmul(psB[:, 1024 + i * 128:1024 + (i + 1) * 128],
                            ATT[par][:, i * 128:(i + 1) * 128], ident[:, :],
                            start=True, stop=True),
                   waits=w, incs=[(s_mm, 1)])
                trv.append(s_mm.n)
            s_so = s_sA if h < 2 else s_sB
            aout = attn_out[h // 2]
            for i in range(4):
                ast = AST[i % 2]
                op("scalar", lambda e, i=i, ast=ast, par=par:
                   e.activation(ast, psB[:, 1024 + i * 128:1024 + (i + 1) * 128],
                                AF.Copy, scale=ZINV[par][:, i:i + 1]),
                   waits=[(s_mm, trv[i]), (s_ve, rinv_v), (s_so, s_so.n)], incs=[(s_ac, 1)])
                nv = s_ac.n
                qb2 = 4 * g + i
                op("sync", lambda e, ast=ast, qb2=qb2, aout=aout, h=h:
                   e.dma_start(out=aout.ap()[qb2 * 128:(qb2 + 1) * 128, (h % 2) * 128:(h % 2) * 128 + 128],
                               in_=ast),
                   waits=[(s_ac, nv)], incs=[(s_so, 16)])
            nrm_prev = s_ac.n
        ctx.cache[("hdone", layer, h)] = (s_sA if h < 2 else s_sB).n


def emit_wo_base(ctx, nc, sems, sb, psA, psB, ag, w_in, bias_row, x_res, base, cc_wait, own_scr):
    op = ctx.op
    s_wl, s_al, s_st, s_mm, s_ve, s_ac, s_cc = (
        sems[k] for k in ("s_wl", "s_al", "s_st", "s_mm", "s_ve", "s_ac", "s_cc"))
    s_gb, s_x2, s_sB = sems["s_gb"], sems["s_x2"], sems["s_sB"]
    abf, af32 = sb["abf"], sb["af32"]
    ident, ones = sb["ident_bf"], sb["ones_col"]
    WOB = abf[:, 0:32768]
    ATMp = [abf[:, 32768:34816], abf[:, 38912:40960]]
    AFM = [abf[:, 34816:36864], abf[:, 36864:38912]]
    x_res_t = x_res.ap().rearrange("(n p) d -> n p d", p=128)
    base_t = base.ap().rearrange("(n p) d -> n p d", p=128)

    op("gpsimd", lambda e: e.dma_start(out=WOB.rearrange("p (k o) -> p k o", k=16),
                                       in_=w_in.ap().rearrange("(k p) o -> p k o", p=128)),
       waits=[(s_mm, s_mm.n), (s_ve, s_ve.n)], incs=[(s_wl, 16)])
    wo_v = s_wl.n

    if ctx.who == "gpsimd" and "my_off" not in ctx.cache:
        pid = ctx.eng.partition_id()
        ctx.cache["my_off"] = ctx.eng.snap((pid % 4) * TOK)
    for half in range(2):
        agx = ag[half]
        for r in range(4):
            def gfn(e, r=r, half=half, agx=agx):
                mo = ctx.cache["my_off"]
                return e.dma_start(out=own_scr.ap()[:, r * 512 + half * 256: r * 512 + half * 256 + 256],
                                   in_=agx.ap()[r * S:, :][bass.ds(mo, TOK), :])
            op("gpsimd", gfn,
               waits=([(s_cc, cc_wait - 1 + half)] if r == 0 else []), incs=[(s_gb, 16)])
    own_v = s_gb.n
    own_t = own_scr.ap().rearrange("(n p) d -> n p d", p=128)

    for t in range(16):
        ATM = ATMp[t % 2]
        s_x = [sems["s_xA"], sems["s_xB"]][t % 2]
        op("sync", lambda e, t=t, ATM=ATM: e.dma_start(out=ATM, in_=own_t[t]),
           waits=[(s_gb, own_v), (s_mm, ctx.cache.get(("wotr", t - 2), 0))],
           incs=[(s_x, 16)])
        gv = (s_x, s_x.n)
        op("sync", lambda e, t=t: e.dma_start(out=af32[:, 0:2048], in_=x_res_t[t]),
           waits=[(s_ve, ctx.cache.get(("wocb", t - 1), 0))], incs=[(s_x2, 16)])
        xv = s_x2.n
        for kc in range(16):
            op("tensor", lambda e, kc=kc, ATM=ATM:
               e.matmul(psB[:, kc * 128:(kc + 1) * 128], ATM[:, kc * 128:(kc + 1) * 128],
                        ident[:, :], start=True, stop=True),
               waits=([gv, (s_wl, wo_v), (s_ve, ctx.cache.get(("woev", t - 1), 0))]
                      if kc == 0 else []),
               incs=([(s_mm, 1)] if kc == 15 else []))
        ctx.cache[("wotr", t)] = s_mm.n
        mmv = s_mm.n
        afm = AFM[t % 2]
        op("vector", lambda e, afm=afm: e.tensor_copy(afm, psB[:, :]),
           waits=[(s_mm, mmv)], incs=[(s_ve, 1)])
        ctx.cache[("woev", t)] = s_ve.n
        afm_v = s_ve.n
        for og in range(4):
            bank = (og % 2) * 512
            for kc in range(16):
                op("tensor", lambda e, afm=afm, kc=kc, og=og, bank=bank:
                   e.matmul(psA[:, bank:bank + 512], afm[:, kc * 128:(kc + 1) * 128],
                            WOB[:, kc * 2048 + og * 512: kc * 2048 + og * 512 + 512],
                            start=(kc == 0), stop=(kc == 15 and bias_row is None)),
                   waits=([(s_ve, afm_v), (s_ve, ctx.cache.get(("wocbv", t, og - 2), 0))]
                          if kc == 0 else []),
                   incs=([(s_mm, 1)] if (kc == 15 and bias_row is None) else []))
            if bias_row is not None:
                op("tensor", lambda e, og=og, bank=bank:
                   e.matmul(psA[:, bank:bank + 512], ones[:, :], bias_row[0:1, og * 512:(og + 1) * 512],
                            start=False, stop=True),
                   incs=[(s_mm, 1)])
            mmv = s_mm.n
            op("vector", lambda e, og=og, bank=bank:
               e.tensor_tensor(af32[:, 2048 + og * 512: 2048 + og * 512 + 512],
                               psA[:, bank:bank + 512], af32[:, og * 512:(og + 1) * 512], ALU.add),
               waits=[(s_mm, mmv), (s_x2, xv), (s_sB, s_sB.n)],
               incs=[(s_ve, 1)])
            ctx.cache[("wocbv", t, og)] = s_ve.n
        ctx.cache[("wocb", t)] = s_ve.n
        cbv = s_ve.n
        op("sync", lambda e, t=t: e.dma_start(out=base_t[t], in_=af32[:, 2048:4096]),
           waits=[(s_ve, cbv)], incs=[(s_sB, 16)])


def emit_ffn0(ctx, nc, sems, sb, psA, psB, dr, inp):
    op = ctx.op
    s_wl, s_al, s_st, s_mm, s_ve, s_ac = (
        sems[k] for k in ("s_wl", "s_al", "s_st", "s_mm", "s_ve", "s_ac"))
    s_wA, s_wB, s_x2, s_sA, s_sB = (sems[k] for k in ("s_wA", "s_wB", "s_x2", "s_sA", "s_sB"))
    s_qA, s_qB, s_gb, s_gst, s_cc = (sems[k] for k in ("s_qA", "s_qB", "s_gb", "s_gst", "s_cc"))
    abf, af32 = sb["abf"], sb["af32"]
    su_ug, su_dp = sb["su_ug"], sb["su_dp"]
    XNF = abf[:, 0:8192]
    UBF = abf[:, 0:8192]
    CBF = abf[:, 8192:40960]
    WUBp = [abf[:, 40960:43008], abf[:, 48128:50176]]
    WGBp = [abf[:, 43008:45056], abf[:, 50176:52224]]
    XH = af32[:, 0:8192]
    SIL = af32[:, 8192:8704]
    xn0_v = dr["xn0d"].ap().rearrange("(k p) t -> p k t", p=128)
    wupf_v = dr["wupf"].ap().rearrange("(k p) h -> p k h", p=128)
    wgatef_v = dr["wgatef"].ap().rearrange("(k p) h -> p k h", p=128)
    base_t = dr["base0"].ap().rearrange("(n p) d -> n p d", p=128)
    x1_t = dr["x1d"].ap().rearrange("(n p) d -> n p d", p=128)

    for tt in range(4):
        op("sync", lambda e, tt=tt:
           e.dma_start(out=XNF.rearrange("p (k t) -> p k t", k=16), in_=xn0_v[:, :, tt * 512:(tt + 1) * 512]),
           waits=[(s_mm, s_mm.n), (s_ve, s_ve.n), (s_st, s_st.n),
                  (sems["s_gst"], sems["s_gst"].n), (sems["s_gb"], sems["s_gb"].n)],
           incs=[(s_al, 16)])
        lv = s_al.n
        ctx.cache[("f0mm", tt, -1)] = s_mm.n
        for kc in range(16):
            op("vector", lambda e, kc=kc:
               e.tensor_scalar_mul(XH[:, kc * 512:(kc + 1) * 512], XNF[:, kc * 512:(kc + 1) * 512],
                                   su_ug[:, kc:kc + 1]),
               waits=([(s_al, lv)] if kc == 0 else []), incs=([(s_ve, 1)] if kc == 15 else []))
        for stg in range(4):
            b = 1 << stg
            for i in range(16):
                if i & b:
                    continue
                p = i | b
                op("vector", lambda e, i=i, p=p:
                   e.tensor_add(XH[:, i * 512:(i + 1) * 512], XH[:, i * 512:(i + 1) * 512],
                                XH[:, p * 512:(p + 1) * 512]), incs=[])
                op("vector", lambda e, i=i, p=p:
                   e.scalar_tensor_tensor(XH[:, p * 512:(p + 1) * 512], XH[:, p * 512:(p + 1) * 512],
                                          -2.0, XH[:, i * 512:(i + 1) * 512], ALU.mult, ALU.add),
                   incs=[])
        for kc in range(16):
            op("vector", lambda e, kc=kc:
               e.tensor_copy(UBF[:, kc * 512:(kc + 1) * 512], XH[:, kc * 512:(kc + 1) * 512]),
               incs=([(s_ve, 1)] if kc == 15 else []))
        u_v = s_ve.n
        for hc in range(64):
            WUB, WGB = WUBp[hc % 2], WGBp[hc % 2]
            s_w = [s_wA, s_wB][hc % 2]
            op("sync", lambda e, hc=hc, WUB=WUB:
               e.dma_start(out=WUB.rearrange("p (k h) -> p k h", k=16), in_=wupf_v[:, :, hc * 128:(hc + 1) * 128]),
               waits=[(s_mm, ctx.cache.get(("f0mm", tt, hc - 2), ctx.cache.get(("f0mm", tt, -1), 0)))],
               incs=[(s_w, 16)])
            op("sync", lambda e, hc=hc, WGB=WGB:
               e.dma_start(out=WGB.rearrange("p (k h) -> p k h", k=16), in_=wgatef_v[:, :, hc * 128:(hc + 1) * 128]),
               incs=[(s_w, 16)])
            wv = (s_w, s_w.n)
            bu = (hc % 2) * 1024
            for kc in range(16):
                op("tensor", lambda e, kc=kc, bu=bu, WUB=WUB:
                   e.matmul(psA[:, bu:bu + 512], WUB[:, kc * 128:(kc + 1) * 128],
                            UBF[:, kc * 512:(kc + 1) * 512], start=(kc == 0), stop=(kc == 15)),
                   waits=([wv, (s_ve, u_v), (s_ve, ctx.cache.get(("f0c", tt, hc - 2), 0))]
                          if kc == 0 else []), incs=[])
            for kc in range(16):
                op("tensor", lambda e, kc=kc, bu=bu, WGB=WGB:
                   e.matmul(psA[:, bu + 512:bu + 1024], WGB[:, kc * 128:(kc + 1) * 128],
                            UBF[:, kc * 512:(kc + 1) * 512], start=(kc == 0), stop=(kc == 15)),
                   incs=([(s_mm, 1)] if kc == 15 else []))
            ctx.cache[("f0mm", tt, hc)] = s_mm.n
            mmv = s_mm.n
            op("scalar", lambda e, bu=bu: e.activation(SIL, psA[:, bu + 512:bu + 1024], AF.Silu),
               waits=[(s_mm, mmv), (s_ve, ctx.cache.get(("f0c", tt, hc - 1), 0))], incs=[(s_ac, 1)])
            acv = s_ac.n
            op("vector", lambda e, hc=hc, bu=bu:
               e.scalar_tensor_tensor(CBF[:, hc * 512:(hc + 1) * 512], SIL, su_dp[:, hc:hc + 1],
                                      psA[:, bu:bu + 512], ALU.mult, ALU.mult),
               waits=[(s_ac, acv)], incs=[(s_ve, 1)])
            ctx.cache[("f0c", tt, hc)] = s_ve.n
        c_v = s_ve.n
        for sub in range(4):
            op("gpsimd", lambda e, sub=sub, tt=tt:
               e.dma_start(out=XH[:, sub * 2048:(sub + 1) * 2048], in_=base_t[tt * 4 + sub]),
               waits=([(s_ve, c_v), (s_sB, s_sB.n)] if sub == 0 else []), incs=[(s_gb, 16)])
        bv = s_gb.n
        ctx.cache[("f0dmm0", tt, 0)] = s_mm.n
        wdownf_v = dr["wdownf"].ap()
        WDB = [abf[:, 40960:43008], abf[:, 43008:45056], abf[:, 48128:50176], abf[:, 50176:52224]]
        for og in range(4):
            for hc4 in range(16):
                wb = WDB[hc4 % 4]
                s_w = [s_wA, s_wB, s_qA, s_qB][hc4 % 4]
                op("sync", lambda e, wb=wb, hc4=hc4, og=og:
                   e.dma_start(out=wb.rearrange("p (k o) -> p k o", k=4),
                               in_=wdownf_v[hc4 * 512:(hc4 + 1) * 512, og * 512:(og + 1) * 512]
                               .rearrange("(k p) o -> p k o", p=128)),
                   waits=[(s_mm, ctx.cache.get(("f0dmm", tt, og, hc4 - 4),
                                               ctx.cache.get(("f0dmm0", tt, og), 0)))],
                   incs=[(s_w, 16)])
                wv = (s_w, s_w.n)
                for k in range(4):
                    hc = hc4 * 4 + k
                    for sub in range(4):
                        w_ = []
                        if k == 0 and sub == 0:
                            w_ = [wv, (s_ve, c_v)]
                            if hc == 0:
                                w_.append((s_ve, ctx.cache.get(("f0ev", tt, og - 1), 0)))
                        op("tensor", lambda e, wb=wb, k=k, hc=hc, sub=sub:
                           e.matmul(psB[:, sub * 512:(sub + 1) * 512],
                                    CBF[:, hc * 512 + sub * 128: hc * 512 + sub * 128 + 128],
                                    wb[:, k * 512:(k + 1) * 512],
                                    start=(hc == 0), stop=(hc == 63)),
                           waits=w_,
                           incs=([(s_mm, 1)] if (k == 3 and sub == 3) else []))
                ctx.cache[("f0dmm", tt, og, hc4)] = s_mm.n
            ctx.cache[("f0dmm0", tt, og + 1)] = s_mm.n
            dn_v = s_mm.n
            for sub in range(4):
                op("vector", lambda e, sub=sub, og=og:
                   e.tensor_tensor(af32[:, 10240 + sub * 512: 10240 + sub * 512 + 512],
                                   psB[:, sub * 512:(sub + 1) * 512],
                                   XH[:, sub * 2048 + og * 512: sub * 2048 + og * 512 + 512], ALU.add),
                   waits=[(s_mm, dn_v), (s_gb, bv), (s_gst, s_gst.n)],
                   incs=[(s_ve, 1)])
                vv = s_ve.n
                op("gpsimd", lambda e, sub=sub, og=og, tt=tt:
                   e.dma_start(out=x1_t[tt * 4 + sub][:, og * 512:(og + 1) * 512],
                               in_=af32[:, 10240 + sub * 512: 10240 + sub * 512 + 512]),
                   waits=[(s_ve, vv)], incs=[(s_gst, 16)])
            ctx.cache[("f0ev", tt, og)] = s_ve.n
            ctx.cache[("x1st", tt)] = s_gst.n
        op("gpsimd", lambda e, tt=tt: e.collective_compute(
            "AllGather", ALU.bypass, replica_groups=[[0, 1, 2, 3], [4, 5, 6, 7]],
            ins=[dr["x1d"].ap()[tt * 512:(tt + 1) * 512, :]],
            outs=[dr["x1ag"].ap()[tt * 2048:(tt + 1) * 2048, :]]),
           waits=[(s_gst, ctx.cache[("x1st", tt)])], incs=[(s_cc, 1)])
        ctx.cache[("x1cc", tt)] = s_cc.n


def emit_layer1(ctx, nc, inp, out, dr, sb, psA, psB, sems, dbg):
    op = ctx.op
    s_wl, s_al, s_st, s_gst, s_mm, s_ve, s_ac, s_cc = (
        sems[k] for k in ("s_wl", "s_al", "s_st", "s_gst", "s_mm", "s_ve", "s_ac", "s_cc"))
    s_sA, s_sB = sems["s_sA"], sems["s_sB"]
    abf, af32, stat = sb["abf"], sb["af32"], sb["stat"]
    ident, ones = sb["ident_bf"], sb["ones_col"]
    x1ag_cc = s_cc.n

    XT = [af32[:, 0:2048], af32[:, 2048:4096], af32[:, 4096:6144], af32[:, 6144:8192]]
    SQp = [af32[:, 9728:11776]] * 4
    SQ = af32[:, 8704:9728]
    RMS = af32[:, 8192:8704]
    XNT = [abf[:, 0:2048], abf[:, 2048:4096]]
    WQB = abf[:, 20480:28672]
    WKB = abf[:, 28672:32768]
    WVB = abf[:, 32768:36864]
    x1ag_t = dr["x1ag"].ap().rearrange("(n p) d -> n p d", p=128)
    q1_v = dr["q1d"].ap().rearrange("(h p) s -> p h s", h=NH)
    k1_v = dr["k1d"].ap().rearrange("(h p) s -> p h s", h=2)
    v1_v = dr["v1d"].ap().rearrange("(h p) s -> p h s", h=2)

    def perm(j):
        return ((j % 16) // 4) * 2048 + (j // 16) * 512 + (j % 4) * 128

    op("gpsimd", lambda e: e.dma_start(out=WQB.rearrange("p (k o) -> p k o", k=16),
                                       in_=inp["l1_wq"].ap().rearrange("(k p) o -> p k o", p=128)),
       waits=[(s_mm, s_mm.n), (s_ve, s_ve.n)], incs=[(s_wl, 16)])
    op("gpsimd", lambda e: e.dma_start(out=WKB.rearrange("p (k o) -> p k o", k=16),
                                       in_=inp["l1_wk"].ap().rearrange("(k p) o -> p k o", p=128)),
       incs=[(s_wl, 16)])
    op("gpsimd", lambda e: e.dma_start(out=WVB.rearrange("p (k o) -> p k o", k=16),
                                       in_=inp["l1_wv"].ap().rearrange("(k p) o -> p k o", p=128)),
       incs=[(s_wl, 16)])
    w_v = s_wl.n

    ln_acv = 0
    for mac in range(16):
        XF = abf[:, 4096 + (mac % 2) * 8192: 4096 + (mac % 2) * 8192 + 8192]
        for sub in range(4):
            ti = mac * 4 + sub
            xt = XT[ti % 4]
            s_x = [sems["s_xA"], sems["s_xB"], sems["s_xC"], sems["s_xD"]][ti % 4]
            ccw = [(s_cc, ctx.cache.get(("x1cc", ti // 16), x1ag_cc))] if ti % 16 == 0 else []
            op("sync", lambda e, xt=xt, ti=ti: e.dma_start(out=xt, in_=x1ag_t[ti]),
               waits=ccw + ([(s_ac, ctx.cache.get(("b1ln", ti - 4), 0))] if ti else
                            [(s_ac, s_ac.n), (s_ve, s_ve.n)]),
               incs=[(s_x, 16)])
            lv = s_x.n
            ctx.wait("vector", [(s_x, lv)])
            ctx.wait("scalar", [(s_x, lv)])
            if ti >= 2:
                ctx.wait("scalar", [(s_mm, ctx.cache.get(("b1tr", ti - 2), 0))])
            _, ln_acv = emit_ln(ctx, sems, sb, xt, SQp[ti % 4], stat, XNT[ti % 2], cb=(ti % 4) * 10)
            ctx.cache[("b1ln", ti)] = ln_acv
            acv = s_ac.n
            for kc in range(16):
                op("tensor", lambda e, ti=ti, kc=kc:
                   e.matmul(psB[:, kc * 128:(kc + 1) * 128], XNT[ti % 2][:, kc * 128:(kc + 1) * 128],
                            ident[:, :], start=True, stop=True),
                   waits=([(s_ac, acv), (s_ve, ctx.cache.get(("b1ev", ti - 2), 0))] if kc == 0 else []),
                   incs=([(s_mm, 1)] if kc == 15 else []))
            ctx.cache[("b1tr", ti)] = s_mm.n
            mmv = s_mm.n
            dst = XF.rearrange("p (k t) -> p k t", k=16)[:, :, sub * 128:(sub + 1) * 128]
            op("vector", lambda e, dst=dst:
               e.tensor_copy(dst, psB[:, :].rearrange("p (k t) -> p k t", k=16)),
               waits=[(s_mm, mmv)], incs=[(s_ve, 1)])
            ctx.cache[("b1ev", ti)] = s_ve.n
        xf_ready = s_ve.n

        for sub in range(4):
            ti = mac * 4 + sub
            bank = (sub % 2) * 512
            for kc in range(16):
                op("tensor", lambda e, kc=kc, sub=sub, bank=bank, XF=XF:
                   e.matmul(psA[:, bank:bank + 512], XF[:, kc * 512 + sub * 128: kc * 512 + sub * 128 + 128],
                            WQB[:, kc * 512:(kc + 1) * 512], start=(kc == 0), stop=(kc == 15)),
                   waits=([(s_wl, w_v), (s_ve, xf_ready),
                           (s_ac, ctx.cache.get(("qrm", ti - 2), 0))] if kc == 0 else []),
                   incs=([(s_mm, 1)] if kc == 15 else []))
            qmm = s_mm.n
            op("scalar", lambda e, bank=bank: e.activation(SQ[:, 0:512], psA[:, bank:bank + 512],
                                                           AF.Square),
               waits=[(s_mm, qmm), (s_ve, ctx.cache.get(("qsq", ti - 1), 0))], incs=[(s_ac, 1)])
            sqv = s_ac.n
            op("vector", lambda e: e.reduce_sum(RMS[:, 0:4],
                                                SQ[:, 0:512].rearrange("p (h j) -> p h j", h=4), axis=AX),
               waits=[(s_ac, sqv)], incs=[(s_ve, 1)])
            ctx.cache[("qsq", ti)] = s_ve.n
            rsv = s_ve.n
            op("scalar", lambda e: e.activation(RMS[:, 4:8], RMS[:, 0:4], AF.Sqrt,
                                                scale=1.0 / HD, bias=1e-6),
               waits=[(s_ve, rsv)], incs=[(s_ac, 1)])
            acv = s_ac.n
            op("vector", lambda e: e.reciprocal(RMS[:, 8:12], RMS[:, 4:8]),
               waits=[(s_ac, acv)], incs=[(s_ve, 1)])
            rzv = s_ve.n
            qn = abf[:, 36864 + (sub % 2) * 512: 36864 + (sub % 2) * 512 + 512]
            for hh in range(4):
                op("vector", lambda e, hh=hh, qn=qn, bank=bank:
                   e.tensor_scalar_mul(qn[:, hh * 128:(hh + 1) * 128],
                                       psA[:, bank + hh * 128: bank + hh * 128 + 128],
                                       RMS[:, 8 + hh:9 + hh]),
                   waits=([(s_ve, rzv), (s_mm, ctx.cache.get(("qtr1", ti - 2), 0))] if hh == 0 else []),
                   incs=([(s_ve, 1)] if hh == 3 else []))
            ctx.cache[("qrm", ti)] = s_ac.n
            qnv = s_ve.n
            for hh in range(4):
                op("tensor", lambda e, hh=hh, qn=qn:
                   e.matmul(psB[:, 1024 + hh * 128: 1024 + hh * 128 + 128],
                            qn[:, hh * 128:(hh + 1) * 128], ident[:, :], start=True, stop=True),
                   waits=([(s_ve, qnv), (s_ve, ctx.cache.get(("qfe", ti - 1), 0))] if hh == 0 else []),
                   incs=([(s_mm, 1)] if hh == 3 else []))
            ctx.cache[("qtr1", ti)] = s_mm.n
            mmv = s_mm.n
            qf = abf[:, 37888 + (sub % 2) * 512: 37888 + (sub % 2) * 512 + 512]
            op("vector", lambda e, qf=qf: e.tensor_scalar_mul(qf, psB[:, 1024:1536], sb["qnw"][:, 0:1]),
               waits=[(s_mm, mmv), (s_sA, s_sA.n)], incs=[(s_ve, 1)])
            ctx.cache[("qfe", ti)] = s_ve.n
            qfv = s_ve.n
            op("sync", lambda e, qf=qf, ps=perm(ti):
               e.dma_start(out=q1_v[:, :, ps:ps + 128],
                           in_=qf.rearrange("p (h t) -> p h t", h=4)),
               waits=[(s_ve, qfv)], incs=[(s_sA, 16)])

            for kc in range(16):
                op("tensor", lambda e, kc=kc, sub=sub, XF=XF:
                   e.matmul(psA[:, 1024:1280], XF[:, kc * 512 + sub * 128: kc * 512 + sub * 128 + 128],
                            WKB[:, kc * 256:(kc + 1) * 256], start=(kc == 0), stop=(kc == 15)),
                   waits=([(s_ve, ctx.cache.get(("krm", ti - 1), 0))] if kc == 0 else []),
                   incs=([(s_mm, 1)] if kc == 15 else []))
            kmm = s_mm.n
            op("scalar", lambda e: e.activation(SQ[:, 512:768], psA[:, 1024:1280], AF.Square),
               waits=[(s_mm, kmm), (s_ve, ctx.cache.get(("ksq", ti - 1), 0))], incs=[(s_ac, 1)])
            sqv = s_ac.n
            op("vector", lambda e: e.reduce_sum(RMS[:, 12:14],
                                                SQ[:, 512:768].rearrange("p (h j) -> p h j", h=2), axis=AX),
               waits=[(s_ac, sqv)], incs=[(s_ve, 1)])
            ctx.cache[("ksq", ti)] = s_ve.n
            rsv = s_ve.n
            op("scalar", lambda e: e.activation(RMS[:, 14:16], RMS[:, 12:14], AF.Sqrt,
                                                scale=1.0 / HD, bias=1e-6),
               waits=[(s_ve, rsv)], incs=[(s_ac, 1)])
            acv = s_ac.n
            op("vector", lambda e: e.reciprocal(RMS[:, 16:18], RMS[:, 14:16]),
               waits=[(s_ac, acv)], incs=[(s_ve, 1)])
            rzv = s_ve.n
            kn = abf[:, 38912 + (sub % 2) * 256: 38912 + (sub % 2) * 256 + 256]
            for hh in range(2):
                op("vector", lambda e, hh=hh, kn=kn:
                   e.tensor_scalar_mul(kn[:, hh * 128:(hh + 1) * 128],
                                       psA[:, 1024 + hh * 128: 1024 + hh * 128 + 128],
                                       RMS[:, 16 + hh:17 + hh]),
                   waits=([(s_ve, rzv), (s_mm, ctx.cache.get(("ktr1", ti - 2), 0))] if hh == 0 else []),
                   incs=([(s_ve, 1)] if hh == 1 else []))
            ctx.cache[("krm", ti)] = s_ve.n
            knv = s_ve.n
            for hh in range(2):
                op("tensor", lambda e, hh=hh, kn=kn:
                   e.matmul(psB[:, 1536 + hh * 128: 1536 + hh * 128 + 128],
                            kn[:, hh * 128:(hh + 1) * 128], ident[:, :], start=True, stop=True),
                   waits=([(s_ve, knv), (s_ve, ctx.cache.get(("kfe", ti - 1), 0))] if hh == 0 else []),
                   incs=([(s_mm, 1)] if hh == 1 else []))
            ctx.cache[("ktr1", ti)] = s_mm.n
            mmv = s_mm.n
            kf = abf[:, 39424 + (sub % 2) * 256: 39424 + (sub % 2) * 256 + 256]
            op("vector", lambda e, kf=kf: e.tensor_scalar_mul(kf, psB[:, 1536:1792], sb["knw"][:, 0:1]),
               waits=[(s_mm, mmv), (s_sB, s_sB.n)], incs=[(s_ve, 1)])
            ctx.cache[("kfe", ti)] = s_ve.n
            kfv = s_ve.n
            op("sync", lambda e, kf=kf, ps=perm(ti):
               e.dma_start(out=k1_v[:, :, ps:ps + 128],
                           in_=kf.rearrange("p (h t) -> p h t", h=2)),
               waits=[(s_ve, kfv)], incs=[(s_sB, 16)])

            for kc in range(16):
                op("tensor", lambda e, kc=kc, sub=sub, XF=XF:
                   e.matmul(psA[:, 1536:1792], XF[:, kc * 512 + sub * 128: kc * 512 + sub * 128 + 128],
                            WVB[:, kc * 256:(kc + 1) * 256], start=(kc == 0), stop=(kc == 15)),
                   waits=([(s_ac, ctx.cache.get(("vev", ti - 1), 0))] if kc == 0 else []),
                   incs=([(s_mm, 1)] if kc == 15 else []))
            vmm = s_mm.n
            vst = abf[:, 39936 + (sub % 2) * 256: 39936 + (sub % 2) * 256 + 256]
            op("scalar", lambda e, vst=vst: e.activation(vst, psA[:, 1536:1792], AF.Copy),
               waits=[(s_mm, vmm), (sems["s_sC"], sems["s_sC"].n)], incs=[(s_ac, 1)])
            ctx.cache[("vev", ti)] = s_ac.n
            acv = s_ac.n
            op("sync", lambda e, vst=vst, ps=perm(ti):
               e.dma_start(out=v1_v[:, :, ps:ps + 128],
                           in_=vst.rearrange("p (h t) -> p h t", h=2)),
               waits=[(s_ac, acv)], incs=[(sems["s_sC"], 16)])
    a1q_done = s_st.n

    x1_t = dr["x1d"].ap().rearrange("(n p) d -> n p d", p=128)
    xn1_v = dr["xn1d"].ap().rearrange("(k p) t -> p k t", p=128)
    XNF = abf[:, 4096:36864]
    ln_acv2 = 0
    for ti in range(16):
        xt = XT[ti % 4]
        s_x = [sems["s_xA"], sems["s_xB"], sems["s_xC"], sems["s_xD"]][ti % 4]
        op("sync", lambda e, xt=xt, ti=ti: e.dma_start(out=xt, in_=x1_t[ti]),
           waits=([(s_ac, ctx.cache.get(("o1ln", ti - 4), 0))] if ti else
                  [(s_ac, s_ac.n), (s_ve, s_ve.n)]),
           incs=[(s_x, 16)])
        lv = s_x.n
        ctx.wait("vector", [(s_x, lv)])
        ctx.wait("scalar", [(s_x, lv)])
        if ti >= 2:
            ctx.wait("scalar", [(s_mm, ctx.cache.get(("o1tr", ti - 2), 0))])
        _, ln_acv2 = emit_ln(ctx, sems, sb, xt, SQp[ti % 4], stat, XNT[ti % 2], cb=(ti % 4) * 10)
        ctx.cache[("o1ln", ti)] = ln_acv2
        acv = s_ac.n
        for kc in range(16):
            op("tensor", lambda e, ti=ti, kc=kc:
               e.matmul(psB[:, kc * 128:(kc + 1) * 128], XNT[ti % 2][:, kc * 128:(kc + 1) * 128],
                        ident[:, :], start=True, stop=True),
               waits=([(s_ac, acv), (s_ve, ctx.cache.get(("o1ev", ti - 1), 0))] if kc == 0 else []),
               incs=([(s_mm, 1)] if kc == 15 else []))
        ctx.cache[("o1tr", ti)] = s_mm.n
        mmv = s_mm.n
        dst = XNF[:, :].rearrange("p (k t) -> p k t", k=16)[:, :, ti * 128:(ti + 1) * 128]
        op("vector", lambda e, dst=dst:
           e.tensor_copy(dst, psB[:, :].rearrange("p (k t) -> p k t", k=16)),
           waits=[(s_mm, mmv)], incs=[(s_ve, 1)])
        ctx.cache[("o1ev", ti)] = s_ve.n
    xnf_ready = s_ve.n
    op("sync", lambda e: e.dma_start(out=xn1_v[:, :, :],
                                     in_=XNF[:, :].rearrange("p (k t) -> p k t", k=16)),
       waits=[(s_ve, xnf_ready)], incs=[(s_st, 16)])

    emit_attention(ctx, nc, sems, sb, psA, psB, layer=1,
                   kd=dr["k1d"], vd=dr["v1d"], qd=dr["q1d"],
                   attn_out=(dr["attn_h1a"], dr["attn_h1b"]),
                   proj_done=s_st.n, zscr=dr["zscr"])

    op("gpsimd", lambda e: e.collective_compute(
        "AllGather", ALU.bypass, replica_groups=[[0, 1, 2, 3], [4, 5, 6, 7]],
        ins=[dr["attn_h1a"].ap().opt()], outs=[dr["ag1a"].ap().opt()]),
       waits=[(s_sA, ctx.cache[("hdone", 1, 1)])], incs=[(s_cc, 1)])
    op("gpsimd", lambda e: e.collective_compute(
        "AllGather", ALU.bypass, replica_groups=[[0, 1, 2, 3], [4, 5, 6, 7]],
        ins=[dr["attn_h1b"].ap().opt()], outs=[dr["ag1b"].ap().opt()]),
       waits=[(s_sB, ctx.cache[("hdone", 1, 3)])], incs=[(s_cc, 1)])
    ccv = s_cc.n


    emit_wo_base(ctx, nc, sems, sb, psA, psB, ag=(dr["ag1a"], dr["ag1b"]), w_in=inp["l1_wo"],
                 bias_row=None, x_res=dr["x1d"], base=dr["base1"], cc_wait=ccv,
                 own_scr=dr["attn_own1"])
    if dbg:
        op("sync", lambda e: e.dma_start(out=dbg["dbg_base1"][:, :], in_=dr["base1"][:, :]),
           waits=[(s_st, s_st.n), (s_sB, s_sB.n), (s_ve, s_ve.n)], incs=[(s_st, 16)])

    emit_ffn1(ctx, nc, sems, sb, psA, psB, dr, inp, out)


def emit_ffn1(ctx, nc, sems, sb, psA, psB, dr, inp, out):
    op = ctx.op
    s_wl, s_al, s_st, s_mm, s_ve, s_ac = (
        sems[k] for k in ("s_wl", "s_al", "s_st", "s_mm", "s_ve", "s_ac"))
    s_gA, s_gB, s_x2, s_sA, s_sB = (sems[k] for k in ("s_gA", "s_gB", "s_x2", "s_sA", "s_sB"))
    s_wA, s_wB = sems["s_wA"], sems["s_wB"]
    abf, af32, stat = sb["abf"], sb["af32"], sb["stat"]
    ident, ones = sb["ident_bf"], sb["ones_col"]
    xn1_v = dr["xn1d"].ap().rearrange("(k p) t -> p k t", p=128)
    base_t = dr["base1"].ap().rearrange("(n p) d -> n p d", p=128)
    out_t = out.ap().rearrange("(n p) d -> n p d", p=128)
    w_fmb = sb["w_fmb"]

    WGS = abf[:, 2048:2176]
    op("gpsimd", lambda e: e.dma_start(out=WGS.rearrange("p (k o) -> p k o", k=16),
                                       in_=inp["l1_wg"].ap().rearrange("(k p) o -> p k o", p=128)),
       waits=[(s_mm, s_mm.n), (s_ve, s_ve.n)], incs=[(s_wl, 16)])
    wg_v = s_wl.n
    G8 = af32[:, 0:8]
    GT = af32[:, 8:16]
    GK = af32[:, 16:24]
    GE = af32[:, 24:32]
    for t in range(16):
        XC = abf[:, 0:2048]
        op("sync", lambda e, t=t: e.dma_start(out=XC.rearrange("p (k t) -> p k t", k=16),
                                              in_=xn1_v[:, :, t * 128:(t + 1) * 128]),
           waits=[(s_mm, ctx.cache.get(("g8mm", t - 1), 0))], incs=[(s_al, 16)])
        lv = s_al.n
        for kc in range(16):
            op("tensor", lambda e, kc=kc:
               e.matmul(psA[:, 0:8], XC[:, kc * 128:(kc + 1) * 128], WGS[:, kc * 8:(kc + 1) * 8],
                        start=(kc == 0), stop=False),
               waits=([(s_al, lv), (s_wl, wg_v), (s_ve, ctx.cache.get(("g8ev", t - 1), 0))]
                      if kc == 0 else []), incs=[])
        op("tensor", lambda e: e.matmul(psA[:, 0:8], ones[:, :], sb["bg_row"][0:1, :],
                                        start=False, stop=True), incs=[(s_mm, 1)])
        ctx.cache[("g8mm", t)] = s_mm.n
        mmv = s_mm.n
        op("vector", lambda e: e.tensor_copy(G8, psA[:, 0:8]), waits=[(s_mm, mmv), (s_ac, s_ac.n)],
           incs=[(s_ve, 1)])
        ctx.cache[("g8ev", t)] = s_ve.n
        op("vector", lambda e: e.tensor_reduce(stat[:, 20:21], G8, AX, ALU.min), incs=[(s_ve, 1)])
        op("vector", lambda e: e.tensor_single_scalar(GT, G8, stat[:, 20:21], ALU.is_equal), incs=[(s_ve, 1)])
        op("vector", lambda e: e.scalar_tensor_tensor(GT, GT, 1e30, G8, ALU.mult, ALU.add), incs=[(s_ve, 1)])
        op("vector", lambda e: e.tensor_reduce(stat[:, 21:22], GT, AX, ALU.min), incs=[(s_ve, 1)])
        op("vector", lambda e: e.tensor_single_scalar(GK, G8, stat[:, 21:22], ALU.is_gt), incs=[(s_ve, 1)])
        op("vector", lambda e: e.reduce_max(stat[:, 22:23], G8, axis=AX, negate=True), incs=[(s_ve, 1)])
        vv = s_ve.n
        op("scalar", lambda e: e.activation(GE, G8, AF.Exp, bias=stat[:, 22:23]),
           waits=[(s_ve, vv)], incs=[(s_ac, 1)])
        acv = s_ac.n
        op("vector", lambda e: e.tensor_mul(GE, GE, GK), waits=[(s_ac, acv)], incs=[(s_ve, 1)])
        op("vector", lambda e: e.reduce_sum(stat[:, 23:24], GE, axis=AX), incs=[(s_ve, 1)])
        op("vector", lambda e: e.reciprocal(stat[:, 24:25], stat[:, 23:24]), incs=[(s_ve, 1)])
        op("vector", lambda e: e.tensor_scalar_mul(GE, GE, stat[:, 24:25]), incs=[(s_ve, 1)])
        GEB = abf[:, 2176:2184]
        op("vector", lambda e: e.tensor_copy(GEB, GE), waits=[(s_mm, s_mm.n)], incs=[(s_ve, 1)])
        gv = s_ve.n
        op("tensor", lambda e: e.matmul(psB[0:8, 0:128], GEB, ident[:, :], start=True, stop=True),
           waits=[(s_ve, gv), (s_ve, ctx.cache.get(("gwe", t - 1), 0))], incs=[(s_mm, 1)])
        mmv = s_mm.n
        op("vector", lambda e, t=t: e.tensor_copy(w_fmb[0:8, t * 128:(t + 1) * 128], psB[0:8, 0:128]),
           waits=[(s_mm, mmv)], incs=[(s_ve, 1)])
        ctx.cache[("gwe", t)] = s_ve.n
    gate_done_ve = s_ve.n

    XNF = abf[:, 0:8192]
    S1B = abf[:, 8192:16384]
    CBF = abf[:, 16384:40960]
    WBg = [abf[:, 40960:43008], abf[:, 43008:45056]]
    WBu = [abf[:, 48128:50176], abf[:, 50176:52224]]
    WBC = abf[:, 45056:48128]
    SIL = af32[:, 8192:8704]
    UPS = af32[:, 8704:9216]
    wgate_in = dr["wgate_b"].ap()
    wup_in = dr["wup_b"].ap()
    wdown_in = dr["wdown_b"].ap()
    wsup_in = dr["wsup_b"].ap().rearrange("(k p) o -> p k o", p=128)
    wsdown_in = dr["wsdown_b"].ap()
    precast_v = sems["s_gst"].n

    for tt in range(4):
        op("sync", lambda e, tt=tt:
           e.dma_start(out=XNF.rearrange("p (k t) -> p k t", k=16), in_=xn1_v[:, :, tt * 512:(tt + 1) * 512]),
           waits=[(s_mm, s_mm.n), (s_ve, s_ve.n)], incs=[(s_al, 16)])
        xv = s_al.n
        for n in range(6):
            op("sync", lambda e, n=n, tt=tt:
               e.dma_start(out=sb["w_row"][0:1, n * 512:(n + 1) * 512],
                           in_=w_fmb[n:n + 1, tt * 512:(tt + 1) * 512]),
               waits=([(s_ve, gate_done_ve), (s_mm, ctx.cache.get(("wrow_mm", tt - 1), 0))]
                      if n == 0 else []),
               incs=[(s_x2, 16)])
        wrow_v = s_x2.n
        for n in range(6):
            op("tensor", lambda e, n=n:
               e.matmul(psB[:, 1024 + (n % 2) * 512: 1024 + (n % 2) * 512 + 512],
                        ones[:, :], sb["w_row"][0:1, n * 512:(n + 1) * 512],
                        start=True, stop=True),
               waits=[(s_x2, wrow_v), (s_ve, s_ve.n)],
               incs=[(s_mm, 1)])
            ctx.cache[("wrow_mm", tt)] = s_mm.n
            mmv = s_mm.n
            op("vector", lambda e, n=n:
               e.tensor_copy(WBC[:, n * 512:(n + 1) * 512], psB[:, 1024 + (n % 2) * 512: 1024 + (n % 2) * 512 + 512]),
               waits=[(s_mm, mmv)], incs=[(s_ve, 1)])
        wbc_v = s_ve.n

        tt_start_mm = s_mm.n
        for hc2 in range(16):
            wb = WBg[hc2 % 2]
            s_w = [s_wA, s_wB][hc2 % 2]
            op("sync", lambda e, wb=wb, hc2=hc2:
               e.dma_start(out=wb.rearrange("p (k h) -> p k h", k=16),
                           in_=wsup_in[:, :, hc2 * 128:(hc2 + 1) * 128]),
               waits=[(s_mm, ctx.cache.get(("sup", tt, hc2 - 2), tt_start_mm)),
                      (sems["s_gst"], precast_v)],
               incs=[(s_w, 16)])
            wv = (s_w, s_w.n)
            bank = (hc2 % 2) * 512
            for kc in range(16):
                op("tensor", lambda e, wb=wb, kc=kc, bank=bank:
                   e.matmul(psA[:, bank:bank + 512], wb[:, kc * 128:(kc + 1) * 128],
                            XNF[:, kc * 512:(kc + 1) * 512], start=(kc == 0), stop=(kc == 15)),
                   waits=([wv, (s_al, xv), (s_ac, ctx.cache.get(("s1e", tt, hc2 - 2), 0))]
                          if kc == 0 else []),
                   incs=([(s_mm, 1)] if kc == 15 else []))
            ctx.cache[("sup", tt, hc2)] = s_mm.n
            mmv = s_mm.n
            op("scalar", lambda e, hc2=hc2, bank=bank:
               e.activation(S1B[:, hc2 * 512:(hc2 + 1) * 512], psA[:, bank:bank + 512], AF.Silu,
                            bias=sb["bsup_sb"][:, hc2:hc2 + 1]),
               waits=[(s_mm, mmv)], incs=[(s_ac, 1)])
            ctx.cache[("s1e", tt, hc2)] = s_ac.n
        s1_v = s_ac.n

        ctx.cache[("exw0", tt)] = s_mm.n
        for n in range(6):
            for hc in range(8):
                ci = n * 8 + hc
                wbg, wbu = WBg[ci % 2], WBu[ci % 2]
                s_w = [s_wA, s_wB][ci % 2]
                op("sync", lambda e, wbg=wbg, n=n, hc=hc:
                   e.dma_start(out=wbg.rearrange("p (k h) -> p k h", k=16),
                               in_=wgate_in[n].rearrange("(k p) h -> p k h", p=128)[:, :, hc * 128:(hc + 1) * 128]),
                   waits=[(s_mm, ctx.cache.get(("exw", tt, ci - 2), ctx.cache.get(("exw0", tt), 0)))] +
                         ([(sems["s_gst"], precast_v)] if ci == 0 else []),
                   incs=[(s_w, 16)])
                op("sync", lambda e, wbu=wbu, n=n, hc=hc:
                   e.dma_start(out=wbu.rearrange("p (k h) -> p k h", k=16),
                               in_=wup_in[n].rearrange("(k p) h -> p k h", p=128)[:, :, hc * 128:(hc + 1) * 128]),
                   incs=[(s_w, 16)])
                wv = (s_w, s_w.n)
                bank = (ci % 2) * 1024
                for kc in range(16):
                    op("tensor", lambda e, wbg=wbg, kc=kc, bank=bank:
                       e.matmul(psA[:, bank:bank + 512], wbg[:, kc * 128:(kc + 1) * 128],
                                XNF[:, kc * 512:(kc + 1) * 512], start=(kc == 0), stop=(kc == 15)),
                       waits=([wv, (s_ve, ctx.cache.get(("cbe", tt, ci - 2), 0))] +
                              ([(s_ac, s_ac.n)] if ci < 2 else [])
                              if kc == 0 else []),
                       incs=([(s_mm, 1)] if kc == 15 else []))
                for kc in range(16):
                    op("tensor", lambda e, wbu=wbu, kc=kc, bank=bank:
                       e.matmul(psA[:, bank + 512:bank + 1024], wbu[:, kc * 128:(kc + 1) * 128],
                                XNF[:, kc * 512:(kc + 1) * 512], start=(kc == 0), stop=(kc == 15)),
                       incs=([(s_mm, 1)] if kc == 15 else []))
                ctx.cache[("exw", tt, ci)] = s_mm.n
                umm = s_mm.n
                op("scalar", lambda e, n=n, hc=hc, bank=bank:
                   e.activation(SIL, psA[:, bank:bank + 512], AF.Silu,
                                bias=sb["bgate_sb"][:, n * 8 + hc:n * 8 + hc + 1]),
                   waits=[(s_mm, umm), (s_ve, ctx.cache.get(("cbe", tt, ci - 1), 0))], incs=[(s_ac, 1)])
                op("scalar", lambda e, n=n, hc=hc, bank=bank:
                   e.activation(UPS, psA[:, bank + 512:bank + 1024], AF.Identity,
                                bias=sb["bup_sb"][:, n * 8 + hc:n * 8 + hc + 1]),
                   incs=[(s_ac, 1)])
                acv = s_ac.n
                op("vector", lambda e: e.tensor_mul(SIL, SIL, UPS), waits=[(s_ac, acv)], incs=[(s_ve, 1)])
                op("vector", lambda e, ci=ci, n=n:
                   e.tensor_tensor(CBF[:, ci * 512:(ci + 1) * 512], SIL, WBC[:, n * 512:(n + 1) * 512], ALU.mult),
                   waits=[(s_ve, wbc_v)], incs=[(s_ve, 1)])
                ctx.cache[("cbe", tt, ci)] = s_ve.n
        c_v = s_ve.n

        XH = af32[:, 0:8192]
        for sub in range(4):
            op("sync", lambda e, sub=sub, tt=tt:
               e.dma_start(out=XH[:, sub * 2048:(sub + 1) * 2048], in_=base_t[tt * 4 + sub]),
               waits=([(s_ve, c_v), (s_sB, s_sB.n)] if sub == 0 else []), incs=[(s_x2, 16)])
        bv = s_x2.n

        srcs = [("e", n, q) for n in range(6) for q in range(2)] + \
               [("s", 0, q) for q in range(4)]
        down_guard = s_mm.n
        for og in range(4):
            si_mm = {}
            for si, (kind, n, q) in enumerate(srcs):
                wb = [WBg[0], WBg[1], WBu[0], WBu[1]][si % 4]
                s_w = [s_wA, s_wB][si % 2]
                if kind == "e":
                    src_ap = wdown_in[n][q * 512:(q + 1) * 512, og * 512:(og + 1) * 512]
                else:
                    src_ap = wsdown_in[q * 512:(q + 1) * 512, og * 512:(og + 1) * 512]
                op("sync", lambda e, wb=wb, src_ap=src_ap:
                   e.dma_start(out=wb.rearrange("p (k o) -> p k o", k=4),
                               in_=src_ap.rearrange("(k p) o -> p k o", p=128)),
                   waits=[(s_mm, si_mm.get(si - 2, down_guard))] +
                         ([(sems["s_gst"], precast_v)] if si == 0 else []),
                   incs=[(s_w, 16)])
                wv = (s_w, s_w.n)
                for k in range(4):
                    lhs0 = (CBF[:, (n * 8 + q * 4 + k) * 512:(n * 8 + q * 4 + k) * 512 + 512]
                            if kind == "e" else S1B[:, (q * 4 + k) * 512:(q * 4 + k) * 512 + 512])
                    for sub in range(4):
                        w_ = []
                        if k == 0 and sub == 0:
                            w_ = [wv]
                            if si == 0:
                                w_ += [(s_ve, c_v), (s_ac, s1_v),
                                       (s_ve, ctx.cache.get(("d1ev", tt, og - 1), 0))]
                        op("tensor", lambda e, wb=wb, lhs=lhs0, sub=sub, si=si, k=k:
                           e.matmul(psB[:, sub * 512:(sub + 1) * 512],
                                    lhs[:, sub * 128:sub * 128 + 128], wb[:, k * 512:(k + 1) * 512],
                                    start=(si == 0 and k == 0), stop=False),
                           waits=w_, incs=([(s_mm, 1)] if (k == 3 and sub == 3) else []))
                si_mm[si] = s_mm.n
            for sub in range(4):
                op("tensor", lambda e, sub=sub, og=og, tt=tt:
                   e.matmul(psB[:, sub * 512:(sub + 1) * 512], w_fmb[0:6, tt * 512 + sub * 128: tt * 512 + sub * 128 + 128],
                            sb["bdown_sb"][0:6, og * 512:(og + 1) * 512], start=False, stop=False),
                   incs=[])
                op("tensor", lambda e, sub=sub, og=og:
                   e.matmul(psB[:, sub * 512:(sub + 1) * 512], ones[:, :],
                            sb["bsdown_row"][0:1, og * 512:(og + 1) * 512], start=False, stop=True),
                   incs=[(s_mm, 1)])
            down_guard = s_mm.n
            dn_v = s_mm.n
            for sub in range(4):
                op("vector", lambda e, sub=sub, og=og:
                   e.tensor_tensor(af32[:, 10240 + sub * 512: 10240 + sub * 512 + 512],
                                   psB[:, sub * 512:(sub + 1) * 512],
                                   XH[:, sub * 2048 + og * 512: sub * 2048 + og * 512 + 512], ALU.add),
                   waits=[(s_mm, dn_v), (s_x2, bv), (s_sA, s_sA.n)],
                   incs=[(s_ve, 1)])
                vv = s_ve.n
                op("sync", lambda e, sub=sub, og=og, tt=tt:
                   e.dma_start(out=out_t[tt * 4 + sub][:, og * 512:(og + 1) * 512],
                               in_=af32[:, 10240 + sub * 512: 10240 + sub * 512 + 512]),
                   waits=[(s_ve, vv)], incs=[(s_sA, 16)])
            ctx.cache[("d1ev", tt, og)] = s_ve.n



_CACHED = {}


def _host_inputs(core, I):
    g, a = core // 4, core % 4
    f32 = np.float32
    x = I["L0_attn_x"]
    d = {}
    d["x_full"] = np.ascontiguousarray(x[g]).astype(f32, copy=False)
    d["x_own"] = np.ascontiguousarray(x[g, a * TOK:(a + 1) * TOK])
    d["l0_wq"] = I["L0_attn_Wq"]
    d["l0_bq"] = I["L0_attn_bq"].reshape(1, D)
    cs = slice(4 * a * 128, 4 * a * 128 + 512)
    d["l0_wk"] = np.ascontiguousarray(I["L0_attn_Wk"][:, cs])
    d["l0_bk"] = np.ascontiguousarray(I["L0_attn_bk"][cs].reshape(4, 128).T)
    d["l0_wv"] = np.ascontiguousarray(I["L0_attn_Wv"][:, cs])
    d["l0_bv"] = I["L0_attn_bv"][cs].reshape(1, 512)
    d["l0_wo"] = I["L0_attn_Wo"]
    d["l0_bo"] = I["L0_attn_bo"].reshape(1, D)
    d["l0_su_ug"] = np.ascontiguousarray(I["L0_ffn_SU_ug"].reshape(16, 128).T)
    d["l0_su_dp"] = np.ascontiguousarray(I["L0_ffn_SU_dp"].reshape(64, 128).T)
    d["l0_ws_ug"] = np.ascontiguousarray(np.broadcast_to(I["L0_ffn_Wscale_ug"].reshape(1, 16384), (128, 16384)))
    d["l0_ws_dp"] = np.ascontiguousarray(np.broadcast_to(I["L0_ffn_Wscale_dp"].reshape(1, D), (128, D)))
    d["l0_wupT"] = np.ascontiguousarray(I["L0_ffn_W_up"].T)
    d["l0_wgateT"] = np.ascontiguousarray(I["L0_ffn_W_gate"].T)
    d["l0_wdownT"] = np.ascontiguousarray(I["L0_ffn_W_down"].T)
    qs = slice(4 * a * 128, 4 * a * 128 + 512)
    kvs = slice(2 * a * 128, 2 * a * 128 + 256)
    d["l1_wq"] = np.ascontiguousarray(I["L1_attn_Wq"].T[:, qs])
    d["l1_wk"] = np.ascontiguousarray(I["L1_attn_Wk"].T[:, kvs])
    d["l1_wv"] = np.ascontiguousarray(I["L1_attn_Wv"].T[:, kvs])
    d["l1_wo"] = np.ascontiguousarray(I["L1_attn_Wo"].T)
    d["l1_qnw"] = I["L1_attn_q_norm_w"].reshape(128, 1)
    d["l1_knw"] = I["L1_attn_k_norm_w"].reshape(128, 1)
    d["l1_wg"] = I["L1_ffn_Wg"]
    d["l1_bg"] = I["L1_ffn_bg"].reshape(1, 8)
    d["l1_wgate"] = I["L1_ffn_Wgate"]
    d["l1_bgate"] = np.ascontiguousarray(I["L1_ffn_bgate"].reshape(6, 8, 128).transpose(2, 0, 1).reshape(128, 48))
    d["l1_wup"] = I["L1_ffn_Wup"]
    d["l1_bup"] = np.ascontiguousarray(I["L1_ffn_bup"].reshape(6, 8, 128).transpose(2, 0, 1).reshape(128, 48))
    d["l1_wdown"] = I["L1_ffn_Wdown"]
    d["l1_bdown"] = I["L1_ffn_bdown"]
    d["l1_wsup"] = I["L1_ffn_Ws_up"]
    d["l1_bsup"] = np.ascontiguousarray(I["L1_ffn_bs_up"].reshape(16, 128).T)
    d["l1_wsdown"] = I["L1_ffn_Ws_down"]
    d["l1_bsdown"] = I["L1_ffn_bs_down"].reshape(1, D)
    d["ident"] = np.eye(128, dtype=f32)
    d["h128"] = _hadamard(128)
    d["h64"] = _hadamard(64)
    m0 = np.zeros((4, 128, 512), dtype=f32)
    krow = np.arange(128)[:, None]
    qcol = np.arange(512)[None, :]
    for i in range(4):
        m0[i] = np.where(i * 128 + krow <= qcol, 0.0, NEG)
    d["l0_mask"] = m0
    m1 = np.zeros((8, 128, 512), dtype=f32)
    for i in range(8):
        kc = 56 + i
        qbase = 7168 if i < 4 else 7680
        key = kc * 128 + krow
        m1[i] = np.where((key >= 7168) & (key <= qbase + qcol), NEG, 0.0)
    d["l1_mask"] = m1
    for k in d:
        if d[k].dtype != np.float32:
            d[k] = d[k].astype(np.float32)
        d[k] = np.ascontiguousarray(d[k])
    return d


def kernel(**I):
    if "nc" not in _CACHED:
        _CACHED["nc"] = build_nc(debug=False)
    nc = _CACHED["nc"]
    in_maps = [_host_inputs(c, I) for c in range(8)]
    res = run_bass_kernel_spmd(nc, in_maps, core_ids=list(range(8)))
    parts = [res.results[c]["out"] for c in range(8)]
    full = np.concatenate([np.concatenate(parts[0:4], axis=0)[None],
                           np.concatenate(parts[4:8], axis=0)[None]], axis=0)
    return full.astype(np.float32)
